# version 06
# speedup vs baseline: 1.9849x; 1.0197x over previous
import numpy as np
import ml_dtypes
from contextlib import ExitStack

import concourse.bass as bass
import concourse.mybir as mybir
from concourse.bass_utils import run_bass_kernel_spmd

F32 = mybir.dt.float32
BF16 = mybir.dt.bfloat16
AF = mybir.ActivationFunctionType
ALU = mybir.AluOpType
AX = mybir.AxisListType

NCORES = 8
SEQ = 2048
SPC = 2
T = SEQ * SPC
D = 1024
DC = D // 128

ENGS = ("t", "v", "s", "g", "q")


class Sem:
    def __init__(self, handle):
        self.h = handle
        self.n = 0

    def inc(self, instr, k=1):
        instr.then_inc(self.h, k)
        self.n += k
        return self.n


def _freeze(fn):
    import types
    if getattr(fn, "__closure__", None) is None:
        return fn
    cells = []
    for c in fn.__closure__:
        try:
            cells.append(types.CellType(c.cell_contents))
        except ValueError:
            cells.append(c)
    out = types.FunctionType(fn.__code__, fn.__globals__, fn.__name__,
                             fn.__defaults__, tuple(cells))
    out.__kwdefaults__ = fn.__kwdefaults__
    return out


class FakeBlock:

    def tensor(self, f):
        f(None)

    def vector(self, f):
        f(None)

    def scalar(self, f):
        f(None)

    def gpsimd(self, f):
        f(None)

    def sync(self, f):
        f(None)


class Prog:

    def __init__(self, nc, stack):
        self.nc = nc
        self.stack = stack
        self.sems = {e: Sem(stack.enter_context(nc.semaphore("sem_" + e))) for e in ENGS}
        self.dma_sems = {}
        self.writer = {}
        self.readers = {}
        self.hwm = {e: {} for e in ENGS}
        self.thunks = {e: [] for e in ENGS}

    def dma_sem(self, key):
        if key not in self.dma_sems:
            self.dma_sems[key] = Sem(self.stack.enter_context(
                self.nc.semaphore("dma_" + str(key))))
        return self.dma_sems[key]

    def op(self, ek, eng, emit_fn, reads=(), writes=(), inc=1, dma=None):
        need = {}
        for tok in reads:
            w = self.writer.get(tok)
            if w is not None:
                sem, m, _ = w
                need[sem] = max(need.get(sem, 0), m)
        for tok in writes:
            w = self.writer.get(tok)
            if w is not None:
                sem, m, wek = w
                if not (ek == "t" and wek == "t"):
                    need[sem] = max(need.get(sem, 0), m)
            for sem, m in self.readers.get(tok, {}).items():
                need[sem] = max(need.get(sem, 0), m)
        waits = []
        for sem, m in need.items():
            if ek == "t" and sem is self.sems["t"]:
                continue
            if m > self.hwm[ek].get(sem, 0):
                waits.append((sem.h, m))
                self.hwm[ek][sem] = m
        my_sem = self.dma_sem(dma) if dma is not None else self.sems[ek]
        my_sem.n += inc
        mil = my_sem.n
        self.thunks[ek].append((waits, _freeze(emit_fn), my_sem.h, inc))
        for tok in writes:
            self.writer[tok] = (my_sem, mil, ek)
            self.readers[tok] = {}
        for tok in reads:
            d = self.readers.setdefault(tok, {})
            d[my_sem] = max(d.get(my_sem, 0), mil)
        return mil

    def barrier(self, block=None):
        snap = {sem: sem.n for sem in
                list(self.sems.values()) + list(self.dma_sems.values())}
        for ek in ENGS:
            waits = []
            for sem, n in snap.items():
                if sem is self.sems[ek]:
                    continue
                if n > self.hwm[ek].get(sem, 0):
                    waits.append((sem.h, n))
                    self.hwm[ek][sem] = n
            if waits:
                self.thunks[ek].append((waits, None, None, 0))

    def final_wait(self, eng=None, ek="q"):
        waits = []
        for sem in list(self.sems.values()) + list(self.dma_sems.values()):
            if sem is self.sems[ek]:
                continue
            if sem.n > self.hwm[ek].get(sem, 0):
                waits.append((sem.h, sem.n))
        if self.sems[ek].n > 0:
            waits.append((self.sems[ek].h, self.sems[ek].n))
        self.thunks[ek].append((waits, None, None, 0))

    def flush(self, real_block):
        meths = {"t": real_block.tensor, "v": real_block.vector,
                 "s": real_block.scalar, "g": real_block.gpsimd,
                 "q": real_block.sync}
        for ek in ENGS:
            def body(eng, ek=ek):
                for waits, emit_fn, sem_h, inc in self.thunks[ek]:
                    for h, m in waits:
                        eng.wait_ge(h, m)
                    if emit_fn is not None:
                        ins = emit_fn(eng)
                        ins.then_inc(sem_h, inc)
            meths[ek](body)
            self.thunks[ek] = []


def _chunked(w_hbm):
    return w_hbm.rearrange("(kc kp) n -> kp kc n", kp=128)


def _tok_tiles(dram_2d, t0, n_tt):
    return dram_2d[t0:t0 + 128 * n_tt, :].rearrange("(tt p) d -> p tt d", p=128)


def ln_stats_tm(p, block, x_ap_fn, x_toks, stat, cols, sq_scr, ntt, eps_col,
                eps_tok, width=D):
    @block.vector
    def _(v):
        for tt in range(ntt):
            c0 = cols[tt]
            p.op("v", v, lambda e, tt=tt, c0=c0: e.reduce_sum(
                stat[:, c0 + 0: c0 + 1], x_ap_fn(tt), axis=AX.X),
                reads=[x_toks[tt]], writes=[("st", c0, 0)])
            p.op("v", v, lambda e, c0=c0: e.tensor_scalar_mul(
                stat[:, c0 + 1: c0 + 2], stat[:, c0 + 0: c0 + 1], 1.0 / width),
                reads=[("st", c0, 0)], writes=[("st", c0, 1)])

    @block.scalar
    def _(s):
        for tt in range(ntt):
            c0 = cols[tt]
            p.op("s", s, lambda e, tt=tt, c0=c0: e.activation(
                sq_scr[:, 0:width], x_ap_fn(tt), AF.Square,
                accum_out=stat[:, c0 + 2: c0 + 3]),
                reads=[x_toks[tt]], writes=["sq_scr", ("st", c0, 2)])

    @block.vector
    def _(v):
        for tt in range(ntt):
            c0 = cols[tt]
            p.op("v", v, lambda e, c0=c0: e.tensor_tensor(
                stat[:, c0 + 3: c0 + 4], stat[:, c0 + 1: c0 + 2],
                stat[:, c0 + 1: c0 + 2], ALU.mult),
                reads=[("st", c0, 1)], writes=[("st", c0, 3)])
            p.op("v", v, lambda e, c0=c0: e.tensor_scalar_mul(
                stat[:, c0 + 4: c0 + 5], stat[:, c0 + 2: c0 + 3], 1.0 / width),
                reads=[("st", c0, 2)], writes=[("st", c0, 4)])
            p.op("v", v, lambda e, c0=c0: e.tensor_tensor(
                stat[:, c0 + 5: c0 + 6], stat[:, c0 + 4: c0 + 5],
                stat[:, c0 + 3: c0 + 4], ALU.subtract),
                reads=[("st", c0, 4), ("st", c0, 3)], writes=[("st", c0, 5)])

    @block.scalar
    def _(s):
        for tt in range(ntt):
            c0 = cols[tt]
            p.op("s", s, lambda e, c0=c0: e.activation(
                stat[:, c0 + 6: c0 + 7], stat[:, c0 + 5: c0 + 6], AF.Ln,
                bias=eps_col[:, 0:1]),
                reads=[("st", c0, 5), eps_tok], writes=[("st", c0, 6)])
            p.op("s", s, lambda e, c0=c0: e.activation(
                stat[:, c0 + 7: c0 + 8], stat[:, c0 + 6: c0 + 7], AF.Exp, scale=-0.5),
                reads=[("st", c0, 6)], writes=[("st", c0, 7)])


def _std_wrap(decl_fn, emit_fn, T_, SEQ_, **kw):
    nc = bass.Bass("TRN2", target_bir_lowering=False, debug=False)
    aps = decl_fn(nc, "", T_, SEQ_)
    with ExitStack() as st_:
        p = Prog(nc, st_)
        block = FakeBlock()
        with ExitStack() as sst:
            emit_fn(nc, p, block, sst, "", aps, T_, SEQ_, **kw)
        p.final_wait()
        real_block = st_.enter_context(nc.Block())
        p.flush(real_block)
    return nc


def decl_stage_a(nc, pfx, T_, SEQ_=None, x_in=None, x_out=None):
    d = lambda n, s, dt=F32, kind="ExternalInput": nc.dram_tensor(pfx + n, s, dt, kind=kind).ap()
    return {
        "x_in": x_in if x_in is not None else d("x_in", [T_, D]), "w_q": d("w_q", [D, D]), "w_k": d("w_k", [D, D]),
        "w_v": d("w_v", [D, D]), "w_c": d("w_c", [D, D]),
        "ident": d("ident", [128, 128], BF16),
        "x_out": x_out if x_out is not None else d("x_out", [T_, D], F32, "ExternalOutput"),
    }


def emit_stage_a(nc, p, block, st_, pfx, A, T_=T, SEQ_=SEQ):
    x_in, w_q, w_k, w_v, w_c, ident, x_out = (A["x_in"], A["w_q"], A["w_k"], A["w_v"],
                                              A["w_c"], A["ident"], A["x_out"])
    CH = 256
    NCH = T_ // CH
    CPS = SEQ_ // CH
    NTT = CH // 128

    if True:
        sb = lambda n, s, dt=F32: st_.enter_context(nc.sbuf_tensor(pfx + n, s, dt))
        ps = lambda n, s, dt=F32: st_.enter_context(nc.psum_tensor(pfx + n, s, dt))
        wq = sb("wq", [128, DC, D], BF16)
        wk = sb("wk", [128, DC, D], BF16)
        wv = sb("wv", [128, DC, D], BF16)
        wc = sb("wc", [128, DC, D], BF16)
        idn = sb("idn", [128, 128], BF16)
        x_tm = sb("x_tm", [128, 2, NTT, D])
        out_tm = sb("out_tm", [128, NTT, D])
        xn_tm = sb("xn_tm", [128, NTT, D], BF16)
        xn_fm = sb("xn_fm", [128, DC, CH], BF16)
        v_fm = sb("v_fm", [128, DC, CH])
        k_fm = sb("k_fm", [128, DC, CH])
        kv_fm = sb("kv_fm", [128, DC, CH])
        kc_fm = sb("kc_fm", [128, DC, CH])
        qa = sb("qa", [128, DC, CH])
        qb = sb("qb", [128, DC, CH])
        o_fm = sb("o_fm", [128, DC, CH], BF16)
        sq_scr = sb("sq_scr", [128, D])
        zero_col = sb("zero_col", [128, 1])
        col_eps5 = sb("col_eps5", [128, 1])
        col_eps6 = sb("col_eps6", [128, 1])
        col_one = sb("col_one", [128, 1])
        carry_k = sb("carry_k", [128, DC])
        carry_kv = sb("carry_kv", [128, DC])
        stat = sb("stat", [128, 8 * NTT])
        ysum = sb("ysum", [128, NTT, 2])
        ysq = sb("ysq", [128, NTT, 2])
        ptr = [ps("ptr0", [128, 128], BF16), ps("ptr1", [128, 128], BF16)]
        pa = [ps("pa0", [128, CH]), ps("pa1", [128, CH])]
        py = [ps("py0", [128, 512]), ps("py1", [128, 512])]

        @block.gpsimd
        def _(g):
            for w_hbm, w_sb, tok in ((w_q, wq, "wq"), (w_k, wk, "wk"),
                                     (w_v, wv, "wv"), (w_c, wc, "wc")):
                p.op("g", g, lambda e, a=w_hbm, b=w_sb: e.dma_start(out=b[:], in_=_chunked(a)),
                     writes=[tok], inc=16, dma=tok)

        def load_chunk(sync, ci):
            p.op("q", sync, lambda e, ci=ci: e.dma_start(
                out=x_tm[:, ci % 2, :, :], in_=_tok_tiles(x_in, ci * CH, NTT)),
                writes=[("x", ci % 2)], inc=16, dma="x%d" % (ci % 2))

        @block.sync
        def _(sync):
            p.op("q", sync, lambda e: e.dma_start(out=idn[:], in_=ident[:, :]),
                 writes=["idn"], inc=16, dma="idn")
            load_chunk(sync, 0)
            load_chunk(sync, 1)

        @block.vector
        def _(v):
            p.op("v", v, lambda e: e.memset(zero_col[:], 0.0), writes=["zero"])
            p.op("v", v, lambda e: e.memset(col_eps5[:], 1e-5), writes=["eps5"])
            p.op("v", v, lambda e: e.memset(col_eps6[:], 1e-6), writes=["eps6"])
            p.op("v", v, lambda e: e.memset(col_one[:], 1.0), writes=["one"])

        ev_par = [0]
        pa_i = [0]
        py_i = [0]

        for ci in range(NCH):
            buf = ci % 2
            first_in_seq = (ci % CPS == 0)
            xtok = ("x", buf)

            ln_stats_tm(p, block, lambda tt, buf=buf, x_tm=x_tm: x_tm[:, buf, tt, :], [xtok] * NTT,
                        stat, [0, 8], sq_scr, NTT, col_eps5, "eps5")

            @block.vector
            def _(v):
                for tt in range(NTT):
                    c0 = tt * 8
                    p.op("v", v, lambda e, tt=tt, c0=c0: e.tensor_scalar(
                        xn_tm[:, tt, :], x_tm[:, buf, tt, :],
                        stat[:, c0 + 1: c0 + 2], stat[:, c0 + 7: c0 + 8],
                        ALU.subtract, ALU.mult),
                        reads=[xtok, ("st", c0, 1), ("st", c0, 7)],
                        writes=[("xn_tm", tt)])

            for tt in range(NTT):
                for c in range(DC):
                    par = ev_par[0] % 2
                    ev_par[0] += 1

                    @block.tensor
                    def _(t, tt=tt, c=c, par=par):
                        p.op("t", t, lambda e: e.transpose(
                            ptr[par][:, :], xn_tm[:, tt, c * 128:(c + 1) * 128], idn[:]),
                            reads=[("xn_tm", tt), "idn"], writes=[("ptr", par)])

                    @block.scalar
                    def _(s, tt=tt, c=c, par=par):
                        p.op("s", s, lambda e: e.copy(
                            xn_fm[:, c, tt * 128:(tt + 1) * 128], ptr[par][:, :]),
                            reads=[("ptr", par)], writes=["xn_fm"])

            for pj, wsb, wtok, dst, dtok in (("v", wv, "wv", v_fm, "v_fm"),
                                             ("k", wk, "wk", k_fm, "k_fm"),
                                             ("q", wq, "wq", qa, "qa")):
                for oc in range(DC):
                    par = pa_i[0] % 2
                    pa_i[0] += 1

                    @block.tensor
                    def _(t, wsb=wsb, wtok=wtok, oc=oc, par=par):
                        def mmgroup(e):
                            for kc in range(DC):
                                ins = e.matmul(pa[par][:, :],
                                               wsb[:, kc, oc * 128:(oc + 1) * 128],
                                               xn_fm[:, kc, :],
                                               start=(kc == 0), stop=(kc == DC - 1))
                            return ins
                        p.op("t", t, mmgroup, reads=["xn_fm", wtok], writes=[("pa", par)])

                    @block.scalar
                    def _(s, pj=pj, dst=dst, dtok=dtok, oc=oc, par=par):
                        if pj == "v":
                            fn = lambda e: e.copy(dst[:, oc, :], pa[par][:, :])
                        elif pj == "k":
                            fn = lambda e: e.activation(dst[:, oc, :], pa[par][:, :], AF.Exp)
                        else:
                            fn = lambda e: e.activation(dst[:, oc, :], pa[par][:, :], AF.Exp, scale=-1.0)
                        p.op("s", s, fn, reads=[("pa", par)], writes=[(dtok, oc)])

            @block.scalar
            def _(s):
                p.op("s", s, lambda e: e.activation(qb[:, :, :], qa[:, :, :], AF.Ln,
                                                    bias=col_one[:, 0:1]),
                     reads=[("qa", oc) for oc in range(DC)] + ["one"], writes=["qb"])
                p.op("s", s, lambda e: e.activation(qa[:, :, :], qb[:, :, :], AF.Exp, scale=-1.0),
                     reads=["qb"], writes=["sig"])

            @block.vector
            def _(v):
                p.op("v", v, lambda e: e.tensor_tensor(kv_fm[:, :, :], k_fm[:, :, :],
                                                       v_fm[:, :, :], ALU.mult),
                     reads=[("k_fm", oc) for oc in range(DC)] +
                           [("v_fm", oc) for oc in range(DC)],
                     writes=["kv_fm"])
                for c in range(DC):
                    if first_in_seq:
                        init_k, init_kv = 0.0, 0.0
                        rk, rkv = [], []
                    else:
                        init_k = carry_k[:, c:c + 1]
                        init_kv = carry_kv[:, c:c + 1]
                        rk, rkv = [("ck", c)], [("ckv", c)]
                    p.op("v", v, lambda e, c=c, init_k=init_k: e.tensor_tensor_scan(
                        kc_fm[:, c, :], k_fm[:, c, :],
                        zero_col[:, 0:1].broadcast_to([128, CH]), init_k, ALU.add, ALU.add),
                        reads=[("k_fm", c), "zero"] + rk, writes=[("kc_fm", c)])
                    p.op("v", v, lambda e, c=c: e.tensor_copy(
                        carry_k[:, c:c + 1], kc_fm[:, c, CH - 1:CH]),
                        reads=[("kc_fm", c)], writes=[("ck", c)])
                    p.op("v", v, lambda e, c=c, init_kv=init_kv: e.tensor_tensor_scan(
                        k_fm[:, c, :], kv_fm[:, c, :],
                        zero_col[:, 0:1].broadcast_to([128, CH]), init_kv, ALU.add, ALU.add),
                        reads=["kv_fm", "zero"] + rkv, writes=[("kvc", c), ("k_fm", c)])
                    p.op("v", v, lambda e, c=c: e.tensor_copy(
                        carry_kv[:, c:c + 1], k_fm[:, c, CH - 1:CH]),
                        reads=[("kvc", c)], writes=[("ckv", c)])

            @block.scalar
            def _(s):
                p.op("s", s, lambda e: e.activation(qb[:, :, :], kc_fm[:, :, :], AF.Ln,
                                                    bias=col_eps6[:, 0:1]),
                     reads=[("kc_fm", c) for c in range(DC)] + ["eps6", "sig"], writes=["qb"])
                p.op("s", s, lambda e: e.activation(qb[:, :, :], qb[:, :, :], AF.Exp, scale=-1.0),
                     reads=["qb"], writes=["qb"])

            @block.vector
            def _(v):
                p.op("v", v, lambda e: e.tensor_tensor(v_fm[:, :, :], k_fm[:, :, :],
                                                       qb[:, :, :], ALU.mult),
                     reads=[("kvc", c) for c in range(DC)] + ["qb"],
                     writes=[("v_fm", oc) for oc in range(DC)] + ["S"])
                p.op("v", v, lambda e: e.tensor_tensor(o_fm[:, :, :], v_fm[:, :, :],
                                                       qa[:, :, :], ALU.mult),
                     reads=["S", "sig"], writes=["o_fm"])

            for tt in range(NTT):
                for nb in range(2):
                    par = py_i[0] % 2
                    py_i[0] += 1

                    @block.tensor
                    def _(t, tt=tt, nb=nb, par=par):
                        def mmg(e):
                            for c in range(DC):
                                ins = e.matmul(py[par][:, :],
                                               o_fm[:, c, tt * 128:(tt + 1) * 128],
                                               wc[:, c, nb * 512:(nb + 1) * 512],
                                               start=(c == 0), stop=(c == DC - 1))
                            return ins
                        p.op("t", t, mmg, reads=["o_fm", "wc"], writes=[("py", par)])

                    @block.scalar
                    def _(s, tt=tt, nb=nb, par=par):
                        p.op("s", s, lambda e: e.copy(
                            out_tm[:, tt, nb * 512:(nb + 1) * 512], py[par][:, :]),
                            reads=[("py", par)], writes=[("out", tt, nb)])
                        p.op("s", s, lambda e: e.activation(
                            sq_scr[:, 0:512], py[par][:, :], AF.Square,
                            accum_out=ysq[:, tt, nb:nb + 1]),
                            reads=[("py", par)], writes=["sq_scr", ("ysq", tt, nb)])

            @block.vector
            def _(v):
                for tt in range(NTT):
                    c0 = tt * 8
                    p.op("v", v, lambda e, tt=tt: e.reduce_sum(
                        ysum[:, tt, 0:1], out_tm[:, tt, 0:512], axis=AX.X),
                        reads=[("out", tt, 0)], writes=[("ysum", tt, 0)])
                    p.op("v", v, lambda e, tt=tt: e.reduce_sum(
                        ysum[:, tt, 1:2], out_tm[:, tt, 512:1024], axis=AX.X),
                        reads=[("out", tt, 1)], writes=[("ysum", tt, 1)])
                    p.op("v", v, lambda e, tt=tt, c0=c0: e.tensor_tensor(
                        stat[:, c0 + 0: c0 + 1], ysum[:, tt, 0:1], ysum[:, tt, 1:2], ALU.add),
                        reads=[("ysum", tt, 0), ("ysum", tt, 1)], writes=[("st", c0, 0)])
                    p.op("v", v, lambda e, tt=tt, c0=c0: e.tensor_tensor(
                        stat[:, c0 + 2: c0 + 3], ysq[:, tt, 0:1], ysq[:, tt, 1:2], ALU.add),
                        reads=[("ysq", tt, 0), ("ysq", tt, 1)], writes=[("st", c0, 2)])
                    p.op("v", v, lambda e, c0=c0: e.tensor_scalar_mul(
                        stat[:, c0 + 1: c0 + 2], stat[:, c0 + 0: c0 + 1], 1.0 / D),
                        reads=[("st", c0, 0)], writes=[("st", c0, 1)])
                    p.op("v", v, lambda e, c0=c0: e.tensor_scalar_mul(
                        stat[:, c0 + 4: c0 + 5], stat[:, c0 + 2: c0 + 3], 1.0 / D),
                        reads=[("st", c0, 2)], writes=[("st", c0, 4)])
                    p.op("v", v, lambda e, c0=c0: e.tensor_tensor(
                        stat[:, c0 + 3: c0 + 4], stat[:, c0 + 1: c0 + 2],
                        stat[:, c0 + 1: c0 + 2], ALU.mult),
                        reads=[("st", c0, 1)], writes=[("st", c0, 3)])
                    p.op("v", v, lambda e, c0=c0: e.tensor_tensor(
                        stat[:, c0 + 5: c0 + 6], stat[:, c0 + 4: c0 + 5],
                        stat[:, c0 + 3: c0 + 4], ALU.subtract),
                        reads=[("st", c0, 4), ("st", c0, 3)], writes=[("st", c0, 5)])

            @block.scalar
            def _(s):
                for tt in range(NTT):
                    c0 = tt * 8
                    p.op("s", s, lambda e, c0=c0: e.activation(
                        stat[:, c0 + 6: c0 + 7], stat[:, c0 + 5: c0 + 6], AF.Ln,
                        bias=col_eps5[:, 0:1]),
                        reads=[("st", c0, 5), "eps5"], writes=[("st", c0, 6)])
                    p.op("s", s, lambda e, c0=c0: e.activation(
                        stat[:, c0 + 7: c0 + 8], stat[:, c0 + 6: c0 + 7], AF.Exp, scale=-0.5),
                        reads=[("st", c0, 6)], writes=[("st", c0, 7)])

            @block.vector
            def _(v):
                for tt in range(NTT):
                    c0 = tt * 8
                    p.op("v", v, lambda e, tt=tt, c0=c0: e.tensor_scalar(
                        out_tm[:, tt, :], out_tm[:, tt, :],
                        stat[:, c0 + 1: c0 + 2], stat[:, c0 + 7: c0 + 8],
                        ALU.subtract, ALU.mult),
                        reads=[("out", tt, 0), ("out", tt, 1), ("st", c0, 1), ("st", c0, 7)],
                        writes=[("outn", tt)])
                    p.op("v", v, lambda e, tt=tt: e.tensor_tensor(
                        out_tm[:, tt, :], out_tm[:, tt, :], x_tm[:, buf, tt, :], ALU.add),
                        reads=[("outn", tt), xtok], writes=[("out", tt, 0), ("out", tt, 1)])

            @block.sync
            def _(sync):
                p.op("q", sync, lambda e, ci=ci: e.dma_start(
                    out=_tok_tiles(x_out, ci * CH, NTT), in_=out_tm[:, :, :]),
                    reads=[("out", tt, nb) for tt in range(NTT) for nb in range(2)],
                    writes=["x_out_hbm"], inc=16, dma="xout")
                if ci + 2 < NCH:
                    load_chunk(sync, ci + 2)


def build_stage_a(T_=T, SEQ_=SEQ):
    return _std_wrap(decl_stage_a, emit_stage_a, T_, SEQ_)


def decl_stage_c(nc, pfx, T_, SEQ_=None, x_in=None, x_out=None):
    d = lambda n, s, dt=F32, kind="ExternalInput": nc.dram_tensor(pfx + n, s, dt, kind=kind).ap()
    return {
        "x_in": x_in if x_in is not None else d("x_in", [T_, D]), "w_q": d("w_q", [D, D]), "w_k": d("w_k", [D, D]),
        "w_v": d("w_v", [D, D]), "ident": d("ident", [128, 128], BF16),
        "identf": d("identf", [128, 128]), "arange": d("arange", [128, 1]),
        "x_out": x_out if x_out is not None else d("x_out", [T_, D], F32, "ExternalOutput"),
    }


def emit_stage_c(nc, p, block, st_, pfx, A, T_=T, SEQ_=SEQ):
    x_in, w_q, w_k, w_v, ident, identf, arange, x_out = (
        A["x_in"], A["w_q"], A["w_k"], A["w_v"], A["ident"], A["identf"],
        A["arange"], A["x_out"])
    CH = 256
    NCH = T_ // CH
    CPS = SEQ_ // CH
    NTT = CH // 128
    NH = 16

    if True:
        sb = lambda n, s, dt=F32: st_.enter_context(nc.sbuf_tensor(pfx + n, s, dt))
        ps = lambda n, s, dt=F32: st_.enter_context(nc.psum_tensor(pfx + n, s, dt))
        wq = sb("wq", [128, DC, D], BF16)
        wk = sb("wk", [128, DC, D], BF16)
        wv = sb("wv", [128, DC, D], BF16)
        idn = sb("idn", [128, 128], BF16)
        idnf = sb("idnf", [128, 128], F32)
        arangec = sb("arangec", [128, 1])
        x_tm = sb("x_tm", [128, 2, NTT, D])
        out_tm = sb("out_tm", [128, NTT, D])
        xn_tm = sb("xn_tm", [128, NTT, D], BF16)
        xn_fm = sb("xn_fm", [128, DC, CH], BF16)
        v_fm = sb("v_fm", [128, DC, CH])
        vc_fm = sb("vc_fm", [128, DC, CH])
        v_tm = sb("v_tm", [128, NTT, D])
        vc_tm = sb("vc_tm", [128, NTT, D])
        q_tm = sb("q_tm", [128, NTT, D])
        k_tm = sb("k_tm", [128, NTT, D])
        att_tm = sb("att_tm", [128, NTT, D])
        sq_scr = sb("sq_scr", [128, D])
        col_eps5 = sb("col_eps5", [128, 1])
        zero_col = sb("zero_col", [128, 1])
        carry_v = sb("carry_v", [128, DC])
        stat = sb("stat", [128, 8 * NTT])
        sd = sb("sd", [128, NTT, NH])
        sm_ = sb("sm_", [128, NTT, NH])
        mm_ = sb("mm_", [128, NTT, NH])
        e_ = sb("e_", [128, NTT, NH])
        z_ = sb("z_", [128, NTT, NH])
        den_ = sb("den_", [128, NTT, NH])
        rden_ = sb("rden_", [128, NTT, NH])
        icol = sb("icol", [128, NTT])
        ptr = [ps("ptr0", [128, 128], BF16), ps("ptr1", [128, 128], BF16)]
        ptrf = [ps("ptrf0", [128, 128], F32), ps("ptrf1", [128, 128], F32)]
        pa = [ps("pa0", [128, CH]), ps("pa1", [128, CH])]
        py = [ps("py0", [128, 512]), ps("py1", [128, 512])]

        @block.gpsimd
        def _(g):
            for w_hbm, w_sb, tok in ((w_q, wq, "wq"), (w_k, wk, "wk"), (w_v, wv, "wv")):
                p.op("g", g, lambda e, a=w_hbm, b=w_sb: e.dma_start(out=b[:], in_=_chunked(a)),
                     writes=[tok], inc=16, dma=tok)

        def load_chunk(sync, ci):
            p.op("q", sync, lambda e, ci=ci: e.dma_start(
                out=x_tm[:, ci % 2, :, :], in_=_tok_tiles(x_in, ci * CH, NTT)),
                writes=[("x", ci % 2)], inc=16, dma="x%d" % (ci % 2))

        @block.sync
        def _(sync):
            p.op("q", sync, lambda e: e.dma_start(out=idn[:], in_=ident[:, :]),
                 writes=["idn"], inc=16, dma="idn")
            p.op("q", sync, lambda e: e.dma_start(out=idnf[:], in_=identf[:, :]),
                 writes=["idnf"], inc=16, dma="idnf")
            p.op("q", sync, lambda e: e.dma_start(out=arangec[:], in_=arange[:, :]),
                 writes=["arange"], inc=16, dma="arange")
            load_chunk(sync, 0)
            load_chunk(sync, 1)

        @block.vector
        def _(v):
            p.op("v", v, lambda e: e.memset(col_eps5[:], 1e-5), writes=["eps5"])
            p.op("v", v, lambda e: e.memset(zero_col[:], 0.0), writes=["zero"])

        ev_par = [0]
        evf_par = [0]
        pa_i = [0]
        py_i = [0]

        for ci in range(NCH):
            buf = ci % 2
            first_in_seq = (ci % CPS == 0)
            xtok = ("x", buf)
            seq_base = (ci % CPS) * CH

            ln_stats_tm(p, block, lambda tt, buf=buf, x_tm=x_tm: x_tm[:, buf, tt, :], [xtok] * NTT,
                        stat, [0, 8], sq_scr, NTT, col_eps5, "eps5")

            @block.vector
            def _(v):
                for tt in range(NTT):
                    c0 = tt * 8
                    p.op("v", v, lambda e, tt=tt, c0=c0: e.tensor_scalar(
                        xn_tm[:, tt, :], x_tm[:, buf, tt, :],
                        stat[:, c0 + 1: c0 + 2], stat[:, c0 + 7: c0 + 8],
                        ALU.subtract, ALU.mult),
                        reads=[xtok, ("st", c0, 1), ("st", c0, 7)],
                        writes=[("xn_tm", tt)])
                    p.op("v", v, lambda e, tt=tt: e.tensor_scalar_add(
                        icol[:, tt:tt + 1], arangec[:, 0:1], float(seq_base + tt * 128)),
                        reads=["arange"], writes=[("icol", tt)])

            for tt in range(NTT):
                for c in range(DC):
                    par = ev_par[0] % 2
                    ev_par[0] += 1

                    @block.tensor
                    def _(t, tt=tt, c=c, par=par):
                        p.op("t", t, lambda e: e.transpose(
                            ptr[par][:, :], xn_tm[:, tt, c * 128:(c + 1) * 128], idn[:]),
                            reads=[("xn_tm", tt), "idn"], writes=[("ptr", par)])

                    @block.scalar
                    def _(s, tt=tt, c=c, par=par):
                        p.op("s", s, lambda e: e.copy(
                            xn_fm[:, c, tt * 128:(tt + 1) * 128], ptr[par][:, :]),
                            reads=[("ptr", par)], writes=["xn_fm"])

            for oc in range(DC):
                par = pa_i[0] % 2
                pa_i[0] += 1

                @block.tensor
                def _(t, oc=oc, par=par):
                    def mmg(e):
                        for kc in range(DC):
                            ins = e.matmul(pa[par][:, :], wv[:, kc, oc * 128:(oc + 1) * 128],
                                           xn_fm[:, kc, :], start=(kc == 0), stop=(kc == DC - 1))
                        return ins
                    p.op("t", t, mmg, reads=["xn_fm", "wv"], writes=[("pa", par)])

                @block.scalar
                def _(s, oc=oc, par=par):
                    p.op("s", s, lambda e: e.copy(v_fm[:, oc, :], pa[par][:, :]),
                         reads=[("pa", par)], writes=[("v_fm", oc)])

            for pj, wsb, wtok, dst, dtok in (("q", wq, "wq", q_tm, "q_tm"),
                                             ("k", wk, "wk", k_tm, "k_tm")):
                for tt in range(NTT):
                    for nb in range(2):
                        par = py_i[0] % 2
                        py_i[0] += 1

                        @block.tensor
                        def _(t, wsb=wsb, wtok=wtok, tt=tt, nb=nb, par=par):
                            def mmg(e):
                                for c in range(DC):
                                    ins = e.matmul(py[par][:, :],
                                                   xn_fm[:, c, tt * 128:(tt + 1) * 128],
                                                   wsb[:, c, nb * 512:(nb + 1) * 512],
                                                   start=(c == 0), stop=(c == DC - 1))
                                return ins
                            p.op("t", t, mmg, reads=["xn_fm", wtok], writes=[("py", par)])

                        @block.scalar
                        def _(s, dst=dst, dtok=dtok, tt=tt, nb=nb, par=par):
                            p.op("s", s, lambda e: e.copy(
                                dst[:, tt, nb * 512:(nb + 1) * 512], py[par][:, :]),
                                reads=[("py", par)], writes=[(dtok, tt)])

            @block.vector
            def _(v):
                for tt in range(NTT):
                    p.op("v", v, lambda e, tt=tt: e.tensor_tensor(
                        att_tm[:, tt, :], q_tm[:, tt, :], k_tm[:, tt, :], ALU.mult),
                        reads=[("q_tm", tt), ("k_tm", tt)], writes=[("att", tt)])
                    p.op("v", v, lambda e, tt=tt: e.tensor_reduce(
                        sd[:, tt, :], att_tm[:, tt, :].rearrange("p (h d) -> p h d", d=64),
                        AX.X, ALU.add),
                        reads=[("att", tt)], writes=[("sd", tt)])
                    p.op("v", v, lambda e, tt=tt: e.tensor_scalar_mul(
                        sm_[:, tt, :], sd[:, tt, :], 0.125),
                        reads=[("sd", tt)], writes=[("sm", tt)])
                    p.op("v", v, lambda e, tt=tt: e.tensor_scalar_max(
                        mm_[:, tt, :], sm_[:, tt, :], 0.0),
                        reads=[("sm", tt)], writes=[("m", tt)])
                    p.op("v", v, lambda e, tt=tt: e.tensor_tensor(
                        e_[:, tt, :], sm_[:, tt, :], mm_[:, tt, :], ALU.subtract),
                        reads=[("sm", tt), ("m", tt)], writes=[("e", tt)])

            @block.scalar
            def _(s):
                for tt in range(NTT):
                    p.op("s", s, lambda e, tt=tt: e.activation(
                        e_[:, tt, :], e_[:, tt, :], AF.Exp),
                        reads=[("e", tt)], writes=[("e", tt)])
                    p.op("s", s, lambda e, tt=tt: e.activation(
                        z_[:, tt, :], mm_[:, tt, :], AF.Exp, scale=-1.0),
                        reads=[("m", tt)], writes=[("z", tt)])

            @block.vector
            def _(v):
                for tt in range(NTT):
                    p.op("v", v, lambda e, tt=tt: e.scalar_tensor_tensor(
                        den_[:, tt, :], z_[:, tt, :], icol[:, tt:tt + 1], e_[:, tt, :],
                        ALU.mult, ALU.add),
                        reads=[("z", tt), ("icol", tt), ("e", tt)], writes=[("den", tt)])

            @block.scalar
            def _(s):
                for tt in range(NTT):
                    p.op("s", s, lambda e, tt=tt: e.activation(
                        rden_[:, tt, :], den_[:, tt, :], AF.Ln),
                        reads=[("den", tt)], writes=[("rden", tt)])
                    p.op("s", s, lambda e, tt=tt: e.activation(
                        rden_[:, tt, :], rden_[:, tt, :], AF.Exp, scale=-1.0),
                        reads=[("rden", tt)], writes=[("rden", tt)])

            @block.vector
            def _(v):
                for c in range(DC):
                    if first_in_seq:
                        init_v = 0.0
                        rv = []
                    else:
                        init_v = carry_v[:, c:c + 1]
                        rv = [("cv", c)]
                    p.op("v", v, lambda e, c=c, init_v=init_v: e.tensor_tensor_scan(
                        vc_fm[:, c, :], v_fm[:, c, :],
                        zero_col[:, 0:1].broadcast_to([128, CH]), init_v, ALU.add, ALU.add),
                        reads=[("v_fm", c), "zero"] + rv, writes=[("vc_fm", c)])
                    p.op("v", v, lambda e, c=c: e.tensor_copy(
                        carry_v[:, c:c + 1], vc_fm[:, c, CH - 1:CH]),
                        reads=[("vc_fm", c)], writes=[("cv", c)])

            for src, stok, dst, dtok in ((v_fm, "v_fm", v_tm, "v_tm"),
                                         (vc_fm, "vc_fm", vc_tm, "vc_tm")):
                for tt in range(NTT):
                    for c in range(DC):
                        par = evf_par[0] % 2
                        evf_par[0] += 1

                        @block.tensor
                        def _(t, src=src, stok=stok, tt=tt, c=c, par=par):
                            p.op("t", t, lambda e: e.transpose(
                                ptrf[par][:, :], src[:, c, tt * 128:(tt + 1) * 128], idnf[:]),
                                reads=[(stok, c), "idnf"], writes=[("ptrf", par)])

                        @block.scalar
                        def _(s, dst=dst, dtok=dtok, tt=tt, c=c, par=par):
                            p.op("s", s, lambda e: e.copy(
                                dst[:, tt, c * 128:(c + 1) * 128], ptrf[par][:, :]),
                                reads=[("ptrf", par)], writes=[(dtok, tt)])

            @block.vector
            def _(v):
                for tt in range(NTT):
                    h3 = lambda ap: ap.unsqueeze(2).broadcast_to([128, NH, 64])
                    vv = lambda buf_, tt=tt: buf_[:, tt, :].rearrange("p (h d) -> p h d", d=64)
                    p.op("v", v, lambda e, tt=tt: e.tensor_tensor(
                        vc_tm[:, tt, :], vc_tm[:, tt, :], v_tm[:, tt, :], ALU.subtract),
                        reads=[("vc_tm", tt), ("v_tm", tt)], writes=[("vc_tm", tt)])
                    p.op("v", v, lambda e, tt=tt, h3=h3, vv=vv: e.tensor_tensor(
                        vv(vc_tm), vv(vc_tm), h3(z_[:, tt, :]), ALU.mult),
                        reads=[("vc_tm", tt), ("z", tt)], writes=[("vc_tm", tt)])
                    p.op("v", v, lambda e, tt=tt, h3=h3, vv=vv: e.tensor_tensor(
                        vv(v_tm), vv(v_tm), h3(e_[:, tt, :]), ALU.mult),
                        reads=[("v_tm", tt), ("e", tt)], writes=[("v_tm", tt)])
                    p.op("v", v, lambda e, tt=tt: e.tensor_tensor(
                        att_tm[:, tt, :], vc_tm[:, tt, :], v_tm[:, tt, :], ALU.add),
                        reads=[("vc_tm", tt), ("v_tm", tt)], writes=[("att", tt)])
                    p.op("v", v, lambda e, tt=tt, h3=h3, vv=vv: e.tensor_tensor(
                        vv(att_tm), vv(att_tm), h3(rden_[:, tt, :]), ALU.mult),
                        reads=[("att", tt), ("rden", tt)], writes=[("att", tt)])

            ln_stats_tm(p, block, lambda tt: att_tm[:, tt, :],
                        [("att", tt) for tt in range(NTT)],
                        stat, [0, 8], sq_scr, NTT, col_eps5, "eps5")

            @block.vector
            def _(v):
                for tt in range(NTT):
                    c0 = tt * 8
                    p.op("v", v, lambda e, tt=tt, c0=c0: e.tensor_scalar(
                        out_tm[:, tt, :], att_tm[:, tt, :],
                        stat[:, c0 + 1: c0 + 2], stat[:, c0 + 7: c0 + 8],
                        ALU.subtract, ALU.mult),
                        reads=[("att", tt), ("st", c0, 1), ("st", c0, 7)],
                        writes=[("out", tt)])
                    p.op("v", v, lambda e, tt=tt: e.tensor_tensor(
                        out_tm[:, tt, :], out_tm[:, tt, :], x_tm[:, buf, tt, :], ALU.add),
                        reads=[("out", tt), xtok], writes=[("out", tt)])

            @block.sync
            def _(sync):
                p.op("q", sync, lambda e, ci=ci: e.dma_start(
                    out=_tok_tiles(x_out, ci * CH, NTT), in_=out_tm[:, :, :]),
                    reads=[("out", tt) for tt in range(NTT)],
                    writes=["x_out_hbm"], inc=16, dma="xout")
                if ci + 2 < NCH:
                    load_chunk(sync, ci + 2)


def build_stage_c(T_=T, SEQ_=SEQ):
    return _std_wrap(decl_stage_c, emit_stage_c, T_, SEQ_)


def decl_stage_b(nc, pfx, T_, SEQ_, depth_=6, x_in=None, x_out=None):
    NT = SEQ_ // 128
    NB1 = 22
    d = lambda n, s, dt=F32, kind="ExternalInput": nc.dram_tensor(pfx + n, s, dt, kind=kind).ap()
    return {
        "x_in": x_in if x_in is not None else d("x_in", [T_, D]),
        "wqkv_in": d("wqkv_in", [depth_, D, 3072]),
        "wo_in": d("wo_in", [depth_, D, D]),
        "qs_in": d("qs_in", [depth_, 1, 64]),
        "ks_in": d("ks_in", [depth_, 1, 64]),
        "lng_in": d("lng_in", [depth_, DC, 128]),
        "lnb_in": d("lnb_in", [depth_, DC, 128]),
        "w1_in": d("w1_in", [depth_, D, 2 * NB1 * 128]),
        "b1_in": d("b1_in", [depth_, 2 * NB1, 128]),
        "w2_in": d("w2_in", [depth_, NB1 * 128, D]),
        "b2_in": d("b2_in", [depth_, 1, D]),
        "tab_cs": d("tab_cs", [128, NT, 64], BF16),
        "tab_ss": d("tab_ss", [128, NT, 64], BF16),
        "tab_ci": d("tab_ci", [128, NT, 64], BF16),
        "tab_si": d("tab_si", [128, NT, 64], BF16),
        "maskA": d("maskA", [128, 128], BF16),
        "maskC": d("maskC", [128, 128], BF16),
        "ident": d("ident", [128, 128], BF16),
        "identf": d("identf", [128, 128]),
        "onesf": d("onesf", [1, 128]),
        "onescb": d("onescb", [128, 1], BF16),
        "x_out": x_out if x_out is not None else d("x_out", [T_, D], F32, "ExternalOutput"),
    }


def emit_stage_b(nc, p, block, st_, pfx, A, T_=T, SEQ_=SEQ, depth_=6,
                 half_layers=None, debug_r=False):
    if half_layers is None:
        half_layers = 2 * depth_
    NT = SEQ_ // 128
    NW = SEQ_ // 256
    NSEQ = T_ // SEQ_
    NB1 = 22
    (x_in, wqkv_in, wo_in, qs_in, ks_in, lng_in, lnb_in, w1_in, b1_in, w2_in,
     b2_in, tab_cs_in, tab_ss_in, tab_ci_in, tab_si_in, maskA_in, maskC_in,
     ident, identf, onesf, onescb, x_out) = (
        A["x_in"], A["wqkv_in"], A["wo_in"], A["qs_in"], A["ks_in"], A["lng_in"],
        A["lnb_in"], A["w1_in"], A["b1_in"], A["w2_in"], A["b2_in"], A["tab_cs"],
        A["tab_ss"], A["tab_ci"], A["tab_si"], A["maskA"], A["maskC"], A["ident"],
        A["identf"], A["onesf"], A["onescb"], A["x_out"])

    if True:
        sb = lambda n, s, dt=F32: st_.enter_context(nc.sbuf_tensor(pfx + n, s, dt))
        r_tm = sb("r_tm", [128, NT, D])
        idn = sb("idn", [128, 128], BF16)
        idnf = sb("idnf", [128, 128])
        onesf_sb = sb("onesf_sb", [1, 128])
        onescb_sb = sb("onescb_sb", [128, 1], BF16)
        tabs = {k2: sb("tabsb_" + k2, [128, NT, 64], BF16) for k2 in ("cs", "ss", "ci", "si")}
        maskA = sb("mA", [128, 128], BF16)
        maskC = sb("mC", [128, 128], BF16)
        col_eps5 = sb("col_eps5", [128, 1])

        @block.sync
        def _(sync):
            for a, b2, tok in ((ident, idn, "idn"), (identf, idnf, "idnf"),
                               (onesf, onesf_sb, "onesf"),
                               (onescb, onescb_sb, "onescb"),
                               (maskA_in, maskA, "mA"), (maskC_in, maskC, "mC")):
                p.op("q", sync, lambda e, a=a, b2=b2: e.dma_start(out=b2[:], in_=a[:, :]),
                     writes=[tok], inc=16, dma=tok)
            for k2 in ("cs", "ss", "ci", "si"):
                p.op("q", sync, lambda e, k2=k2: e.dma_start(
                    out=tabs[k2][:], in_=[tab_cs_in, tab_ss_in, tab_ci_in, tab_si_in][
                        ["cs", "ss", "ci", "si"].index(k2)][:, :, :]),
                    writes=["tab" + k2], inc=16, dma="tab" + k2)

        @block.vector
        def _(v):
            p.op("v", v, lambda e: e.memset(col_eps5[:], 1e-5), writes=["eps5"])

        for sq in range(NSEQ):
            pro_stack = ExitStack()
            x_io = pro_stack.enter_context(nc.sbuf_tensor(pfx + "x_io_p%d" % sq, [128, 2, 2, D], F32))
            sq_scr = pro_stack.enter_context(nc.sbuf_tensor(pfx + "sqs_p%d" % sq, [128, D], F32))
            stat = pro_stack.enter_context(nc.sbuf_tensor(pfx + "stat_p%d" % sq, [128, 16], F32))

            def load_x(sync, sq, ci, key, x_io=x_io):
                p.op("q", sync, lambda e: e.dma_start(
                    out=x_io[:, ci % 2, :, :], in_=_tok_tiles(x_in, sq * SEQ_ + ci * 256, 2)),
                    writes=[(key, ci % 2)], inc=16, dma="%s%d" % (key, ci % 2))

            @block.sync
            def _(sync, sq=sq):
                load_x(sync, sq, 0, "xp")
                load_x(sync, sq, 1, "xp")

            for ci in range(NT // 2):
                buf = ci % 2

                ln_stats_tm(p, block, lambda tt, ci=ci, buf=buf, x_io=x_io: x_io[:, buf, tt, :],
                            [("xp", buf)] * 2, stat, [0, 8], sq_scr, 2, col_eps5, "eps5")

                @block.vector
                def _(v, ci=ci, buf=buf, x_io=x_io, stat=stat):
                    for tt in range(2):
                        c0 = tt * 8
                        tile = ci * 2 + tt
                        p.op("v", v, lambda e, tt=tt, c0=c0, tile=tile: e.tensor_scalar(
                            r_tm[:, tile, :], x_io[:, buf, tt, :],
                            stat[:, c0 + 1: c0 + 2], stat[:, c0 + 7: c0 + 8],
                            ALU.subtract, ALU.mult),
                            reads=[("xp", buf), ("st", c0, 1), ("st", c0, 7)],
                            writes=[("r", tile)])

                @block.sync
                def _(sync, sq=sq, ci=ci):
                    if ci + 2 < NT // 2:
                        load_x(sync, sq, ci + 2, "xp")

            p.barrier(block)
            pro_stack.close()

            for li in range(depth_):
                if 2 * li >= half_layers:
                    break
                _emit_attn_layer(nc, p, block, st_, li, sq, r_tm, idn, onesf_sb,
                                 tabs, maskA, maskC, wqkv_in, wo_in, qs_in, ks_in,
                                 NT, NW, pfx)
                if 2 * li + 1 >= half_layers:
                    break
                _emit_ffn_layer(nc, p, block, st_, li, sq, r_tm, idn, idnf, onesf_sb,
                                onescb_sb, col_eps5, lng_in, lnb_in, w1_in, b1_in,
                                w2_in, b2_in, NT, NB1, pfx)

            epi_stack = ExitStack()
            x_io = epi_stack.enter_context(nc.sbuf_tensor(pfx + "x_io_e%d" % sq, [128, 2, 2, D], F32))
            sq_scr = epi_stack.enter_context(nc.sbuf_tensor(pfx + "sqs_e%d" % sq, [128, D], F32))
            stat = epi_stack.enter_context(nc.sbuf_tensor(pfx + "stat_e%d" % sq, [128, 16], F32))

            def load_x(sync, sq, ci, key, x_io=x_io):
                p.op("q", sync, lambda e: e.dma_start(
                    out=x_io[:, ci % 2, :, :], in_=_tok_tiles(x_in, sq * SEQ_ + ci * 256, 2)),
                    writes=[(key, ci % 2)], inc=16, dma="%s%d" % (key, ci % 2))

            @block.sync
            def _(sync, sq=sq):
                load_x(sync, sq, 0, "xe")
                load_x(sync, sq, 1, "xe")

            for ci in range(NT // 2):
                buf = ci % 2

                if not debug_r:
                    ln_stats_tm(p, block, lambda tt, ci=ci: r_tm[:, ci * 2 + tt, :],
                                [("r", ci * 2 + tt) for tt in range(2)],
                                stat, [0, 8], sq_scr, 2, col_eps5, "eps5")

                @block.vector
                def _(v, sq=sq, ci=ci, buf=buf, x_io=x_io, stat=stat, sq_scr=sq_scr):
                    for tt in range(2):
                        c0 = tt * 8
                        tile = ci * 2 + tt
                        if debug_r:
                            p.op("v", v, lambda e, tt=tt, tile=tile: e.tensor_copy(
                                x_io[:, buf, tt, :], r_tm[:, tile, :]),
                                reads=[("r", tile), ("xe", buf)], writes=[("xe", buf)])
                        else:
                            p.op("v", v, lambda e, tt=tt, c0=c0, tile=tile: e.tensor_scalar(
                                sq_scr[:, :], r_tm[:, tile, :],
                                stat[:, c0 + 1: c0 + 2], stat[:, c0 + 7: c0 + 8],
                                ALU.subtract, ALU.mult),
                                reads=[("r", tile), ("st", c0, 1), ("st", c0, 7)],
                                writes=["sq_scr"])
                            p.op("v", v, lambda e, tt=tt: e.tensor_tensor(
                                x_io[:, buf, tt, :], x_io[:, buf, tt, :], sq_scr[:, :],
                                ALU.add),
                                reads=["sq_scr", ("xe", buf)], writes=[("xe", buf)])

                @block.sync
                def _(sync, sq=sq, ci=ci, buf=buf, x_io=x_io):
                    p.op("q", sync, lambda e: e.dma_start(
                        out=_tok_tiles(x_out, sq * SEQ_ + ci * 256, 2),
                        in_=x_io[:, buf, :, :]),
                        reads=[("xe", buf)], writes=["xoutb"], inc=16, dma="xoutb")
                    if ci + 2 < NT // 2:
                        load_x(sync, sq, ci + 2, "xe")

            p.barrier(block)
            epi_stack.close()


def build_stage_b(T_=T, SEQ_=SEQ, depth_=6, half_layers=None, debug_r=False):
    return _std_wrap(lambda nc, pfx, T2, S2: decl_stage_b(nc, pfx, T2, S2, depth_),
                     emit_stage_b, T_, SEQ_, depth_=depth_,
                     half_layers=half_layers, debug_r=debug_r)


def _emit_attn_layer(nc, p, block, st_, li, sq, r_tm, idn, onesf_sb, tabs,
                     maskA, maskC, wqkv_in, wo_in, qs_in, ks_in, NT, NW, pfx=""):
    tag = pfx + "_a%d_%d" % (li, sq)
    with ExitStack() as ph:
        sb = lambda n, s, dt=F32: ph.enter_context(nc.sbuf_tensor(n + tag, s, dt))
        ps = lambda n, s, dt=F32: ph.enter_context(nc.psum_tensor(n + tag, s, dt))
        wqkv_sb = sb("wqkv_sb", [128, DC, 3072], BF16)
        wo_sb = sb("wo_sb", [128, DC, D], BF16)
        qs_bc = sb("qs_bc", [128, 64])
        ks_bc = sb("ks_bc", [128, 64])
        qs_sg = sb("qs_sg", [128, 64])
        ks_sg = sb("ks_sg", [128, 64])
        qs_row = sb("qs_row", [1, 64])
        ks_row = sb("ks_row", [1, 64])
        xs_fm = sb("xs_fm", [128, 2, DC, 257], BF16)
        rcast = sb("rcast", [128, D], BF16)
        q_tm = sb("q_tm", [128, 2, D], BF16)
        k_tm = sb("k_tm", [128, 2, D], BF16)
        v_tm = sb("v_tm", [128, 3, 2, D], BF16)
        q_fm = sb("q_fm", [128, 2, DC, 256], BF16)
        k_fm = sb("k_fm", [128, 3, DC, 256], BF16)
        ao_fm = sb("ao_fm", [128, DC, 256], BF16)
        scr2 = sb("scr2", [128, D], BF16)
        scr3 = sb("scr3", [128, D], BF16)
        ss_t = sb("ss_t", [128, 16])
        rn_t = sb("rn_t", [128, 16])
        a1_t = sb("a1_t", [128, 64])
        a2_t = sb("a2_t", [128, 64])
        p_sb = sb("p_sb", [128, 6, 384], BF16)
        pT_sb = sb("pT_sb", [128, 6, 3, 128], BF16)
        sm1 = sb("sm1", [128, 6, 4])
        ptr = [ps("bptr0", [128, 4, 128], BF16), ps("bptr1", [128, 4, 128], BF16)]
        pq = [ps("bpq0", [128, 512])]
        psim = [ps("bpsim%d" % i2, [128, 384]) for i2 in range(4)]
        ppv = [ps("bppv0", [64, 4, 128])]

        @block.gpsimd
        def _(g):
            p.op("g", g, lambda e: e.dma_start(
                out=wqkv_sb[:], in_=_chunked(wqkv_in[li])), writes=["wqkv"], inc=16, dma="wqkv")
            p.op("g", g, lambda e: e.dma_start(
                out=wo_sb[:], in_=_chunked(wo_in[li])), writes=["wo"], inc=16, dma="wo")

        @block.sync
        def _(sync):
            p.op("q", sync, lambda e: e.dma_start(out=qs_row[:], in_=qs_in[li, :, :]),
                 writes=["qs_row"], inc=16, dma="qs_row")
            p.op("q", sync, lambda e: e.dma_start(out=ks_row[:], in_=ks_in[li, :, :]),
                 writes=["ks_row"], inc=16, dma="ks_row")

        col_ln8 = sb("col_ln8", [128, 1])

        @block.vector
        def _(v):
            p.op("v", v, lambda e: e.memset(col_ln8[:], 2.0794415416798357),
                 writes=["ln8"])

        for row, rtok, bc, btok, sg, sgtok in ((qs_row, "qs_row", qs_bc, "qs_bc", qs_sg, "qs_sg"),
                                               (ks_row, "ks_row", ks_bc, "ks_bc", ks_sg, "ks_sg")):
            @block.tensor
            def _(t, row=row, rtok=rtok):
                p.op("t", t, lambda e: e.matmul(pq[0][:, 0:64], onesf_sb[0:1, :], row[0:1, :],
                                                start=True, stop=True),
                     reads=["onesf", rtok], writes=[("bpq", 0)])

            @block.scalar
            def _(s, bc=bc, btok=btok):
                p.op("s", s, lambda e: e.copy(bc[:, :], pq[0][:, 0:64]),
                     reads=[("bpq", 0)], writes=[btok])

            @block.vector
            def _(v, bc=bc, btok=btok, sg=sg, sgtok=sgtok):
                p.op("v", v, lambda e: e.tensor_scalar_mul(sg[:, 0:32], bc[:, 32:64], -1.0),
                     reads=[btok], writes=[sgtok])
                p.op("v", v, lambda e: e.tensor_copy(sg[:, 32:64], bc[:, 0:32]),
                     reads=[btok], writes=[sgtok])

        tr_par = [0]
        pq_par = [0]
        pv_par = [0]
        unit_k = [0]

        def tr_pack4(srcs, dst_view, dtoks, rtoks):
            par = tr_par[0] % 2
            tr_par[0] += 1
            n = len(srcs)

            @block.tensor
            def _(t, srcs=srcs, par=par):
                def mmg(e):
                    for i2, src in enumerate(srcs):
                        ins = e.transpose(ptr[par][:, i2, :], src, idn[:])
                    return ins
                p.op("t", t, mmg, reads=rtoks + ["idn"], writes=[("bptr", par)])

            @block.scalar
            def _(s, dst_view=dst_view, par=par, n=n):
                p.op("s", s, lambda e: e.copy(dst_view, ptr[par][:, 0:n, :]),
                     reads=[("bptr", par)], writes=dtoks)

        def build_xs(w):
            wb = w % 2

            @block.vector
            def _(v, w=w, wb=wb):
                if w == 0:
                    p.op("v", v, lambda e: e.memset(xs_fm[:, wb, :, 0:1], 0.0),
                         reads=[("xs", w - 2)] if w >= 2 else [], writes=[("xscol", w)])
                else:
                    p.op("v", v, lambda e: e.tensor_copy(
                        xs_fm[:, wb, :, 0:1], xs_fm[:, 1 - wb, :, 256:257]),
                        reads=[("xs", w - 1)] + ([("xs", w - 2)] if w >= 2 else []),
                        writes=[("xscol", w)])

            for tt in range(2):
                tile = 2 * w + tt

                @block.vector
                def _(v, tile=tile):
                    p.op("v", v, lambda e: e.tensor_copy(rcast[:, :], r_tm[:, tile, :]),
                         reads=[("r", tile)], writes=["rcast"])

                for cg in range(2):
                    srcs = [rcast[:, (cg * 4 + i2) * 128:(cg * 4 + i2 + 1) * 128]
                            for i2 in range(4)]
                    tr_pack4(srcs,
                             xs_fm[:, wb, cg * 4:(cg + 1) * 4, 1 + tt * 128:1 + (tt + 1) * 128],
                             [("xs", w)], ["rcast"])

        def prep(w):
            wb = w % 2
            for o in range(6):
                for tt in range(2):
                    par = 0

                    @block.tensor
                    def _(t, w=w, wb=wb, o=o, tt=tt, par=par):
                        def mmg(e):
                            for c in range(DC):
                                off = 1 if c < 4 else 0
                                ins = e.matmul(pq[par][:, :],
                                               xs_fm[:, wb, c, off + tt * 128: off + (tt + 1) * 128],
                                               wqkv_sb[:, c, o * 512:(o + 1) * 512],
                                               start=(c == 0), stop=(c == DC - 1))
                            return ins
                        p.op("t", t, mmg, reads=[("xs", w), ("xscol", w), "wqkv"],
                             writes=[("bpq", par)])

                    @block.scalar
                    def _(s, w=w, wb=wb, o=o, tt=tt, par=par):
                        if o < 2:
                            p.op("s", s, lambda e: e.copy(
                                q_tm[:, tt, (o % 2) * 512:(o % 2 + 1) * 512], pq[par][:, :]),
                                reads=[("bpq", par)], writes=[("q_tm", tt)])
                        elif o < 4:
                            p.op("s", s, lambda e: e.copy(
                                k_tm[:, tt, (o % 2) * 512:(o % 2 + 1) * 512], pq[par][:, :]),
                                reads=[("bpq", par)], writes=[("k_tm", tt)])
                        else:
                            p.op("s", s, lambda e: e.copy(
                                v_tm[:, w % 3, tt, (o % 2) * 512:(o % 2 + 1) * 512], pq[par][:, :]),
                                reads=[("bpq", par)], writes=[("v", w % 3, tt)])

            for src, stok, bc, btok, sg, sgtok, tcs, tss in (
                    (q_tm, "q_tm", qs_bc, "qs_bc", qs_sg, "qs_sg", "cs", "ss"),
                    (k_tm, "k_tm", ks_bc, "ks_bc", ks_sg, "ks_sg", "ci", "si")):
                for tt in range(2):
                    tg = 2 * w + tt

                    @block.scalar
                    def _(s, src=src, stok=stok, tt=tt):
                        p.op("s", s, lambda e: e.activation(
                            scr2[:, :], src[:, tt, :], AF.Square),
                            reads=[(stok, tt)], writes=["scr2"])

                    @block.vector
                    def _(v, tt=tt):
                        p.op("v", v, lambda e: e.tensor_reduce(
                            ss_t[:, :], scr2[:, :].rearrange("p (h d) -> p h d", d=64),
                            AX.X, ALU.add),
                            reads=["scr2"], writes=["ss_t"])
                        p.op("v", v, lambda e: e.tensor_scalar_max(
                            ss_t[:, :], ss_t[:, :], 1e-24),
                            reads=["ss_t"], writes=["ss_t"])

                    @block.scalar
                    def _(s, stok=stok):
                        p.op("s", s, lambda e: e.activation(rn_t[:, :], ss_t[:, :], AF.Ln),
                             reads=["ss_t"], writes=["rn_t"])
                        if stok == "q_tm":
                            p.op("s", s, lambda e: e.activation(
                                rn_t[:, :], rn_t[:, :], AF.Exp, scale=-0.5,
                                bias=col_ln8[:, 0:1]),
                                reads=["rn_t", "ln8"], writes=["rn_t"])
                        else:
                            p.op("s", s, lambda e: e.activation(
                                rn_t[:, :], rn_t[:, :], AF.Exp, scale=-0.5),
                                reads=["rn_t"], writes=["rn_t"])

                    @block.vector
                    def _(v, tg=tg, bc=bc, btok=btok, sg=sg, sgtok=sgtok, tcs=tcs, tss=tss):
                        p.op("v", v, lambda e: e.tensor_tensor(
                            a1_t[:, :], tabs[tcs][:, tg, :], bc[:, :], ALU.mult),
                            reads=["tab" + tcs, btok], writes=["a1_t"])
                        p.op("v", v, lambda e: e.tensor_tensor(
                            a2_t[:, :], tabs[tss][:, tg, :], sg[:, :], ALU.mult),
                            reads=["tab" + tss, sgtok], writes=["a2_t"])

                    h64 = lambda buf_, ttt: buf_[:, ttt, :].rearrange("p (h d) -> p h d", d=64)

                    @block.gpsimd
                    def _(g, src=src, stok=stok, tt=tt, h64=h64):
                        p.op("g", g, lambda e: e.tensor_tensor(
                            scr2[:, :].rearrange("p (h d) -> p h d", d=64), h64(src, tt),
                            a1_t[:, :].unsqueeze(1).broadcast_to([128, 16, 64]), ALU.mult),
                            reads=[(stok, tt), "a1_t"], writes=["scr2"])
                        p.op("g", g, lambda e: e.tensor_tensor(
                            scr3[:, :].rearrange("p (h d) -> p h d", d=64)[:, :, 0:32],
                            h64(src, tt)[:, :, 32:64],
                            a2_t[:, 0:32].unsqueeze(1).broadcast_to([128, 16, 32]), ALU.mult),
                            reads=[(stok, tt), "a2_t"], writes=["scr3"])
                        p.op("g", g, lambda e: e.tensor_tensor(
                            scr3[:, :].rearrange("p (h d) -> p h d", d=64)[:, :, 32:64],
                            h64(src, tt)[:, :, 0:32],
                            a2_t[:, 32:64].unsqueeze(1).broadcast_to([128, 16, 32]), ALU.mult),
                            reads=[(stok, tt), "a2_t"], writes=["scr3"])
                        p.op("g", g, lambda e: e.tensor_tensor(
                            scr2[:, :], scr2[:, :], scr3[:, :], ALU.add),
                            reads=["scr2", "scr3"], writes=["scr2"])

                    @block.vector
                    def _(v, src=src, stok=stok, tt=tt):
                        p.op("v", v, lambda e: e.tensor_tensor(
                            src[:, tt, :].rearrange("p (h d) -> p h d", d=64),
                            scr2[:, :].rearrange("p (h d) -> p h d", d=64),
                            rn_t[:, :].unsqueeze(2).broadcast_to([128, 16, 64]), ALU.mult),
                            reads=["scr2", "rn_t", (stok, tt)],
                            writes=[(stok + "r", tt)])

        def prep_tr(w):
            wb = w % 2
            for src, stok, dstf, dtokf in ((q_tm, "q_tm", q_fm, "q_fm"),
                                           (k_tm, "k_tm", k_fm, "k_fm")):
                wi = wb if dtokf == "q_fm" else w % 3
                for tt in range(2):
                    for cg in range(2):
                        srcs = [src[:, tt, (cg * 4 + i2) * 128:(cg * 4 + i2 + 1) * 128]
                                for i2 in range(4)]
                        tr_pack4(srcs,
                                 dstf[:, wi, cg * 4:(cg + 1) * 4, tt * 128:(tt + 1) * 128],
                                 [(dtokf, wi)], [(stok + "r", tt)])

        def head_sims(w, u):
            wb = w % 2
            tt, h, subs, valid = u
            hp, hc = h % 2, h // 2
            k = unit_k[0]
            spar = k % 4

            @block.tensor
            def _(t, wb=wb, tt=tt, hp=hp, hc=hc, spar=spar, subs=subs):
                def mmg(e):
                    ins = None
                    for ks, s_ in enumerate(subs):
                        if s_ is None:
                            continue
                        kb2, kt = s_
                        m = [maskA, None, maskC][ks]
                        ins = e.matmul(
                            psim[spar][:, ks * 128:(ks + 1) * 128],
                            q_fm[hp * 64:(hp + 1) * 64, wb, hc, tt * 128:(tt + 1) * 128],
                            k_fm[hp * 64:(hp + 1) * 64, kb2, hc, kt * 128:(kt + 1) * 128],
                            start=True, stop=(m is None))
                        if m is not None:
                            ins = e.matmul(psim[spar][:, ks * 128:(ks + 1) * 128],
                                           idn[:, :], m[:, :], start=False, stop=True)
                    return ins
                p.op("t", t, mmg, reads=[("q_fm", wb), ("k_fm", 0), ("k_fm", 1),
                                         ("k_fm", 2), "mA", "mC", "idn"],
                     writes=[("psim", spar)])

        def head_stats(k, u):
            tt, h, subs, valid = u
            spar = k % 4
            pi = k % 6

            @block.vector
            def _(v, spar=spar, valid=valid, pi=pi):
                vs = slice(valid[0] * 128, 384)
                p.op("v", v, lambda e, vs=vs: e.reduce_max(
                    sm1[:, pi, 1:2], psim[spar][:, vs], axis=AX.X, negate=True),
                    reads=[("psim", spar)], writes=[("negm", pi)])

            @block.scalar
            def _(s, valid=valid, spar=spar, pi=pi):
                vs = slice(valid[0] * 128, 384)
                p.op("s", s, lambda e, vs=vs: e.activation(
                    p_sb[:, pi, vs], psim[spar][:, vs], AF.Exp,
                    bias=sm1[:, pi, 1:2], accum_out=sm1[:, pi, 2:3]),
                    reads=[("psim", spar), ("negm", pi)],
                    writes=[("p", pi), ("sumex", pi)])

        def head_out(w, k, u, gpos, gfirst, glast, hc0):
            tt, h, subs, valid = u
            hp, hc = h % 2, h // 2
            pi = k % 6

            @block.vector
            def _(v, valid=valid, pi=pi):
                vs = slice(valid[0] * 128, 384)
                p.op("v", v, lambda e: e.reciprocal(sm1[:, pi, 3:4], sm1[:, pi, 2:3]),
                     reads=[("sumex", pi)], writes=[("rsum", pi)])
                p.op("v", v, lambda e, vs=vs: e.tensor_scalar_mul(
                    p_sb[:, pi, vs], p_sb[:, pi, vs], sm1[:, pi, 3:4]),
                    reads=[("p", pi), ("rsum", pi)], writes=[("p", pi)])

            srcs = [p_sb[:, pi, ks * 128:(ks + 1) * 128] for ks in valid]
            par = tr_par[0] % 2
            tr_par[0] += 1

            @block.tensor
            def _(t, srcs=srcs, par=par):
                def mmg(e):
                    for i2, src in enumerate(srcs):
                        ins = e.transpose(ptr[par][:, i2, :], src, idn[:])
                    return ins
                p.op("t", t, mmg, reads=[("p", pi), "idn"], writes=[("bptr", par)])

            @block.scalar
            def _(s, par=par, pi=pi, valid=valid):
                p.op("s", s, lambda e: e.copy(
                    pT_sb[:, pi, 0:len(valid), :], ptr[par][:, 0:len(valid), :]),
                    reads=[("bptr", par)], writes=[("pT", pi)])

            if gfirst:
                pv_par[0] += 1
            pvp = 0

            @block.tensor
            def _(t, h=h, subs=subs, valid=valid, pvp=pvp, pi=pi, gpos=gpos):
                def mmg(e):
                    first = True
                    for i2, ks in enumerate(valid):
                        kb2, kt = subs[ks]
                        ins = e.matmul(ppv[pvp][:, gpos, :],
                                       v_tm[:, kb2, kt, h * 64:(h + 1) * 64],
                                       pT_sb[:, pi, i2, :],
                                       start=first, stop=(i2 == len(valid) - 1))
                        first = False
                    return ins
                p.op("t", t, mmg,
                     reads=[("pT", pi)] +
                           [("v", i, j) for i in range(3) for j in range(2)],
                     writes=[("ppv", pvp)])

            if glast:
                @block.scalar
                def _(s, tt=tt, hp=hp, hc0=hc0, pvp=pvp):
                    p.op("s", s, lambda e: e.copy(
                        ao_fm[hp * 64:(hp + 1) * 64, hc0:hc0 + 4, tt * 128:(tt + 1) * 128],
                        ppv[pvp][:, :, :]),
                        reads=[("ppv", pvp)], writes=["ao_fm"])

        def attn_units(w):
            p0 = w % 3
            pm1 = (w - 1) % 3
            units = []
            for tt in range(2):
                if tt == 0:
                    subs = [(pm1, 0), (pm1, 1), (p0, 0)]
                else:
                    subs = [(pm1, 1), (p0, 0), (p0, 1)]
                if w == 0:
                    subs = [None if s_[0] == pm1 else s_ for s_ in subs]
                valid = [i for i, s_ in enumerate(subs) if s_ is not None]
                for h in (0, 2, 4, 6, 1, 3, 5, 7, 8, 10, 12, 14, 9, 11, 13, 15):
                    units.append((tt, h, subs, valid))
            return units

        def wo_update(w):
            for tt in range(2):
                tile = 2 * w + tt
                for nb in range(2):
                    par = 0

                    @block.tensor
                    def _(t, tt=tt, nb=nb, par=par):
                        def mmg(e):
                            for c in range(DC):
                                ins = e.matmul(pq[par][:, :],
                                               ao_fm[:, c, tt * 128:(tt + 1) * 128],
                                               wo_sb[:, c, nb * 512:(nb + 1) * 512],
                                               start=(c == 0), stop=(c == DC - 1))
                            return ins
                        p.op("t", t, mmg, reads=["ao_fm", "wo"], writes=[("bpq", par)])

                    @block.vector
                    def _(v, tile=tile, nb=nb, par=par):
                        p.op("v", v, lambda e: e.tensor_tensor(
                            r_tm[:, tile, nb * 512:(nb + 1) * 512],
                            r_tm[:, tile, nb * 512:(nb + 1) * 512],
                            pq[par][:, :], ALU.add),
                            reads=[("bpq", par), ("r", tile)], writes=[("r", tile)])

        build_xs(0)
        prep(0)
        prep_tr(0)
        for w in range(NW):
            units = attn_units(w)
            n_u = len(units)
            k0 = unit_k[0]
            for j in range(n_u + 5):
                if j == 2 and w + 1 < NW:
                    build_xs(w + 1)
                    prep(w + 1)
                if j == n_u // 2 and w + 1 < NW:
                    prep_tr(w + 1)
                if j < n_u:
                    head_sims(w, units[j])
                    unit_k[0] += 1
                if 0 <= j - 5 < n_u:
                    jj = j - 5
                    gpos = jj % 4
                    hc0 = units[jj][1] // 2 - gpos
                    head_out(w, k0 + jj, units[jj], gpos, gpos == 0, gpos == 3, hc0)
                if 0 <= j - 1 < n_u:
                    head_stats(k0 + j - 1, units[j - 1])
            wo_update(w)

        p.barrier(block)

def _emit_ffn_layer(nc, p, block, st_, li, sq, r_tm, idn, idnf, onesf_sb, onescb_sb,
                    col_eps5, lng_in, lnb_in, w1_in, b1_in, w2_in, b2_in, NT, NB1, pfx=""):
    NCF = NT // 4
    tag = pfx + "_f%d_%d" % (li, sq)
    with ExitStack() as ph:
        sb = lambda n, s, dt=F32: ph.enter_context(nc.sbuf_tensor(n + tag, s, dt))
        ps = lambda n, s, dt=F32: ph.enter_context(nc.psum_tensor(n + tag, s, dt))
        w2_sb = sb("w2_sb", [128, NB1, D], BF16)
        w1blk = sb("w1blk", [128, 2, DC, 256], BF16)
        act_fm = sb("act_fm", [128, NB1, 512], BF16)
        xs2_fm = sb("xs2_fm", [128, DC, 513], BF16)
        sqf = sb("sqf", [128, DC, 513], BF16)
        xn_fm = sb("xn_fm", [128, DC, 512], BF16)
        rcast = sb("rcast2", [128, D], BF16)
        gb_s = sb("gb_s", [128, 2, 512], BF16)
        m_bc = sb("m_bc", [128, 512])
        rs_bc = sb("rs_bc", [128, 512])
        b2_bc = sb("b2_bc", [128, D])
        scr_w = sb("scr_w", [128, 512])
        g_cols = sb("g_cols", [128, DC])
        b_cols = sb("b_cols", [128, DC])
        b1_cols = sb("b1_cols", [128, 2 * NB1])
        gtmp = sb("gtmp", [DC, 128])
        btmp = sb("btmp", [DC, 128])
        b1tmp = sb("b1tmp", [2 * NB1, 128])
        b2row = sb("b2row", [1, D])
        srow = sb("srow", [1, 2 * 512])
        rrow = sb("rrow", [1, 512])
        ptr2 = [ps("fptr0", [128, 4, 128], BF16)]
        pab = [ps("fpa0", [128, 512]), ps("fpb0", [128, 512]),
               ps("fpa1", [128, 512]), ps("fpb1", [128, 512])]
        pbc = ps("fpbc", [128, 512])
        pst = pbc[0:1, :]
        pyw = [ps("fpy0", [128, 512]), ps("fpy1", [128, 512])]

        @block.gpsimd
        def _(g):
            p.op("g", g, lambda e: e.dma_start(
                out=w2_sb[:], in_=w2_in[li].rearrange("(kc kp) n -> kp kc n", kp=128)),
                writes=["w2sb"], inc=16, dma="w2sb")

        @block.sync
        def _(sync):
            p.op("q", sync, lambda e: e.dma_start(out=gtmp[:], in_=lng_in[li, :, :]),
                 writes=["gtmp"], inc=16, dma="gtmp")
            p.op("q", sync, lambda e: e.dma_start(out=btmp[:], in_=lnb_in[li, :, :]),
                 writes=["btmp"], inc=16, dma="btmp")
            p.op("q", sync, lambda e: e.dma_start(out=b1tmp[:], in_=b1_in[li, :, :]),
                 writes=["b1tmp"], inc=16, dma="b1tmp")
            p.op("q", sync, lambda e: e.dma_start(out=b2row[:], in_=b2_in[li, :, :]),
                 writes=["b2row"], inc=16, dma="b2row")

        for src, stok, dst, dtok, n in ((gtmp, "gtmp", g_cols, "g_cols", DC),
                                        (btmp, "btmp", b_cols, "b_cols", DC),
                                        (b1tmp, "b1tmp", b1_cols, "b1_cols", 2 * NB1)):
            @block.tensor
            def _(t, src=src, stok=stok, n=n):
                p.op("t", t, lambda e: e.transpose(
                    pbc[:, 0:n], src[:, :], idnf[0:n, 0:n]),
                    reads=[stok, "idnf"], writes=["fpbc"])

            @block.scalar
            def _(s, dst=dst, dtok=dtok, n=n):
                p.op("s", s, lambda e: e.copy(dst[:, 0:n], pbc[:, 0:n]),
                     reads=["fpbc"], writes=[dtok])

        for nb in range(2):
            @block.tensor
            def _(t, nb=nb):
                p.op("t", t, lambda e: e.matmul(
                    pbc[:, :], onesf_sb[0:1, :], b2row[0:1, nb * 512:(nb + 1) * 512],
                    start=True, stop=True),
                    reads=["onesf", "b2row"], writes=["fpbc"])

            @block.scalar
            def _(s, nb=nb):
                p.op("s", s, lambda e: e.copy(b2_bc[:, nb * 512:(nb + 1) * 512], pbc[:, :]),
                     reads=["fpbc"], writes=["b2_bc"])

        tr_par = [0]
        ab_seq = [0]
        py_par = [0]

        def build_xs2(cf):
            @block.vector
            def _(v, cf=cf):
                if cf == 0:
                    p.op("v", v, lambda e: e.memset(xs2_fm[:, :, 0:1], 0.0),
                         reads=["xs2"], writes=["xs2"])
                else:
                    p.op("v", v, lambda e: e.tensor_copy(
                        xs2_fm[:, :, 0:1], xs2_fm[:, :, 512:513]),
                        reads=["xs2"], writes=["xs2"])

            for tt in range(4):
                tile = 4 * cf + tt

                @block.vector
                def _(v, tile=tile):
                    p.op("v", v, lambda e: e.tensor_copy(rcast[:, :], r_tm[:, tile, :]),
                         reads=[("r", tile)], writes=["rcast2"])

                for cg in range(2):
                    @block.tensor
                    def _(t, cg=cg):
                        def mmg(e):
                            for i2 in range(4):
                                c = cg * 4 + i2
                                ins = e.transpose(ptr2[0][:, i2, :],
                                                  rcast[:, c * 128:(c + 1) * 128], idn[:])
                            return ins
                        p.op("t", t, mmg, reads=["rcast2", "idn"], writes=["fptr0"])

                    @block.scalar
                    def _(s, tt=tt, cg=cg):
                        p.op("s", s, lambda e: e.copy(
                            xs2_fm[:, cg * 4:(cg + 1) * 4, 1 + tt * 128:1 + (tt + 1) * 128],
                            ptr2[0][:, :, :]),
                            reads=["fptr0"], writes=["xs2"])

        build_xs2(0)

        for cf in range(NCF):
            def shslice(c, n=512):
                off = 1 if c < 4 else 0
                return slice(off, off + n)

            @block.scalar
            def _(s, cf=cf):
                p.op("s", s, lambda e: e.activation(
                    sqf[:, :, :], xs2_fm[:, :, :], AF.Square),
                    reads=["xs2"], writes=["sqf"])

            for which, srcbuf, rowi in (("sum", None, 0), ("sq", None, 1)):
                @block.tensor
                def _(t, which=which, cf=cf, rowi=rowi):
                    def mmg(e):
                        for c in range(DC):
                            src = (xs2_fm[:, c, shslice(c)] if which == "sum"
                                   else sqf[:, c, shslice(c)])
                            ins = e.matmul(pst[0:1, :], onescb_sb[:, 0:1],
                                           src, start=(c == 0), stop=(c == DC - 1))
                        return ins
                    p.op("t", t, mmg,
                         reads=["xs2", "sqf", "onescb"],
                         writes=["fpbc"])

                @block.vector
                def _(v, rowi=rowi):
                    p.op("v", v, lambda e: e.tensor_copy(
                        srow[0:1, rowi * 512:(rowi + 1) * 512], pst[0:1, :]),
                        reads=["fpbc"], writes=[("srow", rowi)])

            @block.vector
            def _(v):
                p.op("v", v, lambda e: e.tensor_scalar_mul(
                    srow[0:1, 0:512], srow[0:1, 0:512], 1.0 / D),
                    reads=[("srow", 0)], writes=[("srow", 0)])
                p.op("v", v, lambda e: e.tensor_tensor(
                    rrow[0:1, :], srow[0:1, 0:512], srow[0:1, 0:512], ALU.mult),
                    reads=[("srow", 0)], writes=["rrow"])
                p.op("v", v, lambda e: e.scalar_tensor_tensor(
                    srow[0:1, 512:1024], srow[0:1, 512:1024], 1.0 / D, rrow[0:1, :],
                    ALU.mult, ALU.subtract),
                    reads=[("srow", 1), "rrow"], writes=[("srow", 1)])

            @block.scalar
            def _(s):
                p.op("s", s, lambda e: e.activation(
                    rrow[0:1, :], srow[0:1, 512:1024], AF.Ln, bias=col_eps5[0:1, 0:1]),
                    reads=[("srow", 1), "eps5"], writes=["rrow"])
                p.op("s", s, lambda e: e.activation(
                    rrow[0:1, :], rrow[0:1, :], AF.Exp, scale=-0.5),
                    reads=["rrow"], writes=["rrow"])

            for src_row, rtok, dst, dtok in ((srow[0:1, 0:512], ("srow", 0), m_bc, "m_bc"),
                                             (rrow[0:1, :], "rrow", rs_bc, "rs_bc")):
                @block.tensor
                def _(t, src_row=src_row, rtok=rtok):
                    p.op("t", t, lambda e: e.matmul(pbc[:, :], onesf_sb[0:1, :], src_row,
                                                    start=True, stop=True),
                         reads=["onesf", rtok], writes=["fpbc"])

                @block.scalar
                def _(s, dst=dst, dtok=dtok):
                    p.op("s", s, lambda e: e.copy(dst[:, :], pbc[:, :]),
                         reads=["fpbc"], writes=[dtok])

            for c in range(DC):
                @block.gpsimd
                def _(g, cf=cf, c=c):
                    p.op("g", g, lambda e: e.tensor_tensor(
                        scr_w[:, :], xs2_fm[:, c, shslice(c)], m_bc[:, :], ALU.subtract),
                        reads=["xs2", "m_bc"], writes=["scr_w"])
                    p.op("g", g, lambda e: e.tensor_tensor(
                        scr_w[:, :], scr_w[:, :], rs_bc[:, :], ALU.mult),
                        reads=["scr_w", "rs_bc"], writes=["scr_w"])

                @block.vector
                def _(v, c=c):
                    p.op("v", v, lambda e: e.tensor_scalar(
                        xn_fm[:, c, :], scr_w[:, :], g_cols[:, c:c + 1], b_cols[:, c:c + 1],
                        ALU.mult, ALU.add),
                        reads=["scr_w", "g_cols", "b_cols"], writes=["xn_fm"])

            for i in range(NB1):
                wpar = i % 2

                @block.gpsimd
                def _(g, li=li, i=i, wpar=wpar):
                    p.op("g", g, lambda e: e.dma_start(
                        out=w1blk[:, wpar, :, :],
                        in_=w1_in[li, :, i * 256:(i + 1) * 256].rearrange(
                            "(kc kp) n -> kp kc n", kp=128)),
                        writes=[("w1b", wpar)], inc=16, dma="w1b%d" % wpar)

                @block.tensor
                def _(t, i=i, wpar=wpar):
                    def mma(e):
                        for c in range(DC):
                            ins = e.matmul(pab[2 * wpar][:, :], w1blk[:, wpar, c, 0:128],
                                           xn_fm[:, c, :], start=(c == 0), stop=(c == DC - 1))
                        return ins
                    p.op("t", t, mma, reads=[("w1b", wpar), "xn_fm"], writes=[("fpa", wpar)])

                    def mmb(e):
                        for c in range(DC):
                            ins = e.matmul(pab[2 * wpar + 1][:, :], w1blk[:, wpar, c, 128:256],
                                           xn_fm[:, c, :], start=(c == 0), stop=(c == DC - 1))
                        return ins
                    p.op("t", t, mmb, reads=[("w1b", wpar), "xn_fm"], writes=[("fpb", wpar)])

                @block.scalar
                def _(s, i=i, wpar=wpar):
                    p.op("s", s, lambda e: e.activation(
                        gb_s[:, wpar, :], pab[2 * wpar + 1][:, :], AF.Gelu,
                        bias=b1_cols[:, 2 * i + 1:2 * i + 2]),
                        reads=[("fpb", wpar), "b1_cols"], writes=[("gb_s", wpar)])

                @block.vector
                def _(v, i=i, wpar=wpar):
                    p.op("v", v, lambda e: e.scalar_tensor_tensor(
                        act_fm[:, i, :], pab[2 * wpar][:, :], b1_cols[:, 2 * i:2 * i + 1],
                        gb_s[:, wpar, :], ALU.add, ALU.mult),
                        reads=[("fpa", wpar), ("gb_s", wpar), "b1_cols"], writes=[("act", i)])

            for tt in range(4):
                tile = 4 * cf + tt
                for nb in range(2):
                    par = py_par[0] % 2
                    py_par[0] += 1

                    @block.tensor
                    def _(t, tt=tt, nb=nb, par=par):
                        def mmg(e):
                            for kc in range(NB1):
                                ins = e.matmul(pyw[par][:, :], act_fm[:, kc, tt * 128:(tt + 1) * 128],
                                               w2_sb[:, kc, nb * 512:(nb + 1) * 512],
                                               start=(kc == 0), stop=(kc == NB1 - 1))
                            return ins
                        p.op("t", t, mmg, reads=[("act", kc2) for kc2 in range(NB1)] + ["w2sb"],
                             writes=[("fpy", par)])

                    @block.vector
                    def _(v, tile=tile, nb=nb, par=par):
                        p.op("v", v, lambda e: e.tensor_tensor(
                            scr_w[:, :], pyw[par][:, :], b2_bc[:, nb * 512:(nb + 1) * 512],
                            ALU.add),
                            reads=[("fpy", par), "b2_bc"], writes=["scr_w"])
                        p.op("v", v, lambda e: e.tensor_tensor(
                            r_tm[:, tile, nb * 512:(nb + 1) * 512],
                            r_tm[:, tile, nb * 512:(nb + 1) * 512],
                            scr_w[:, :], ALU.add),
                            reads=["scr_w", ("r", tile)], writes=[("r", tile)])

            if cf + 1 < NCF:
                build_xs2(cf + 1)

        p.barrier(block)


def _stage_b_consts(SEQ_):
    dim_head, window = 64, 256
    half = np.arange(0, dim_head, 2, dtype=np.float32)
    inv_freq = (1.0 / 10000.0 ** (half / dim_head)).astype(np.float32)
    t_ = np.arange(SEQ_, dtype=np.float32)
    freqs = t_[:, None] * inv_freq[None, :]
    freqs = np.concatenate([freqs, freqs], -1)
    base_scale = (half + 0.4 * dim_head) / (1.4 * dim_head)
    power = (t_ - SEQ_ // 2) / (window // 2)
    scale = (base_scale[None, :] ** power[:, None]).astype(np.float32)
    scale = np.concatenate([scale, scale], -1)
    cos, sin = np.cos(freqs), np.sin(freqs)
    NT = SEQ_ // 128
    shp = lambda a: np.ascontiguousarray(
        a.reshape(NT, 128, 64).transpose(1, 0, 2).astype(ml_dtypes.bfloat16))
    u = np.arange(128)[:, None]
    v = np.arange(128)[None, :]
    maskA = np.where(u <= v, 0.0, -3.0e38).astype(ml_dtypes.bfloat16)
    maskC = np.where(v <= u, 0.0, -3.0e38).astype(ml_dtypes.bfloat16)
    return {
        "tab_cs": shp(cos * scale), "tab_ss": shp(sin * scale),
        "tab_ci": shp(cos / scale), "tab_si": shp(sin / scale),
        "maskA": maskA, "maskC": maskC,
        "onescb": np.ones((128, 1), dtype=ml_dtypes.bfloat16),
        "onesf": np.ones((1, 128), np.float32),
    }


def _stage_b_weights(Wqkv, q_scale, k_scale, Wo, ln_g, ln_b, W1, b1, W2, b2):
    dep = Wqkv.shape[0]
    NB1 = 22
    a = W1[:, :, :2730]
    g = W1[:, :, 2730:]
    ba = b1[:, :2730]
    bg = b1[:, 2730:]
    W1P = np.zeros((dep, D, 2 * NB1 * 128), dtype=np.float32)
    b1P = np.zeros((dep, 2 * NB1, 128), dtype=np.float32)
    for i in range(NB1):
        lo, hi = i * 128, min((i + 1) * 128, 2730)
        n = hi - lo
        W1P[:, :, i * 256: i * 256 + n] = a[:, :, lo:hi]
        W1P[:, :, i * 256 + 128: i * 256 + 128 + n] = g[:, :, lo:hi]
        b1P[:, 2 * i, :n] = ba[:, lo:hi]
        b1P[:, 2 * i + 1, :n] = bg[:, lo:hi]
    W2P = np.zeros((dep, NB1 * 128, D), dtype=np.float32)
    W2P[:, :2730, :] = W2
    return {
        "wqkv_in": np.ascontiguousarray(Wqkv), "wo_in": np.ascontiguousarray(Wo),
        "qs_in": np.ascontiguousarray(q_scale.reshape(dep, 1, 64)),
        "ks_in": np.ascontiguousarray(k_scale.reshape(dep, 1, 64)),
        "lng_in": np.ascontiguousarray(ln_g.reshape(dep, DC, 128)),
        "lnb_in": np.ascontiguousarray(ln_b.reshape(dep, DC, 128)),
        "w1_in": W1P, "b1_in": b1P, "w2_in": W2P,
        "b2_in": np.ascontiguousarray(b2.reshape(dep, 1, D)),
    }


def run_stage_b(x_full, Wqkv, q_scale, k_scale, Wo, ln_g, ln_b, W1, b1, W2, b2,
                half_layers=None, debug_r=False):
    nc = build_stage_b(half_layers=half_layers, debug_r=debug_r)
    shards = _shard_x(x_full)
    wts = _stage_b_weights(Wqkv, q_scale, k_scale, Wo, ln_g, ln_b, W1, b1, W2, b2)
    consts = {**_stage_b_consts(SEQ), **_consts()}
    del consts["arange"]
    in_maps = [{"x_in": shards[c], **wts, **consts} for c in range(NCORES)]
    res = run_bass_kernel_spmd(nc, in_maps, core_ids=list(range(NCORES)))
    return np.stack([res.results[c]["x_out"].reshape(SPC, SEQ, D) for c in range(NCORES)]
                    ).reshape(NCORES * SPC, SEQ, D)


def run_stage_c(x_full, wqkv):
    nc = build_stage_c()
    shards = _shard_x(x_full)
    wr = np.ascontiguousarray(wqkv.reshape(D, 16, 192))
    w1q = np.ascontiguousarray(wr[:, :, 0:64].reshape(D, D))
    w1k = np.ascontiguousarray(wr[:, :, 64:128].reshape(D, D))
    w1v = np.ascontiguousarray(wr[:, :, 128:192].reshape(D, D))
    consts = _consts()
    in_maps = [{"x_in": shards[c], "w_q": w1q, "w_k": w1k, "w_v": w1v, **consts}
               for c in range(NCORES)]
    res = run_bass_kernel_spmd(nc, in_maps, core_ids=list(range(NCORES)))
    return np.stack([res.results[c]["x_out"].reshape(SPC, SEQ, D) for c in range(NCORES)]
                    ).reshape(NCORES * SPC, SEQ, D)


def decl_stage_d(nc, pfx, T_, SEQ_=None, x_in=None, x_out=None):
    NE = 16
    d = lambda n, s, dt=F32, kind="ExternalInput": nc.dram_tensor(pfx + n, s, dt, kind=kind).ap()
    return {
        "x_in": x_in if x_in is not None else d("x_in", [T_, D]),
        "ns_in": d("ns_in", [1, D]),
        "wg_in": d("wg_in", [D, NE]), "bg_in": d("bg_in", [1, NE]),
        "w1_in": d("w1_in", [NE, D, 2048]), "b1_in": d("b1_in", [NE, 16, 128]),
        "w2_in": d("w2_in", [NE, D, D]), "b2_in": d("b2_in", [NE, D]),
        "ident": d("ident", [128, 128], BF16), "identf": d("identf", [128, 128]),
        "onesf": d("onesf", [1, 128]), "esel": d("esel", [NE, NE * 128]),
        "x_out": x_out if x_out is not None else d("x_out", [T_, D], F32, "ExternalOutput"),
        "z_hbm": d("z_hbm", [T_, D], F32, "Internal"),
        "xn2_hbm": d("xn2_hbm", [DC, 128, T_], BF16, "Internal"),
    }


def emit_stage_d(nc, p, block, st_, pfx, A, T_=T, SEQ_=SEQ):
    NE = 16
    (x_in, ns_in, wg_in, bg_in, w1_in, b1_in, w2_in, b2_in, ident, identf,
     onesf, esel, x_out, z_hbm, xn2_hbm) = (
        A["x_in"], A["ns_in"], A["wg_in"], A["bg_in"], A["w1_in"], A["b1_in"],
        A["w2_in"], A["b2_in"], A["ident"], A["identf"], A["onesf"], A["esel"],
        A["x_out"], A["z_hbm"], A["xn2_hbm"])

    CH = 256
    NCH = T_ // CH
    NTT = CH // 128
    TOK = 256
    NSLAB = T_ // TOK

    if True:
        sb = lambda n, s, dt=F32: st_.enter_context(nc.sbuf_tensor(pfx + n, s, dt))
        ps = lambda n, s, dt=F32: st_.enter_context(nc.psum_tensor(pfx + n, s, dt))
        idn = sb("idn", [128, 128], BF16)
        idnf = sb("idnf", [128, 128], F32)
        onesf_sb = sb("onesf_sb", [1, 128], F32)
        esel_sb = sb("esel_sb", [NE, NE * 128], F32)
        ns_row = sb("ns_row", [1, D], F32)
        ns_bc = sb("ns_bc", [128, D], F32)
        wgsb = sb("wgsb", [128, DC, NE], BF16)
        bg_row = sb("bg_row", [1, NE], F32)
        bg_bc = sb("bg_bc", [128, NE], F32)
        gates_sb = sb("gates_sb", [128, T_ // 128, NE])
        gatesT = sb("gatesT", [NE, T_], F32)
        t_acc = sb("t_acc", [128, DC, T_], BF16)
        b2sb = sb("b2sb", [NE, D], F32)
        col_eps5 = sb("col_eps5", [128, 1])
        sq_scr = sb("sq_scr", [128, D])
        stat = sb("stat", [128, 8 * NTT])

        @block.sync
        def _(sync):
            for ap_hbm, ap_sb, tok in ((ident, idn, "idn"), (identf, idnf, "idnf"),
                                       (onesf, onesf_sb, "onesf"), (esel, esel_sb, "esel"),
                                       (ns_in, ns_row, "ns_row"), (bg_in, bg_row, "bg_row"),
                                       (b2_in, b2sb, "b2sb")):
                p.op("q", sync, lambda e, a=ap_hbm, b=ap_sb: e.dma_start(out=b[:], in_=a[:, :] if a.ndim == 2 else a[:, :, :]),
                     writes=[tok], inc=16, dma=tok)

        @block.gpsimd
        def _(g):
            p.op("g", g, lambda e: e.dma_start(out=wgsb[:], in_=_chunked(wg_in)),
                 writes=["wg"], inc=16, dma="wg")

        @block.vector
        def _(v):
            p.op("v", v, lambda e: e.memset(col_eps5[:], 1e-5), writes=["eps5"])

        with ExitStack() as ph1:
            ps1 = lambda n, s, dt=F32: ph1.enter_context(nc.psum_tensor(pfx + n, s, dt))
            sb1 = lambda n, s, dt=F32: ph1.enter_context(nc.sbuf_tensor(pfx + n, s, dt))
            ptr0 = ps1("ptr0", [128, 128], BF16)
            pgt = ps1("pgt", [16, 128], F32)
            pa0 = ps1("pa0", [128, 512], F32)
            pa1 = ps1("pa1", [128, 512], F32)
            x_tm = sb1("x_tm", [128, 2, NTT, D])
            z_tm = sb1("z_tm", [128, NTT, D])
            xn2_tm = sb1("xn2_tm", [128, NTT, D], BF16)
            xn2_fm = sb1("xn2_fm", [128, DC, CH], BF16)
            lg = sb1("lg", [128, NTT, NE])
            cmp_scr = sb1("cmp_scr", [128, NE * NE])
            cnt = sb1("cnt", [128, NE])
            sel = sb1("sel", [128, NE])
            gsml = sb1("gsml", [128, 6])

            for nb in range(2):
                @block.tensor
                def _(t, nb=nb):
                    p.op("t", t, lambda e: e.matmul(
                        pa0[:, 0:512], onesf_sb[0:1, :], ns_row[0:1, nb * 512:(nb + 1) * 512],
                        start=True, stop=True),
                        reads=["onesf", "ns_row"], writes=[("pa0",)])

                @block.scalar
                def _(s, nb=nb):
                    p.op("s", s, lambda e: e.copy(ns_bc[:, nb * 512:(nb + 1) * 512], pa0[:, 0:512]),
                         reads=[("pa0",)], writes=["ns_bc"])

            @block.tensor
            def _(t):
                p.op("t", t, lambda e: e.matmul(pa1[:, 0:NE], onesf_sb[0:1, :], bg_row[0:1, :],
                                                start=True, stop=True),
                     reads=["onesf", "bg_row"], writes=[("pa1",)])

            @block.scalar
            def _(s):
                p.op("s", s, lambda e: e.copy(bg_bc[:, :], pa1[:, 0:NE]),
                     reads=[("pa1",)], writes=["bg_bc"])

            def load_chunk(sync, ci):
                p.op("q", sync, lambda e, ci=ci: e.dma_start(
                    out=x_tm[:, ci % 2, :, :], in_=_tok_tiles(x_in, ci * CH, NTT)),
                    writes=[("x", ci % 2)], inc=16, dma="x%d" % (ci % 2))

            @block.sync
            def _(sync):
                load_chunk(sync, 0)
                load_chunk(sync, 1)

            ev_par = [0]
            pa_par = [0]
            for ci in range(NCH):
                buf = ci % 2
                xtok = ("x", buf)

                ln_stats_tm(p, block, lambda tt, buf=buf, x_tm=x_tm: x_tm[:, buf, tt, :], [xtok] * NTT,
                            stat, [0, 8], sq_scr, NTT, col_eps5, "eps5")

                @block.vector
                def _(v):
                    for tt in range(NTT):
                        c0 = tt * 8
                        p.op("v", v, lambda e, tt=tt, c0=c0: e.tensor_scalar(
                            z_tm[:, tt, :], x_tm[:, buf, tt, :],
                            stat[:, c0 + 1: c0 + 2], stat[:, c0 + 7: c0 + 8],
                            ALU.subtract, ALU.mult),
                            reads=[xtok, ("st", c0, 1), ("st", c0, 7)],
                            writes=[("z", tt)])

                @block.sync
                def _(sync):
                    p.op("q", sync, lambda e, ci=ci: e.dma_start(
                        out=_tok_tiles(z_hbm, ci * CH, NTT), in_=z_tm[:, :, :]),
                        reads=[("z", tt) for tt in range(NTT)], writes=["z_hbm"],
                        inc=16, dma="zout")

                @block.scalar
                def _(s):
                    for tt in range(NTT):
                        c0 = tt * 8
                        p.op("s", s, lambda e, tt=tt, c0=c0: e.activation(
                            sq_scr[:, :], z_tm[:, tt, :], AF.Square,
                            accum_out=stat[:, c0 + 2: c0 + 3]),
                            reads=[("z", tt)], writes=["sq_scr", ("st", c0, 2)])
                        p.op("s", s, lambda e, c0=c0: e.activation(
                            stat[:, c0 + 6: c0 + 7], stat[:, c0 + 2: c0 + 3], AF.Ln,
                            bias=col_eps5[:, 0:1], scale=1.0 / D),
                            reads=[("st", c0, 2), "eps5"], writes=[("st", c0, 6)])
                        p.op("s", s, lambda e, c0=c0: e.activation(
                            stat[:, c0 + 7: c0 + 8], stat[:, c0 + 6: c0 + 7], AF.Exp, scale=-0.5),
                            reads=[("st", c0, 6)], writes=[("st", c0, 7)])

                @block.vector
                def _(v):
                    for tt in range(NTT):
                        c0 = tt * 8
                        p.op("v", v, lambda e, tt=tt: e.tensor_tensor(
                            z_tm[:, tt, :], z_tm[:, tt, :], ns_bc[:, :], ALU.mult),
                            reads=[("z", tt), "ns_bc"], writes=[("z", tt)])
                        p.op("v", v, lambda e, tt=tt, c0=c0: e.tensor_scalar_mul(
                            xn2_tm[:, tt, :], z_tm[:, tt, :], stat[:, c0 + 7: c0 + 8]),
                            reads=[("z", tt), ("st", c0, 7)], writes=[("xn2", tt)])


                for tt in range(NTT):
                    for c in range(DC):
                        @block.tensor
                        def _(t, tt=tt, c=c):
                            p.op("t", t, lambda e: e.transpose(
                                ptr0[:, :], xn2_tm[:, tt, c * 128:(c + 1) * 128], idn[:]),
                                reads=[("xn2", tt), "idn"], writes=["ptr0"])

                        @block.scalar
                        def _(s, tt=tt, c=c):
                            p.op("s", s, lambda e: e.copy(
                                xn2_fm[:, c, tt * 128:(tt + 1) * 128], ptr0[:, :]),
                                reads=["ptr0"], writes=["xn2_fm"])

                @block.sync
                def _(sync):
                    p.op("q", sync, lambda e, ci=ci: e.dma_start(
                        out=xn2_hbm[:, :, ci * CH:(ci + 1) * CH].rearrange("kc p t -> p kc t"),
                        in_=xn2_fm[:, :, :]),
                        reads=["xn2_fm"], writes=["xn2_hbm"], inc=16, dma="xn2out")

                for tt in range(NTT):
                    tg = ci * NTT + tt
                    par = pa_par[0] % 2
                    pa_par[0] += 1
                    pa_ = [pa0, pa1][par]
                    ptok = ("pa0",) if par == 0 else ("pa1",)

                    @block.tensor
                    def _(t, tt=tt, pa_=pa_, ptok=ptok):
                        def mmg(e):
                            for c in range(DC):
                                ins = e.matmul(pa_[:, 0:NE], xn2_fm[:, c, tt * 128:(tt + 1) * 128],
                                               wgsb[:, c, :], start=(c == 0), stop=(c == DC - 1))
                            return ins
                        p.op("t", t, mmg, reads=["xn2_fm", "wg"], writes=[ptok])

                    @block.vector
                    def _(v, tt=tt, tg=tg, pa_=pa_, ptok=ptok):
                        p.op("v", v, lambda e: e.tensor_tensor(
                            lg[:, tt, :], pa_[:, 0:NE], bg_bc[:, :], ALU.add),
                            reads=[ptok, "bg_bc"], writes=[("lg", tt)])
                        p.op("v", v, lambda e: e.tensor_tensor(
                            cmp_scr[:, :].rearrange("p (a b) -> p a b", b=NE),
                            lg[:, tt, :].unsqueeze(1).broadcast_to([128, NE, NE]),
                            lg[:, tt, :].unsqueeze(2).broadcast_to([128, NE, NE]),
                            ALU.is_gt),
                            reads=[("lg", tt)], writes=["cmp"])
                        p.op("v", v, lambda e: e.tensor_reduce(
                            cnt[:, :], cmp_scr[:, :].rearrange("p (a b) -> p a b", a=NE),
                            AX.X, ALU.add),
                            reads=["cmp"], writes=["cnt"])
                        p.op("v", v, lambda e: e.tensor_single_scalar(
                            sel[:, :], cnt[:, :], 4.0, ALU.is_lt),
                            reads=["cnt"], writes=["sel"])
                        p.op("v", v, lambda e: e.reduce_max(
                            gsml[:, 0:1], lg[:, tt, :], axis=AX.X),
                            reads=[("lg", tt)], writes=["mx"])
                        p.op("v", v, lambda e: e.tensor_scalar(
                            gates_sb[:, tg, :], lg[:, tt, :], gsml[:, 0:1], None,
                            ALU.subtract),
                            reads=[("lg", tt), "mx"], writes=[("gates", tg)])

                    @block.scalar
                    def _(s, tg=tg):
                        p.op("s", s, lambda e: e.activation(
                            gates_sb[:, tg, :], gates_sb[:, tg, :], AF.Exp),
                            reads=[("gates", tg)], writes=[("gates", tg)])

                    @block.vector
                    def _(v, tg=tg):
                        p.op("v", v, lambda e: e.tensor_tensor(
                            gates_sb[:, tg, :], gates_sb[:, tg, :], sel[:, :], ALU.mult),
                            reads=[("gates", tg), "sel"], writes=[("gates", tg)])
                        p.op("v", v, lambda e: e.reduce_sum(
                            gsml[:, 1:2], gates_sb[:, tg, :], axis=AX.X),
                            reads=[("gates", tg)], writes=["dn"])
                        p.op("v", v, lambda e: e.reciprocal(
                            gsml[:, 2:3], gsml[:, 1:2]),
                            reads=["dn"], writes=["rdn"])
                        p.op("v", v, lambda e: e.tensor_scalar_mul(
                            gates_sb[:, tg, :], gates_sb[:, tg, :], gsml[:, 2:3]),
                            reads=[("gates", tg), "rdn"], writes=[("gates", tg)])

                    @block.tensor
                    def _(t, tg=tg):
                        p.op("t", t, lambda e: e.transpose(
                            pgt[:, :], gates_sb[:, tg, :], idnf[:]),
                            reads=[("gates", tg), "idnf"], writes=["pgt"])

                    @block.scalar
                    def _(s, tg=tg):
                        p.op("s", s, lambda e: e.copy(
                            gatesT[:, tg * 128:(tg + 1) * 128], pgt[:, :]),
                            reads=["pgt"], writes=["gatesT"])

                @block.sync
                def _(sync):
                    if ci + 2 < NCH:
                        load_chunk(sync, ci + 2)

            p.barrier(block)

        with ExitStack() as ph2:
            ps2 = lambda n, s, dt=F32: ph2.enter_context(nc.psum_tensor(pfx + n, s, dt))
            sb2 = lambda n, s, dt=F32: ph2.enter_context(nc.sbuf_tensor(pfx + n, s, dt))
            pa0 = ps2("p2a0", [128, TOK], F32)
            pa1 = ps2("p2a1", [128, TOK], F32)
            py0 = ps2("p2y0", [128, TOK], F32)
            py1 = ps2("p2y1", [128, TOK], F32)
            pgb = ps2("pgb", [128, 512], F32)
            w1sb = sb2("w1sb", [128, DC, 2048], BF16)
            w2sb = sb2("w2sb", [128, DC, D], BF16)
            b1r = sb2("b1r", [16, 128], F32)
            b1col = sb2("b1col", [128, 16], F32)
            gate_bc = sb2("gate_bc", [128, T_], F32)
            xslab = sb2("xslab", [128, 2, DC, TOK], BF16)
            a_s = sb2("a_s", [128, 2, TOK], F32)
            b_s = sb2("b_s", [128, 2, TOK], F32)
            g_s = sb2("g_s", [128, 2, TOK], F32)
            v1_s = sb2("v1_s", [128, 2, TOK], F32)
            ss_s = sb2("ss_s", [128, DC, TOK], BF16)

            def load_slab(sync, e, sl):
                p.op("q", sync, lambda ee, e=e, sl=sl: ee.dma_start(
                    out=xslab[:, sl % 2, :, :],
                    in_=xn2_hbm[:, :, sl * TOK:(sl + 1) * TOK].rearrange("kc p t -> p kc t")),
                    reads=["xn2_hbm"], writes=[("xs", sl % 2)], inc=16, dma="xs%d" % (sl % 2))

            for e_i in range(NE):
                @block.gpsimd
                def _(g, e_i=e_i):
                    p.op("g", g, lambda e: e.dma_start(
                        out=w1sb[:], in_=w1_in[e_i].rearrange("(kc kp) n -> kp kc n", kp=128)),
                        writes=["w1"], inc=16, dma="w1")
                    p.op("g", g, lambda e: e.dma_start(
                        out=w2sb[:], in_=w2_in[e_i].rearrange("(kc kp) n -> kp kc n", kp=128)),
                        writes=["w2"], inc=16, dma="w2")

                @block.sync
                def _(sync, e_i=e_i):
                    p.op("q", sync, lambda e: e.dma_start(out=b1r[:], in_=b1_in[e_i, :, :]),
                         writes=["b1r"], inc=16, dma="b1r")
                    if e_i == 0:
                        load_slab(sync, e_i, 0)
                        load_slab(sync, e_i, 1)

                @block.tensor
                def _(t):
                    p.op("t", t, lambda e: e.transpose(pa0[:, 0:16], b1r[:, :], idnf[0:16, 0:16]),
                         reads=["b1r", "idnf"], writes=[("p2a0",)])

                @block.scalar
                def _(s):
                    p.op("s", s, lambda e: e.copy(b1col[:, :], pa0[:, 0:16]),
                         reads=[("p2a0",)], writes=["b1col"])

                for gslab in range(T_ // 512):
                    @block.tensor
                    def _(t, e_i=e_i, gslab=gslab):
                        p.op("t", t, lambda e: e.matmul(
                            pgb[:, :], esel_sb[:, e_i * 128:(e_i + 1) * 128],
                            gatesT[:, gslab * 512:(gslab + 1) * 512], start=True, stop=True),
                            reads=["esel", "gatesT"], writes=["pgb"])

                    @block.scalar
                    def _(s, gslab=gslab):
                        p.op("s", s, lambda e: e.copy(
                            gate_bc[:, gslab * 512:(gslab + 1) * 512], pgb[:, :]),
                            reads=["pgb"], writes=["gate_bc"])

                for sl in range(NSLAB):
                    sbuf_ = sl % 2
                    xs_tok = ("xs", sbuf_)

                    for j in range(DC):
                        @block.tensor
                        def _(t, j=j, sbuf_=sbuf_):
                            def mma(e):
                                for kc in range(DC):
                                    ins = e.matmul(pa0[:, :], w1sb[:, kc, j * 128:(j + 1) * 128],
                                                   xslab[:, sbuf_, kc, :],
                                                   start=(kc == 0), stop=(kc == DC - 1))
                                return ins
                            p.op("t", t, mma, reads=["w1", xs_tok], writes=[("p2a0",)])

                            def mmb(e):
                                for kc in range(DC):
                                    ins = e.matmul(pa1[:, :], w1sb[:, kc, 1024 + j * 128:1024 + (j + 1) * 128],
                                                   xslab[:, sbuf_, kc, :],
                                                   start=(kc == 0), stop=(kc == DC - 1))
                                return ins
                            p.op("t", t, mmb, reads=["w1", xs_tok], writes=[("p2a1",)])

                        jp = j % 2

                        @block.vector
                        def _(v, j=j, jp=jp):
                            p.op("v", v, lambda e: e.tensor_scalar(
                                a_s[:, jp, :], pa0[:, :], b1col[:, j:j + 1], 7.0, ALU.add, ALU.min),
                                reads=[("p2a0",), "b1col"], writes=[("a_s", jp)])
                            p.op("v", v, lambda e: e.tensor_scalar(
                                b_s[:, jp, :], pa1[:, :], b1col[:, 8 + j:9 + j], -7.0, ALU.add, ALU.max),
                                reads=[("p2a1",), "b1col"], writes=[("b_s", jp)])
                            p.op("v", v, lambda e: e.tensor_scalar_min(
                                b_s[:, jp, :], b_s[:, jp, :], 7.0),
                                reads=[("b_s", jp)], writes=[("b_s", jp)])

                        @block.scalar
                        def _(s, j=j, jp=jp):
                            p.op("s", s, lambda e: e.activation(
                                g_s[:, jp, :], a_s[:, jp, :], AF.Sigmoid, scale=1.702),
                                reads=[("a_s", jp)], writes=[("g_s", jp)])

                        @block.vector
                        def _(v, j=j, jp=jp):
                            p.op("v", v, lambda e: e.scalar_tensor_tensor(
                                v1_s[:, jp, :], b_s[:, jp, :], 1.0, g_s[:, jp, :], ALU.add, ALU.mult),
                                reads=[("b_s", jp), ("g_s", jp)], writes=[("v1_s", jp)])

                        @block.gpsimd
                        def _(g, j=j, sl=sl, jp=jp):
                            p.op("g", g, lambda e: e.tensor_tensor(
                                v1_s[:, jp, :], v1_s[:, jp, :], a_s[:, jp, :], ALU.mult),
                                reads=[("v1_s", jp), ("a_s", jp)], writes=[("v1_s", jp)])
                            p.op("g", g, lambda e, sl=sl, j=j: e.tensor_tensor(
                                ss_s[:, j, :], v1_s[:, jp, :], gate_bc[:, sl * TOK:(sl + 1) * TOK],
                                ALU.mult),
                                reads=[("v1_s", jp), "gate_bc"], writes=[("ss", j)])

                    for oc2 in range(DC):
                        ypar = oc2 % 2
                        py_ = [py0, py1][ypar]
                        ytok = ("p2y", ypar)

                        @block.tensor
                        def _(t, oc2=oc2, py_=py_, ytok=ytok):
                            def mmy(e):
                                for j in range(DC):
                                    ins = e.matmul(py_[:, :], w2sb[:, j, oc2 * 128:(oc2 + 1) * 128],
                                                   ss_s[:, j, :], start=(j == 0), stop=(j == DC - 1))
                                return ins
                            p.op("t", t, mmy,
                                 reads=["w2"] + [("ss", j) for j in range(DC)],
                                 writes=[ytok])

                        @block.vector
                        def _(v, e_i=e_i, sl=sl, oc2=oc2, py_=py_, ytok=ytok):
                            if e_i == 0:
                                p.op("v", v, lambda e: e.tensor_copy(
                                    t_acc[:, oc2, sl * TOK:(sl + 1) * TOK], py_[:, :]),
                                    reads=[ytok], writes=[("tacc", sl, oc2)])
                            else:
                                p.op("v", v, lambda e: e.tensor_tensor(
                                    t_acc[:, oc2, sl * TOK:(sl + 1) * TOK],
                                    t_acc[:, oc2, sl * TOK:(sl + 1) * TOK],
                                    py_[:, :], ALU.add),
                                    reads=[ytok, ("tacc", sl, oc2)],
                                    writes=[("tacc", sl, oc2)])

                    @block.sync
                    def _(sync, e_i=e_i, sl=sl):
                        nxt = sl + 2
                        if nxt < NSLAB:
                            load_slab(sync, e_i, nxt)
                        elif e_i + 1 < NE:
                            load_slab(sync, e_i + 1, nxt - NSLAB)

            p.barrier(block)

        with ExitStack() as ph3:
            ps3 = lambda n, s, dt=F32: ph3.enter_context(nc.psum_tensor(pfx + n, s, dt))
            sb3 = lambda n, s, dt=F32: ph3.enter_context(nc.sbuf_tensor(pfx + n, s, dt))
            ptr0 = ps3("p3tr", [128, 128], BF16)
            pb2 = [ps3("pb20", [128, 512], F32), ps3("pb21", [128, 512], F32)]
            x_tm = sb3("x3_tm", [128, 2, NTT, D])
            z3_tm = sb3("z3_tm", [128, 2, NTT, D])
            u_tm = sb3("u_tm", [128, NTT, D])
            out_tm = sb3("out3_tm", [128, NTT, D])

            def load_x3(sync, ci):
                p.op("q", sync, lambda e, ci=ci: e.dma_start(
                    out=x_tm[:, ci % 2, :, :], in_=_tok_tiles(x_in, ci * CH, NTT)),
                    writes=[("x3", ci % 2)], inc=16, dma="x3%d" % (ci % 2))
                p.op("q", sync, lambda e, ci=ci: e.dma_start(
                    out=z3_tm[:, ci % 2, :, :], in_=_tok_tiles(z_hbm, ci * CH, NTT)),
                    reads=["z_hbm"], writes=[("z3", ci % 2)], inc=16, dma="z3%d" % (ci % 2))

            @block.sync
            def _(sync):
                load_x3(sync, 0)
                load_x3(sync, 1)

            for ci in range(NCH):
                buf = ci % 2
                for tt in range(NTT):
                    tg = ci * NTT + tt

                    @block.tensor
                    def _(t, tg=tg):
                        p.op("t", t, lambda e: e.matmul(
                            pb2[0][:, 0:512], gatesT[:, tg * 128:(tg + 1) * 128],
                            b2sb[:, 0:512], start=True, stop=True),
                            reads=["gatesT", "b2sb"], writes=[("pb2", 0)])
                        p.op("t", t, lambda e: e.matmul(
                            pb2[1][:, 0:512], gatesT[:, tg * 128:(tg + 1) * 128],
                            b2sb[:, 512:1024], start=True, stop=True),
                            reads=["gatesT", "b2sb"], writes=[("pb2", 1)])

                    for c in range(DC):
                        @block.tensor
                        def _(t, tg=tg, c=c):
                            p.op("t", t, lambda e: e.transpose(
                                ptr0[:, :], t_acc[:, c, tg * 128:(tg + 1) * 128], idn[:]),
                                reads=[("tacc", tg // (TOK // 128), c), "idn"],
                                writes=["p3tr"])

                        @block.scalar
                        def _(s, tt=tt, c=c):
                            p.op("s", s, lambda e: e.copy(
                                u_tm[:, tt, c * 128:(c + 1) * 128], ptr0[:, :]),
                                reads=["p3tr"], writes=[("u", tt)])

                    @block.vector
                    def _(v, tt=tt, buf=buf):
                        p.op("v", v, lambda e: e.tensor_tensor(
                            u_tm[:, tt, :], u_tm[:, tt, :], z3_tm[:, buf, tt, :], ALU.add),
                            reads=[("u", tt), ("z3", buf)], writes=[("u", tt)])
                        p.op("v", v, lambda e: e.tensor_tensor(
                            u_tm[:, tt, 0:512], u_tm[:, tt, 0:512], pb2[0][:, 0:512], ALU.add),
                            reads=[("u", tt), ("pb2", 0)], writes=[("u", tt)])
                        p.op("v", v, lambda e: e.tensor_tensor(
                            u_tm[:, tt, 512:1024], u_tm[:, tt, 512:1024], pb2[1][:, 0:512], ALU.add),
                            reads=[("u", tt), ("pb2", 1)], writes=[("u", tt)])

                ln_stats_tm(p, block, lambda tt: u_tm[:, tt, :],
                            [("u", tt) for tt in range(NTT)],
                            stat, [0, 8], sq_scr, NTT, col_eps5, "eps5")

                @block.vector
                def _(v, ci=ci, buf=buf):
                    for tt in range(NTT):
                        c0 = tt * 8
                        p.op("v", v, lambda e, tt=tt, c0=c0: e.tensor_scalar(
                            out_tm[:, tt, :], u_tm[:, tt, :],
                            stat[:, c0 + 1: c0 + 2], stat[:, c0 + 7: c0 + 8],
                            ALU.subtract, ALU.mult),
                            reads=[("u", tt), ("st", c0, 1), ("st", c0, 7)],
                            writes=[("o3", tt)])
                        p.op("v", v, lambda e, tt=tt: e.tensor_tensor(
                            out_tm[:, tt, :], out_tm[:, tt, :], x_tm[:, buf, tt, :], ALU.add),
                            reads=[("o3", tt), ("x3", buf)], writes=[("o3", tt)])

                @block.sync
                def _(sync, ci=ci):
                    p.op("q", sync, lambda e, ci=ci: e.dma_start(
                        out=_tok_tiles(x_out, ci * CH, NTT), in_=out_tm[:, :, :]),
                        reads=[("o3", tt) for tt in range(NTT)],
                        writes=["x_out_hbm"], inc=16, dma="xo3")
                    if ci + 2 < NCH:
                        load_x3(sync, ci + 2)


def build_stage_d(T_=T, SEQ_=SEQ):
    return _std_wrap(decl_stage_d, emit_stage_d, T_, SEQ_)


def _moe_prep(w1, b1, w2, b2):
    w1p = np.concatenate([w1[:, :, 0::2], w1[:, :, 1::2]], axis=2)
    b1p = np.concatenate([b1[:, 0::2], b1[:, 1::2]], axis=1)
    b1t = np.ascontiguousarray(b1p.reshape(16, 16, 128))
    return (np.ascontiguousarray(w1p), b1t,
            np.ascontiguousarray(w2), np.ascontiguousarray(b2))


def run_stage_d(x_full, norm_scale, wg, bg, w1, b1, w2, b2):
    nc = build_stage_d()
    shards = _shard_x(x_full)
    w1p, b1t, w2c, b2c = _moe_prep(w1, b1, w2, b2)
    esel = np.zeros((16, 16 * 128), dtype=np.float32)
    for e in range(16):
        esel[e, e * 128:(e + 1) * 128] = 1.0
    consts = _consts()
    in_maps = [{"x_in": shards[c], "ns_in": norm_scale.reshape(1, D), "wg_in": wg,
                "bg_in": bg.reshape(1, 16), "w1_in": w1p, "b1_in": b1t,
                "w2_in": w2c, "b2_in": b2c, "onesf": np.ones((1, 128), np.float32),
                "esel": esel, **consts}
               for c in range(NCORES)]
    res = run_bass_kernel_spmd(nc, in_maps, core_ids=list(range(NCORES)))
    return np.stack([res.results[c]["x_out"].reshape(SPC, SEQ, D) for c in range(NCORES)]
                    ).reshape(NCORES * SPC, SEQ, D)


def _consts():
    return {
        "ident": np.eye(128, dtype=ml_dtypes.bfloat16),
        "identf": np.eye(128, dtype=np.float32),
        "arange": np.arange(128, dtype=np.float32).reshape(128, 1),
    }


def _shard_x(x):
    x = np.ascontiguousarray(x, dtype=np.float32)
    return [x[c * SPC:(c + 1) * SPC].reshape(T, D) for c in range(NCORES)]


def run_stage_a(x_full, Wq, Wk, Wv, Wc):
    nc = build_stage_a()
    shards = _shard_x(x_full)
    consts = _consts()
    in_maps = [{"x_in": shards[c], "w_q": Wq, "w_k": Wk, "w_v": Wv, "w_c": Wc, **consts}
               for c in range(NCORES)]
    res = run_bass_kernel_spmd(nc, in_maps, core_ids=list(range(NCORES)))
    return np.stack([res.results[c]["x_out"].reshape(SPC, SEQ, D) for c in range(NCORES)]
                    ).reshape(NCORES * SPC, SEQ, D)


def build_fused(T_=T, SEQ_=SEQ):
    nc = bass.Bass("TRN2", target_bir_lowering=False, debug=False)
    x1 = nc.dram_tensor("x1_hbm", [T_, D], F32, kind="Internal").ap()
    x2 = nc.dram_tensor("x2_hbm", [T_, D], F32, kind="Internal").ap()
    x3 = nc.dram_tensor("x3_hbm", [T_, D], F32, kind="Internal").ap()
    A = decl_stage_a(nc, "a_", T_, x_out=x1)
    B = decl_stage_b(nc, "b_", T_, SEQ_, x_in=x1, x_out=x2)
    C = decl_stage_c(nc, "c_", T_, x_in=x2, x_out=x3)
    Dd = decl_stage_d(nc, "d_", T_, x_in=x3)

    with ExitStack() as st_:
        p = Prog(nc, st_)
        block = FakeBlock()
        with ExitStack() as sA:
            emit_stage_a(nc, p, block, sA, "a_", A, T_, SEQ_)
            p.barrier(block)
        with ExitStack() as sB:
            emit_stage_b(nc, p, block, sB, "b_", B, T_, SEQ_)
            p.barrier(block)
        with ExitStack() as sC:
            emit_stage_c(nc, p, block, sC, "c_", C, T_, SEQ_)
            p.barrier(block)
        with ExitStack() as sD:
            emit_stage_d(nc, p, block, sD, "d_", Dd, T_, SEQ_)
        p.final_wait()
        real_block = st_.enter_context(nc.Block())
        p.flush(real_block)
    return nc


def _fused_in_maps(inputs):
    shards = _shard_x(inputs["L0_attn_X"])
    cst = _consts()
    wr = np.ascontiguousarray(inputs["L1_attn_wqkv"].reshape(D, 16, 192))
    bwts = _stage_b_weights(inputs["L0_ffn_Wqkv"], inputs["L0_ffn_q_scale"],
                            inputs["L0_ffn_k_scale"], inputs["L0_ffn_Wo"],
                            inputs["L0_ffn_ln_g"], inputs["L0_ffn_ln_b"],
                            inputs["L0_ffn_W1"], inputs["L0_ffn_b1"],
                            inputs["L0_ffn_W2"], inputs["L0_ffn_b2"])
    bcst = _stage_b_consts(SEQ)
    w1p, b1t, w2c, b2c = _moe_prep(inputs["L1_ffn_w1"], inputs["L1_ffn_b1"],
                                   inputs["L1_ffn_w2"], inputs["L1_ffn_b2"])
    esel = np.zeros((16, 16 * 128), dtype=np.float32)
    for e in range(16):
        esel[e, e * 128:(e + 1) * 128] = 1.0
    shared = {}
    shared.update({"a_w_q": inputs["L0_attn_Wq"], "a_w_k": inputs["L0_attn_Wk"],
                   "a_w_v": inputs["L0_attn_Wv"], "a_w_c": inputs["L0_attn_Wc"],
                   "a_ident": cst["ident"]})
    shared.update({"b_" + k: v for k, v in bwts.items()})
    shared.update({"b_tab_cs": bcst["tab_cs"], "b_tab_ss": bcst["tab_ss"],
                   "b_tab_ci": bcst["tab_ci"], "b_tab_si": bcst["tab_si"],
                   "b_maskA": bcst["maskA"], "b_maskC": bcst["maskC"],
                   "b_ident": cst["ident"], "b_identf": cst["identf"],
                   "b_onesf": bcst["onesf"], "b_onescb": bcst["onescb"]})
    shared.update({"c_w_q": np.ascontiguousarray(wr[:, :, 0:64].reshape(D, D)),
                   "c_w_k": np.ascontiguousarray(wr[:, :, 64:128].reshape(D, D)),
                   "c_w_v": np.ascontiguousarray(wr[:, :, 128:192].reshape(D, D)),
                   "c_ident": cst["ident"], "c_identf": cst["identf"],
                   "c_arange": cst["arange"]})
    shared.update({"d_ns_in": inputs["L1_ffn_norm_scale"].reshape(1, D),
                   "d_wg_in": inputs["L1_ffn_wg"],
                   "d_bg_in": inputs["L1_ffn_bg"].reshape(1, 16),
                   "d_w1_in": w1p, "d_b1_in": b1t, "d_w2_in": w2c, "d_b2_in": b2c,
                   "d_ident": cst["ident"], "d_identf": cst["identf"],
                   "d_onesf": bcst["onesf"], "d_esel": esel})
    return [{"a_x_in": shards[c], **shared} for c in range(NCORES)]


def kernel(**inputs):
    nc = build_fused()
    in_maps = _fused_in_maps(inputs)
    res = run_bass_kernel_spmd(nc, in_maps, core_ids=list(range(NCORES)))
    return np.stack([res.results[c]["d_x_out"].reshape(SPC, SEQ, D) for c in range(NCORES)]
                    ).reshape(NCORES * SPC, SEQ, D)


def kernel_multilaunch(**inputs):
    x = run_stage_a(inputs["L0_attn_X"], inputs["L0_attn_Wq"], inputs["L0_attn_Wk"],
                    inputs["L0_attn_Wv"], inputs["L0_attn_Wc"])
    x = run_stage_b(x, inputs["L0_ffn_Wqkv"], inputs["L0_ffn_q_scale"],
                    inputs["L0_ffn_k_scale"], inputs["L0_ffn_Wo"], inputs["L0_ffn_ln_g"],
                    inputs["L0_ffn_ln_b"], inputs["L0_ffn_W1"], inputs["L0_ffn_b1"],
                    inputs["L0_ffn_W2"], inputs["L0_ffn_b2"])
    x = run_stage_c(x, inputs["L1_attn_wqkv"])
    x = run_stage_d(x, inputs["L1_ffn_norm_scale"], inputs["L1_ffn_wg"], inputs["L1_ffn_bg"],
                    inputs["L1_ffn_w1"], inputs["L1_ffn_b1"], inputs["L1_ffn_w2"],
                    inputs["L1_ffn_b2"])
    return x


# version 08
# speedup vs baseline: 2.0092x; 1.0123x over previous
import numpy as np
import ml_dtypes
from contextlib import ExitStack

import concourse.bass as bass
import concourse.mybir as mybir
from concourse.bass_utils import run_bass_kernel_spmd

F32 = mybir.dt.float32
BF16 = mybir.dt.bfloat16
AF = mybir.ActivationFunctionType
ALU = mybir.AluOpType
AX = mybir.AxisListType

NCORES = 8
SEQ = 2048
SPC = 2
T = SEQ * SPC
D = 1024
DC = D // 128

ENGS = ("t", "v", "s", "g", "q")


class Sem:
    def __init__(self, handle):
        self.h = handle
        self.n = 0

    def inc(self, instr, k=1):
        instr.then_inc(self.h, k)
        self.n += k
        return self.n


def _freeze(fn):
    import types
    if getattr(fn, "__closure__", None) is None:
        return fn
    cells = []
    for c in fn.__closure__:
        try:
            cells.append(types.CellType(c.cell_contents))
        except ValueError:
            cells.append(c)
    out = types.FunctionType(fn.__code__, fn.__globals__, fn.__name__,
                             fn.__defaults__, tuple(cells))
    out.__kwdefaults__ = fn.__kwdefaults__
    return out


class FakeBlock:

    def tensor(self, f):
        f(None)

    def vector(self, f):
        f(None)

    def scalar(self, f):
        f(None)

    def gpsimd(self, f):
        f(None)

    def sync(self, f):
        f(None)


class Prog:

    def __init__(self, nc, stack):
        self.nc = nc
        self.stack = stack
        self.sems = {e: Sem(stack.enter_context(nc.semaphore("sem_" + e))) for e in ENGS}
        self.dma_sems = {}
        self.writer = {}
        self.readers = {}
        self.hwm = {e: {} for e in ENGS}
        self.thunks = {e: [] for e in ENGS}

    def dma_sem(self, key):
        if key not in self.dma_sems:
            self.dma_sems[key] = Sem(self.stack.enter_context(
                self.nc.semaphore("dma_" + str(key))))
        return self.dma_sems[key]

    def op(self, ek, eng, emit_fn, reads=(), writes=(), inc=1, dma=None):
        need = {}
        for tok in reads:
            w = self.writer.get(tok)
            if w is not None:
                sem, m, _ = w
                need[sem] = max(need.get(sem, 0), m)
        for tok in writes:
            w = self.writer.get(tok)
            if w is not None:
                sem, m, wek = w
                if not (ek == "t" and wek == "t"):
                    need[sem] = max(need.get(sem, 0), m)
            for sem, m in self.readers.get(tok, {}).items():
                need[sem] = max(need.get(sem, 0), m)
        waits = []
        for sem, m in need.items():
            if ek == "t" and sem is self.sems["t"]:
                continue
            if m > self.hwm[ek].get(sem, 0):
                waits.append((sem.h, m))
                self.hwm[ek][sem] = m
        my_sem = self.dma_sem(dma) if dma is not None else self.sems[ek]
        my_sem.n += inc
        mil = my_sem.n
        self.thunks[ek].append((waits, _freeze(emit_fn), my_sem.h, inc))
        for tok in writes:
            self.writer[tok] = (my_sem, mil, ek)
            self.readers[tok] = {}
        for tok in reads:
            d = self.readers.setdefault(tok, {})
            d[my_sem] = max(d.get(my_sem, 0), mil)
        return mil

    def barrier(self, block=None):
        snap = {sem: sem.n for sem in
                list(self.sems.values()) + list(self.dma_sems.values())}
        for ek in ENGS:
            waits = []
            for sem, n in snap.items():
                if sem is self.sems[ek]:
                    continue
                if n > self.hwm[ek].get(sem, 0):
                    waits.append((sem.h, n))
                    self.hwm[ek][sem] = n
            if waits:
                self.thunks[ek].append((waits, None, None, 0))

    def final_wait(self, eng=None, ek="q"):
        waits = []
        for sem in list(self.sems.values()) + list(self.dma_sems.values()):
            if sem is self.sems[ek]:
                continue
            if sem.n > self.hwm[ek].get(sem, 0):
                waits.append((sem.h, sem.n))
        if self.sems[ek].n > 0:
            waits.append((self.sems[ek].h, self.sems[ek].n))
        self.thunks[ek].append((waits, None, None, 0))

    def flush(self, real_block):
        meths = {"t": real_block.tensor, "v": real_block.vector,
                 "s": real_block.scalar, "g": real_block.gpsimd,
                 "q": real_block.sync}
        for ek in ENGS:
            def body(eng, ek=ek):
                for waits, emit_fn, sem_h, inc in self.thunks[ek]:
                    for h, m in waits:
                        eng.wait_ge(h, m)
                    if emit_fn is not None:
                        ins = emit_fn(eng)
                        ins.then_inc(sem_h, inc)
            meths[ek](body)
            self.thunks[ek] = []


def _chunked(w_hbm):
    return w_hbm.rearrange("(kc kp) n -> kp kc n", kp=128)


def _tok_tiles(dram_2d, t0, n_tt):
    return dram_2d[t0:t0 + 128 * n_tt, :].rearrange("(tt p) d -> p tt d", p=128)


def ln_stats_tm(p, block, x_ap_fn, x_toks, stat, cols, sq_scr, ntt, eps_col,
                eps_tok, width=D, stok=("st",), sqtok="sq_scr"):
    if not isinstance(stok, tuple):
        stok = (stok,)
    ST = lambda c0, k: stok + (c0, k)
    @block.vector
    def _(v):
        for tt in range(ntt):
            c0 = cols[tt]
            p.op("v", v, lambda e, tt=tt, c0=c0: e.reduce_sum(
                stat[:, c0 + 0: c0 + 1], x_ap_fn(tt), axis=AX.X),
                reads=[x_toks[tt]], writes=[ST(c0, 0)])
            p.op("v", v, lambda e, c0=c0: e.tensor_scalar_mul(
                stat[:, c0 + 1: c0 + 2], stat[:, c0 + 0: c0 + 1], 1.0 / width),
                reads=[ST(c0, 0)], writes=[ST(c0, 1)])

    @block.scalar
    def _(s):
        for tt in range(ntt):
            c0 = cols[tt]
            p.op("s", s, lambda e, tt=tt, c0=c0: e.activation(
                sq_scr[:, 0:width], x_ap_fn(tt), AF.Square,
                accum_out=stat[:, c0 + 2: c0 + 3]),
                reads=[x_toks[tt]], writes=[sqtok, ST(c0, 2)])

    @block.vector
    def _(v):
        for tt in range(ntt):
            c0 = cols[tt]
            p.op("v", v, lambda e, c0=c0: e.tensor_tensor(
                stat[:, c0 + 3: c0 + 4], stat[:, c0 + 1: c0 + 2],
                stat[:, c0 + 1: c0 + 2], ALU.mult),
                reads=[ST(c0, 1)], writes=[ST(c0, 3)])
            p.op("v", v, lambda e, c0=c0: e.tensor_scalar_mul(
                stat[:, c0 + 4: c0 + 5], stat[:, c0 + 2: c0 + 3], 1.0 / width),
                reads=[ST(c0, 2)], writes=[ST(c0, 4)])
            p.op("v", v, lambda e, c0=c0: e.tensor_tensor(
                stat[:, c0 + 5: c0 + 6], stat[:, c0 + 4: c0 + 5],
                stat[:, c0 + 3: c0 + 4], ALU.subtract),
                reads=[ST(c0, 4), ST(c0, 3)], writes=[ST(c0, 5)])

    @block.scalar
    def _(s):
        for tt in range(ntt):
            c0 = cols[tt]
            p.op("s", s, lambda e, c0=c0: e.activation(
                stat[:, c0 + 6: c0 + 7], stat[:, c0 + 5: c0 + 6], AF.Ln,
                bias=eps_col[:, 0:1]),
                reads=[ST(c0, 5), eps_tok], writes=[ST(c0, 6)])
            p.op("s", s, lambda e, c0=c0: e.activation(
                stat[:, c0 + 7: c0 + 8], stat[:, c0 + 6: c0 + 7], AF.Exp, scale=-0.5),
                reads=[ST(c0, 6)], writes=[ST(c0, 7)])


def _std_wrap(decl_fn, emit_fn, T_, SEQ_, **kw):
    nc = bass.Bass("TRN2", target_bir_lowering=False, debug=False)
    aps = decl_fn(nc, "", T_, SEQ_)
    with ExitStack() as st_:
        p = Prog(nc, st_)
        block = FakeBlock()
        with ExitStack() as sst:
            emit_fn(nc, p, block, sst, "", aps, T_, SEQ_, **kw)
        p.final_wait()
        real_block = st_.enter_context(nc.Block())
        p.flush(real_block)
    return nc


def decl_stage_a(nc, pfx, T_, SEQ_=None, x_in=None, x_out=None):
    d = lambda n, s, dt=F32, kind="ExternalInput": nc.dram_tensor(pfx + n, s, dt, kind=kind).ap()
    return {
        "x_in": x_in if x_in is not None else d("x_in", [T_, D]), "w_q": d("w_q", [D, D]), "w_k": d("w_k", [D, D]),
        "w_v": d("w_v", [D, D]), "w_c": d("w_c", [D, D]),
        "ident": d("ident", [128, 128], BF16),
        "x_out": x_out if x_out is not None else d("x_out", [T_, D], F32, "ExternalOutput"),
    }


def emit_stage_a(nc, p, block, st_, pfx, A, T_=T, SEQ_=SEQ):
    x_in, w_q, w_k, w_v, w_c, ident, x_out = (A["x_in"], A["w_q"], A["w_k"], A["w_v"],
                                              A["w_c"], A["ident"], A["x_out"])
    CH = 256
    NCH = T_ // CH
    CPS = SEQ_ // CH
    NTT = CH // 128

    if True:
        sb = lambda n, s, dt=F32: st_.enter_context(nc.sbuf_tensor(pfx + n, s, dt))
        ps = lambda n, s, dt=F32: st_.enter_context(nc.psum_tensor(pfx + n, s, dt))
        wq = sb("wq", [128, DC, D], BF16)
        wk = sb("wk", [128, DC, D], BF16)
        wv = sb("wv", [128, DC, D], BF16)
        wc = sb("wc", [128, DC, D], BF16)
        idn = sb("idn", [128, 128], BF16)
        x_tm = sb("x_tm", [128, 2, NTT, D])
        out_tm = sb("out_tm", [128, 2, NTT, D])
        xn_tm = sb("xn_tm", [128, 2, NTT, D], BF16)
        xn_fm = sb("xn_fm", [128, 2, DC, CH], BF16)
        v_fm = sb("v_fm", [128, 2, DC, CH])
        k_fm = sb("k_fm", [128, 2, DC, CH])
        kv_fm = sb("kv_fm", [128, DC, CH])
        kc_fm = sb("kc_fm", [128, DC, CH])
        qa = sb("qa", [128, 2, DC, CH])
        qb = sb("qb", [128, DC, CH])
        o_fm = sb("o_fm", [128, 2, DC, CH], BF16)
        sq_scr = sb("sq_scr", [128, 2, D])
        zero_col = sb("zero_col", [128, 1])
        col_eps5 = sb("col_eps5", [128, 1])
        col_eps6 = sb("col_eps6", [128, 1])
        col_one = sb("col_one", [128, 1])
        carry_k = sb("carry_k", [128, DC])
        carry_kv = sb("carry_kv", [128, DC])
        stat = sb("stat", [128, 2, 8 * NTT])
        ysum = sb("ysum", [128, 2, NTT, 2])
        ysq = sb("ysq", [128, 2, NTT, 2])
        ptr = [ps("ptr0", [128, 4, 128], BF16), ps("ptr1", [128, 4, 128], BF16)]
        pa = [ps("pa0", [128, CH]), ps("pa1", [128, CH])]
        py = [ps("py0", [128, 512]), ps("py1", [128, 512])]

        @block.gpsimd
        def _(g):
            for w_hbm, w_sb, tok in ((w_q, wq, "wq"), (w_k, wk, "wk"),
                                     (w_v, wv, "wv"), (w_c, wc, "wc")):
                p.op("g", g, lambda e, a=w_hbm, b=w_sb: e.dma_start(out=b[:], in_=_chunked(a)),
                     writes=[tok], inc=16, dma=tok)

        def load_chunk(sync, ci):
            p.op("q", sync, lambda e, ci=ci: e.dma_start(
                out=x_tm[:, ci % 2, :, :], in_=_tok_tiles(x_in, ci * CH, NTT)),
                writes=[("x", ci % 2)], inc=16, dma="x%d" % (ci % 2))

        @block.sync
        def _(sync):
            p.op("q", sync, lambda e: e.dma_start(out=idn[:], in_=ident[:, :]),
                 writes=["idn"], inc=16, dma="idn")
            load_chunk(sync, 0)
            load_chunk(sync, 1)

        @block.vector
        def _(v):
            p.op("v", v, lambda e: e.memset(zero_col[:], 0.0), writes=["zero"])
            p.op("v", v, lambda e: e.memset(col_eps5[:], 1e-5), writes=["eps5"])
            p.op("v", v, lambda e: e.memset(col_eps6[:], 1e-6), writes=["eps6"])
            p.op("v", v, lambda e: e.memset(col_one[:], 1.0), writes=["one"])

        ev_par = [0]
        pa_i = [0]
        py_i = [0]

        def tr_pack4(srcs, dst_view, dtoks, rtoks):
            par = ev_par[0] % 2
            ev_par[0] += 1
            n = len(srcs)

            @block.tensor
            def _(t, srcs=srcs, par=par):
                def mmg(e):
                    for i2, src in enumerate(srcs):
                        ins = e.transpose(ptr[par][:, i2, :], src, idn[:])
                    return ins
                p.op("t", t, mmg, reads=rtoks + ["idn"], writes=[("ptr", par)])

            @block.scalar
            def _(s, dst_view=dst_view, par=par, n=n):
                p.op("s", s, lambda e: e.copy(dst_view, ptr[par][:, 0:n, :]),
                     reads=[("ptr", par)], writes=dtoks)

        def front(ci):
            cb = ci % 2
            xtok = ("x", cb)

            ln_stats_tm(p, block, lambda tt, cb=cb: x_tm[:, cb, tt, :], [xtok] * NTT,
                        stat[:, cb, :], [0, 8], sq_scr[:, cb, :], NTT, col_eps5, "eps5",
                        stok=("st", cb), sqtok=("sq", cb))

            @block.vector
            def _(v, cb=cb):
                for tt in range(NTT):
                    c0 = tt * 8
                    p.op("v", v, lambda e, tt=tt, c0=c0: e.tensor_scalar(
                        xn_tm[:, cb, tt, :], x_tm[:, cb, tt, :],
                        stat[:, cb, c0 + 1: c0 + 2], stat[:, cb, c0 + 7: c0 + 8],
                        ALU.subtract, ALU.mult),
                        reads=[xtok, ("st", cb, c0, 1), ("st", cb, c0, 7)],
                        writes=[("xn_tm", cb, tt)])

            for tt in range(NTT):
                for cg in range(2):
                    srcs = [xn_tm[:, cb, tt, (cg * 4 + i2) * 128:(cg * 4 + i2 + 1) * 128]
                            for i2 in range(4)]
                    tr_pack4(srcs,
                             xn_fm[:, cb, cg * 4:(cg + 1) * 4, tt * 128:(tt + 1) * 128],
                             [("xn_fm", cb)], [("xn_tm", cb, tt)])

            for pj, wsb, wtok, dst, dtok in (("v", wv, "wv", v_fm, "v_fm"),
                                             ("k", wk, "wk", k_fm, "k_fm"),
                                             ("q", wq, "wq", qa, "qa")):
                for oc in range(DC):
                    par = pa_i[0] % 2
                    pa_i[0] += 1

                    @block.tensor
                    def _(t, cb=cb, wsb=wsb, wtok=wtok, oc=oc, par=par):
                        def mmg(e):
                            for kc in range(DC):
                                ins = e.matmul(pa[par][:, :], wsb[:, kc, oc * 128:(oc + 1) * 128],
                                               xn_fm[:, cb, kc, :], start=(kc == 0), stop=(kc == DC - 1))
                            return ins
                        p.op("t", t, mmg, reads=[("xn_fm", cb), wtok], writes=[("pa", par)])

                    @block.scalar
                    def _(s, cb=cb, pj=pj, dst=dst, dtok=dtok, oc=oc, par=par):
                        if pj == "v":
                            fn = lambda e: e.copy(dst[:, cb, oc, :], pa[par][:, :])
                        elif pj == "k":
                            fn = lambda e: e.activation(dst[:, cb, oc, :], pa[par][:, :], AF.Exp)
                        else:
                            fn = lambda e: e.activation(dst[:, cb, oc, :], pa[par][:, :], AF.Exp, scale=-1.0)
                        p.op("s", s, fn, reads=[("pa", par)], writes=[(dtok, cb, oc)])

        def back(ci):
            cb = ci % 2
            first_in_seq = (ci % CPS == 0)
            xtok = ("x", cb)

            @block.scalar
            def _(s, cb=cb):
                p.op("s", s, lambda e: e.activation(qb[:, :, :], qa[:, cb, :, :], AF.Ln,
                                                    bias=col_one[:, 0:1]),
                     reads=[("qa", cb, oc) for oc in range(DC)] + ["one"], writes=["qb"])
                p.op("s", s, lambda e: e.activation(qa[:, cb, :, :], qb[:, :, :], AF.Exp, scale=-1.0),
                     reads=["qb"], writes=[("sig", cb)])

            @block.vector
            def _(v, cb=cb, first_in_seq=first_in_seq):
                p.op("v", v, lambda e: e.tensor_tensor(kv_fm[:, :, :], k_fm[:, cb, :, :],
                                                       v_fm[:, cb, :, :], ALU.mult),
                     reads=[("k_fm", cb, oc) for oc in range(DC)] +
                           [("v_fm", cb, oc) for oc in range(DC)],
                     writes=["kv_fm"])
                for c in range(DC):
                    if first_in_seq:
                        init_k, init_kv = 0.0, 0.0
                        rk, rkv = [], []
                    else:
                        init_k = carry_k[:, c:c + 1]
                        init_kv = carry_kv[:, c:c + 1]
                        rk, rkv = [("ck", c)], [("ckv", c)]
                    p.op("v", v, lambda e, c=c, init_k=init_k: e.tensor_tensor_scan(
                        kc_fm[:, c, :], k_fm[:, cb, c, :],
                        zero_col[:, 0:1].broadcast_to([128, CH]), init_k, ALU.add, ALU.add),
                        reads=[("k_fm", cb, c), "zero"] + rk, writes=[("kc_fm", c)])
                    p.op("v", v, lambda e, c=c: e.tensor_copy(
                        carry_k[:, c:c + 1], kc_fm[:, c, CH - 1:CH]),
                        reads=[("kc_fm", c)], writes=[("ck", c)])
                    p.op("v", v, lambda e, c=c, init_kv=init_kv: e.tensor_tensor_scan(
                        k_fm[:, cb, c, :], kv_fm[:, c, :],
                        zero_col[:, 0:1].broadcast_to([128, CH]), init_kv, ALU.add, ALU.add),
                        reads=["kv_fm", "zero"] + rkv, writes=[("kvc", c), ("k_fm", cb, c)])
                    p.op("v", v, lambda e, c=c: e.tensor_copy(
                        carry_kv[:, c:c + 1], k_fm[:, cb, c, CH - 1:CH]),
                        reads=[("kvc", c)], writes=[("ckv", c)])

            @block.scalar
            def _(s, cb=cb):
                p.op("s", s, lambda e: e.activation(qb[:, :, :], kc_fm[:, :, :], AF.Ln,
                                                    bias=col_eps6[:, 0:1]),
                     reads=[("kc_fm", c) for c in range(DC)] + ["eps6", ("sig", cb)],
                     writes=["qb"])
                p.op("s", s, lambda e: e.activation(qb[:, :, :], qb[:, :, :], AF.Exp, scale=-1.0),
                     reads=["qb"], writes=["qb"])

            @block.vector
            def _(v, cb=cb):
                p.op("v", v, lambda e: e.tensor_tensor(v_fm[:, cb, :, :], k_fm[:, cb, :, :],
                                                       qb[:, :, :], ALU.mult),
                     reads=[("kvc", c) for c in range(DC)] + ["qb"],
                     writes=[("v_fm", cb, oc) for oc in range(DC)] + [("S", cb)])
                p.op("v", v, lambda e: e.tensor_tensor(o_fm[:, cb, :, :], v_fm[:, cb, :, :],
                                                       qa[:, cb, :, :], ALU.mult),
                     reads=[("S", cb), ("sig", cb)], writes=[("o_fm", cb)])

            for tt in range(NTT):
                for nb in range(2):
                    par = py_i[0] % 2
                    py_i[0] += 1

                    @block.tensor
                    def _(t, cb=cb, tt=tt, nb=nb, par=par):
                        def mmg(e):
                            for c in range(DC):
                                ins = e.matmul(py[par][:, :],
                                               o_fm[:, cb, c, tt * 128:(tt + 1) * 128],
                                               wc[:, c, nb * 512:(nb + 1) * 512],
                                               start=(c == 0), stop=(c == DC - 1))
                            return ins
                        p.op("t", t, mmg, reads=[("o_fm", cb), "wc"], writes=[("py", par)])

                    @block.scalar
                    def _(s, cb=cb, tt=tt, nb=nb, par=par):
                        p.op("s", s, lambda e: e.copy(
                            out_tm[:, cb, tt, nb * 512:(nb + 1) * 512], py[par][:, :]),
                            reads=[("py", par)], writes=[("out", cb, tt, nb)])
                        p.op("s", s, lambda e: e.activation(
                            sq_scr[:, cb, 0:512], py[par][:, :], AF.Square,
                            accum_out=ysq[:, cb, tt, nb:nb + 1]),
                            reads=[("py", par)], writes=[("sq", cb), ("ysq", cb, tt, nb)])

            @block.vector
            def _(v, cb=cb):
                for tt in range(NTT):
                    c0 = tt * 8
                    p.op("v", v, lambda e, tt=tt: e.reduce_sum(
                        ysum[:, cb, tt, 0:1], out_tm[:, cb, tt, 0:512], axis=AX.X),
                        reads=[("out", cb, tt, 0)], writes=[("ysum", cb, tt, 0)])
                    p.op("v", v, lambda e, tt=tt: e.reduce_sum(
                        ysum[:, cb, tt, 1:2], out_tm[:, cb, tt, 512:1024], axis=AX.X),
                        reads=[("out", cb, tt, 1)], writes=[("ysum", cb, tt, 1)])
                    p.op("v", v, lambda e, tt=tt, c0=c0: e.tensor_tensor(
                        stat[:, cb, c0 + 0: c0 + 1], ysum[:, cb, tt, 0:1], ysum[:, cb, tt, 1:2], ALU.add),
                        reads=[("ysum", cb, tt, 0), ("ysum", cb, tt, 1)], writes=[("st", cb, c0, 0)])
                    p.op("v", v, lambda e, tt=tt, c0=c0: e.tensor_tensor(
                        stat[:, cb, c0 + 2: c0 + 3], ysq[:, cb, tt, 0:1], ysq[:, cb, tt, 1:2], ALU.add),
                        reads=[("ysq", cb, tt, 0), ("ysq", cb, tt, 1)], writes=[("st", cb, c0, 2)])
                    p.op("v", v, lambda e, c0=c0: e.tensor_scalar_mul(
                        stat[:, cb, c0 + 1: c0 + 2], stat[:, cb, c0 + 0: c0 + 1], 1.0 / D),
                        reads=[("st", cb, c0, 0)], writes=[("st", cb, c0, 1)])
                    p.op("v", v, lambda e, c0=c0: e.tensor_scalar_mul(
                        stat[:, cb, c0 + 4: c0 + 5], stat[:, cb, c0 + 2: c0 + 3], 1.0 / D),
                        reads=[("st", cb, c0, 2)], writes=[("st", cb, c0, 4)])
                    p.op("v", v, lambda e, c0=c0: e.tensor_tensor(
                        stat[:, cb, c0 + 3: c0 + 4], stat[:, cb, c0 + 1: c0 + 2],
                        stat[:, cb, c0 + 1: c0 + 2], ALU.mult),
                        reads=[("st", cb, c0, 1)], writes=[("st", cb, c0, 3)])
                    p.op("v", v, lambda e, c0=c0: e.tensor_tensor(
                        stat[:, cb, c0 + 5: c0 + 6], stat[:, cb, c0 + 4: c0 + 5],
                        stat[:, cb, c0 + 3: c0 + 4], ALU.subtract),
                        reads=[("st", cb, c0, 4), ("st", cb, c0, 3)], writes=[("st", cb, c0, 5)])

            @block.scalar
            def _(s, cb=cb):
                for tt in range(NTT):
                    c0 = tt * 8
                    p.op("s", s, lambda e, c0=c0: e.activation(
                        stat[:, cb, c0 + 6: c0 + 7], stat[:, cb, c0 + 5: c0 + 6], AF.Ln,
                        bias=col_eps5[:, 0:1]),
                        reads=[("st", cb, c0, 5), "eps5"], writes=[("st", cb, c0, 6)])
                    p.op("s", s, lambda e, c0=c0: e.activation(
                        stat[:, cb, c0 + 7: c0 + 8], stat[:, cb, c0 + 6: c0 + 7], AF.Exp, scale=-0.5),
                        reads=[("st", cb, c0, 6)], writes=[("st", cb, c0, 7)])

            @block.vector
            def _(v, cb=cb):
                for tt in range(NTT):
                    c0 = tt * 8
                    p.op("v", v, lambda e, tt=tt, c0=c0: e.tensor_scalar(
                        out_tm[:, cb, tt, :], out_tm[:, cb, tt, :],
                        stat[:, cb, c0 + 1: c0 + 2], stat[:, cb, c0 + 7: c0 + 8],
                        ALU.subtract, ALU.mult),
                        reads=[("out", cb, tt, 0), ("out", cb, tt, 1),
                               ("st", cb, c0, 1), ("st", cb, c0, 7)],
                        writes=[("out", cb, tt, 0), ("out", cb, tt, 1)])
                    p.op("v", v, lambda e, tt=tt: e.tensor_tensor(
                        out_tm[:, cb, tt, :], out_tm[:, cb, tt, :], x_tm[:, cb, tt, :], ALU.add),
                        reads=[("out", cb, tt, 0), ("out", cb, tt, 1), xtok],
                        writes=[("out", cb, tt, 0), ("out", cb, tt, 1)])

            @block.sync
            def _(sync, cb=cb, ci=ci):
                p.op("q", sync, lambda e, ci=ci: e.dma_start(
                    out=_tok_tiles(x_out, ci * CH, NTT), in_=out_tm[:, cb, :, :]),
                    reads=[("out", cb, tt, nb) for tt in range(NTT) for nb in range(2)],
                    writes=["x_out_hbm"], inc=16, dma="xout%d" % cb)
                if ci + 2 < NCH:
                    load_chunk(sync, ci + 2)

        front(0)
        for ci in range(NCH):
            if ci + 1 < NCH:
                front(ci + 1)
            back(ci)

def build_stage_a(T_=T, SEQ_=SEQ):
    return _std_wrap(decl_stage_a, emit_stage_a, T_, SEQ_)


def decl_stage_c(nc, pfx, T_, SEQ_=None, x_in=None, x_out=None):
    d = lambda n, s, dt=F32, kind="ExternalInput": nc.dram_tensor(pfx + n, s, dt, kind=kind).ap()
    return {
        "x_in": x_in if x_in is not None else d("x_in", [T_, D]), "w_q": d("w_q", [D, D]), "w_k": d("w_k", [D, D]),
        "w_v": d("w_v", [D, D]), "ident": d("ident", [128, 128], BF16),
        "identf": d("identf", [128, 128]), "arange": d("arange", [128, 1]),
        "x_out": x_out if x_out is not None else d("x_out", [T_, D], F32, "ExternalOutput"),
    }


def emit_stage_c(nc, p, block, st_, pfx, A, T_=T, SEQ_=SEQ):
    x_in, w_q, w_k, w_v, ident, identf, arange, x_out = (
        A["x_in"], A["w_q"], A["w_k"], A["w_v"], A["ident"], A["identf"],
        A["arange"], A["x_out"])
    CH = 256
    NCH = T_ // CH
    CPS = SEQ_ // CH
    NTT = CH // 128
    NH = 16

    if True:
        sb = lambda n, s, dt=F32: st_.enter_context(nc.sbuf_tensor(pfx + n, s, dt))
        ps = lambda n, s, dt=F32: st_.enter_context(nc.psum_tensor(pfx + n, s, dt))
        wq = sb("wq", [128, DC, D], BF16)
        wk = sb("wk", [128, DC, D], BF16)
        wv = sb("wv", [128, DC, D], BF16)
        idn = sb("idn", [128, 128], BF16)
        idnf = sb("idnf", [128, 128], F32)
        arangec = sb("arangec", [128, 1])
        x_tm = sb("x_tm", [128, 2, NTT, D])
        out_tm = sb("out_tm", [128, NTT, D])
        xn_tm = sb("xn_tm", [128, 2, NTT, D], BF16)
        xn_fm = sb("xn_fm", [128, 2, DC, CH], BF16)
        v_fm = sb("v_fm", [128, 2, DC, CH])
        vc_fm = sb("vc_fm", [128, DC, CH])
        v_tm = sb("v_tm", [128, NTT, D])
        vc_tm = sb("vc_tm", [128, NTT, D])
        q_tm = sb("q_tm", [128, 2, NTT, D])
        k_tm = sb("k_tm", [128, 2, NTT, D])
        att_tm = sb("att_tm", [128, NTT, D])
        sq_scr = sb("sq_scr", [128, 2, D])
        col_eps5 = sb("col_eps5", [128, 1])
        zero_col = sb("zero_col", [128, 1])
        carry_v = sb("carry_v", [128, DC])
        stat = sb("stat", [128, 2, 8 * NTT])
        sd = sb("sd", [128, NTT, NH])
        sm_ = sb("sm_", [128, NTT, NH])
        mm_ = sb("mm_", [128, NTT, NH])
        e_ = sb("e_", [128, NTT, NH])
        z_ = sb("z_", [128, NTT, NH])
        den_ = sb("den_", [128, NTT, NH])
        rden_ = sb("rden_", [128, NTT, NH])
        icol = sb("icol", [128, 2, NTT])
        ptr = [ps("ptr0", [128, 4, 128], BF16), ps("ptr1", [128, 4, 128], BF16)]
        ptrf = [ps("ptrf0", [128, 128], F32), ps("ptrf1", [128, 128], F32)]
        pa = [ps("pa0", [128, CH]), ps("pa1", [128, CH])]
        py = [ps("py0", [128, 512]), ps("py1", [128, 512])]

        @block.gpsimd
        def _(g):
            for w_hbm, w_sb, tok in ((w_q, wq, "wq"), (w_k, wk, "wk"), (w_v, wv, "wv")):
                p.op("g", g, lambda e, a=w_hbm, b=w_sb: e.dma_start(out=b[:], in_=_chunked(a)),
                     writes=[tok], inc=16, dma=tok)

        def load_chunk(sync, ci):
            p.op("q", sync, lambda e, ci=ci: e.dma_start(
                out=x_tm[:, ci % 2, :, :], in_=_tok_tiles(x_in, ci * CH, NTT)),
                writes=[("x", ci % 2)], inc=16, dma="x%d" % (ci % 2))

        @block.sync
        def _(sync):
            p.op("q", sync, lambda e: e.dma_start(out=idn[:], in_=ident[:, :]),
                 writes=["idn"], inc=16, dma="idn")
            p.op("q", sync, lambda e: e.dma_start(out=idnf[:], in_=identf[:, :]),
                 writes=["idnf"], inc=16, dma="idnf")
            p.op("q", sync, lambda e: e.dma_start(out=arangec[:], in_=arange[:, :]),
                 writes=["arange"], inc=16, dma="arange")
            load_chunk(sync, 0)
            load_chunk(sync, 1)

        @block.vector
        def _(v):
            p.op("v", v, lambda e: e.memset(col_eps5[:], 1e-5), writes=["eps5"])
            p.op("v", v, lambda e: e.memset(zero_col[:], 0.0), writes=["zero"])

        ev_par = [0]
        evf_par = [0]
        pa_i = [0]
        py_i = [0]

        def tr_pack4(srcs, dst_view, dtoks, rtoks):
            par = ev_par[0] % 2
            ev_par[0] += 1
            n = len(srcs)

            @block.tensor
            def _(t, srcs=srcs, par=par):
                def mmg(e):
                    for i2, src in enumerate(srcs):
                        ins = e.transpose(ptr[par][:, i2, :], src, idn[:])
                    return ins
                p.op("t", t, mmg, reads=rtoks + ["idn"], writes=[("ptr", par)])

            @block.scalar
            def _(s, dst_view=dst_view, par=par, n=n):
                p.op("s", s, lambda e: e.copy(dst_view, ptr[par][:, 0:n, :]),
                     reads=[("ptr", par)], writes=dtoks)

        def front(ci):
            cb = ci % 2
            xtok = ("x", cb)
            seq_base = (ci % CPS) * CH

            ln_stats_tm(p, block, lambda tt, cb=cb: x_tm[:, cb, tt, :], [xtok] * NTT,
                        stat[:, cb, :], [0, 8], sq_scr[:, cb, :], NTT, col_eps5, "eps5",
                        stok=("st", cb), sqtok=("sq", cb))

            @block.vector
            def _(v, cb=cb, seq_base=seq_base):
                for tt in range(NTT):
                    c0 = tt * 8
                    p.op("v", v, lambda e, tt=tt, c0=c0: e.tensor_scalar(
                        xn_tm[:, cb, tt, :], x_tm[:, cb, tt, :],
                        stat[:, cb, c0 + 1: c0 + 2], stat[:, cb, c0 + 7: c0 + 8],
                        ALU.subtract, ALU.mult),
                        reads=[xtok, ("st", cb, c0, 1), ("st", cb, c0, 7)],
                        writes=[("xn_tm", cb, tt)])
                    p.op("v", v, lambda e, tt=tt: e.tensor_scalar_add(
                        icol[:, cb, tt:tt + 1], arangec[:, 0:1], float(seq_base + tt * 128)),
                        reads=["arange"], writes=[("icol", cb, tt)])

            for tt in range(NTT):
                for cg in range(2):
                    srcs = [xn_tm[:, cb, tt, (cg * 4 + i2) * 128:(cg * 4 + i2 + 1) * 128]
                            for i2 in range(4)]
                    tr_pack4(srcs,
                             xn_fm[:, cb, cg * 4:(cg + 1) * 4, tt * 128:(tt + 1) * 128],
                             [("xn_fm", cb)], [("xn_tm", cb, tt)])

            for oc in range(DC):
                par = pa_i[0] % 2
                pa_i[0] += 1

                @block.tensor
                def _(t, cb=cb, oc=oc, par=par):
                    def mmg(e):
                        for kc in range(DC):
                            ins = e.matmul(pa[par][:, :], wv[:, kc, oc * 128:(oc + 1) * 128],
                                           xn_fm[:, cb, kc, :], start=(kc == 0), stop=(kc == DC - 1))
                        return ins
                    p.op("t", t, mmg, reads=[("xn_fm", cb), "wv"], writes=[("pa", par)])

                @block.scalar
                def _(s, cb=cb, oc=oc, par=par):
                    p.op("s", s, lambda e: e.copy(v_fm[:, cb, oc, :], pa[par][:, :]),
                         reads=[("pa", par)], writes=[("v_fm", cb, oc)])

            for pj, wsb, wtok, dst, dtok in (("q", wq, "wq", q_tm, "q_tm"),
                                             ("k", wk, "wk", k_tm, "k_tm")):
                for tt in range(NTT):
                    for nb in range(2):
                        par = py_i[0] % 2
                        py_i[0] += 1

                        @block.tensor
                        def _(t, cb=cb, wsb=wsb, wtok=wtok, tt=tt, nb=nb, par=par):
                            def mmg(e):
                                for c in range(DC):
                                    ins = e.matmul(py[par][:, :],
                                                   xn_fm[:, cb, c, tt * 128:(tt + 1) * 128],
                                                   wsb[:, c, nb * 512:(nb + 1) * 512],
                                                   start=(c == 0), stop=(c == DC - 1))
                                return ins
                            p.op("t", t, mmg, reads=[("xn_fm", cb), wtok], writes=[("py", par)])

                        @block.scalar
                        def _(s, cb=cb, dst=dst, dtok=dtok, tt=tt, nb=nb, par=par):
                            p.op("s", s, lambda e: e.copy(
                                dst[:, cb, tt, nb * 512:(nb + 1) * 512], py[par][:, :]),
                                reads=[("py", par)], writes=[(dtok, cb, tt)])

        def back(ci):
            cb = ci % 2
            xtok = ("x", cb)
            first_in_seq = (ci % CPS == 0)

            @block.vector
            def _(v, cb=cb):
                for tt in range(NTT):
                    p.op("v", v, lambda e, tt=tt: e.tensor_tensor(
                        att_tm[:, tt, :], q_tm[:, cb, tt, :], k_tm[:, cb, tt, :], ALU.mult),
                        reads=[("q_tm", cb, tt), ("k_tm", cb, tt)], writes=[("att", tt)])
                    p.op("v", v, lambda e, tt=tt: e.tensor_reduce(
                        sd[:, tt, :], att_tm[:, tt, :].rearrange("p (h d) -> p h d", d=64),
                        AX.X, ALU.add),
                        reads=[("att", tt)], writes=[("sd", tt)])
                    p.op("v", v, lambda e, tt=tt: e.tensor_scalar_mul(
                        sm_[:, tt, :], sd[:, tt, :], 0.125),
                        reads=[("sd", tt)], writes=[("sm", tt)])
                    p.op("v", v, lambda e, tt=tt: e.tensor_scalar_max(
                        mm_[:, tt, :], sm_[:, tt, :], 0.0),
                        reads=[("sm", tt)], writes=[("m", tt)])
                    p.op("v", v, lambda e, tt=tt: e.tensor_tensor(
                        e_[:, tt, :], sm_[:, tt, :], mm_[:, tt, :], ALU.subtract),
                        reads=[("sm", tt), ("m", tt)], writes=[("e", tt)])

            @block.scalar
            def _(s):
                for tt in range(NTT):
                    p.op("s", s, lambda e, tt=tt: e.activation(
                        e_[:, tt, :], e_[:, tt, :], AF.Exp),
                        reads=[("e", tt)], writes=[("e", tt)])
                    p.op("s", s, lambda e, tt=tt: e.activation(
                        z_[:, tt, :], mm_[:, tt, :], AF.Exp, scale=-1.0),
                        reads=[("m", tt)], writes=[("z", tt)])

            @block.vector
            def _(v, cb=cb):
                for tt in range(NTT):
                    p.op("v", v, lambda e, tt=tt, cb=cb: e.scalar_tensor_tensor(
                        den_[:, tt, :], z_[:, tt, :], icol[:, cb, tt:tt + 1], e_[:, tt, :],
                        ALU.mult, ALU.add),
                        reads=[("z", tt), ("icol", cb, tt), ("e", tt)], writes=[("den", tt)])

            @block.scalar
            def _(s):
                for tt in range(NTT):
                    p.op("s", s, lambda e, tt=tt: e.activation(
                        rden_[:, tt, :], den_[:, tt, :], AF.Ln),
                        reads=[("den", tt)], writes=[("rden", tt)])
                    p.op("s", s, lambda e, tt=tt: e.activation(
                        rden_[:, tt, :], rden_[:, tt, :], AF.Exp, scale=-1.0),
                        reads=[("rden", tt)], writes=[("rden", tt)])

            @block.vector
            def _(v, cb=cb, first_in_seq=first_in_seq):
                for c in range(DC):
                    if first_in_seq:
                        init_v = 0.0
                        rv = []
                    else:
                        init_v = carry_v[:, c:c + 1]
                        rv = [("cv", c)]
                    p.op("v", v, lambda e, c=c, init_v=init_v: e.tensor_tensor_scan(
                        vc_fm[:, c, :], v_fm[:, cb, c, :],
                        zero_col[:, 0:1].broadcast_to([128, CH]), init_v, ALU.add, ALU.add),
                        reads=[("v_fm", cb, c), "zero"] + rv, writes=[("vc_fm", c)])
                    p.op("v", v, lambda e, c=c: e.tensor_copy(
                        carry_v[:, c:c + 1], vc_fm[:, c, CH - 1:CH]),
                        reads=[("vc_fm", c)], writes=[("cv", c)])

            for src3, stok3, dst, dtok in ((v_fm, "v_fm", v_tm, "v_tm"),
                                           (vc_fm, "vc_fm", vc_tm, "vc_tm")):
                for tt in range(NTT):
                    for c in range(DC):
                        par = evf_par[0] % 2
                        evf_par[0] += 1

                        @block.tensor
                        def _(t, cb=cb, src3=src3, stok3=stok3, tt=tt, c=c, par=par):
                            src_ap = (src3[:, cb, c, tt * 128:(tt + 1) * 128]
                                      if stok3 == "v_fm" else src3[:, c, tt * 128:(tt + 1) * 128])
                            rd = ("v_fm", cb, c) if stok3 == "v_fm" else ("vc_fm", c)
                            p.op("t", t, lambda e, src_ap=src_ap: e.transpose(
                                ptrf[par][:, :], src_ap, idnf[:]),
                                reads=[rd, "idnf"], writes=[("ptrf", par)])

                        @block.scalar
                        def _(s, dst=dst, dtok=dtok, tt=tt, c=c, par=par):
                            p.op("s", s, lambda e: e.copy(
                                dst[:, tt, c * 128:(c + 1) * 128], ptrf[par][:, :]),
                                reads=[("ptrf", par)], writes=[(dtok, tt)])

            @block.vector
            def _(v):
                for tt in range(NTT):
                    h3 = lambda ap: ap.unsqueeze(2).broadcast_to([128, NH, 64])
                    vv = lambda buf_, tt=tt: buf_[:, tt, :].rearrange("p (h d) -> p h d", d=64)
                    p.op("v", v, lambda e, tt=tt: e.tensor_tensor(
                        vc_tm[:, tt, :], vc_tm[:, tt, :], v_tm[:, tt, :], ALU.subtract),
                        reads=[("vc_tm", tt), ("v_tm", tt)], writes=[("vc_tm", tt)])
                    p.op("v", v, lambda e, tt=tt, h3=h3, vv=vv: e.tensor_tensor(
                        vv(vc_tm), vv(vc_tm), h3(z_[:, tt, :]), ALU.mult),
                        reads=[("vc_tm", tt), ("z", tt)], writes=[("vc_tm", tt)])
                    p.op("v", v, lambda e, tt=tt, h3=h3, vv=vv: e.tensor_tensor(
                        vv(v_tm), vv(v_tm), h3(e_[:, tt, :]), ALU.mult),
                        reads=[("v_tm", tt), ("e", tt)], writes=[("v_tm", tt)])
                    p.op("v", v, lambda e, tt=tt: e.tensor_tensor(
                        att_tm[:, tt, :], vc_tm[:, tt, :], v_tm[:, tt, :], ALU.add),
                        reads=[("vc_tm", tt), ("v_tm", tt)], writes=[("att", tt)])
                    p.op("v", v, lambda e, tt=tt, h3=h3, vv=vv: e.tensor_tensor(
                        vv(att_tm), vv(att_tm), h3(rden_[:, tt, :]), ALU.mult),
                        reads=[("att", tt), ("rden", tt)], writes=[("att", tt)])

            ln_stats_tm(p, block, lambda tt: att_tm[:, tt, :],
                        [("att", tt) for tt in range(NTT)],
                        stat[:, cb, :], [0, 8], sq_scr[:, cb, :], NTT, col_eps5, "eps5",
                        stok=("st", cb), sqtok=("sq", cb))

            @block.vector
            def _(v, cb=cb):
                for tt in range(NTT):
                    c0 = tt * 8
                    p.op("v", v, lambda e, tt=tt, c0=c0: e.tensor_scalar(
                        out_tm[:, tt, :], att_tm[:, tt, :],
                        stat[:, cb, c0 + 1: c0 + 2], stat[:, cb, c0 + 7: c0 + 8],
                        ALU.subtract, ALU.mult),
                        reads=[("att", tt), ("st", cb, c0, 1), ("st", cb, c0, 7)],
                        writes=[("out", tt)])
                    p.op("v", v, lambda e, tt=tt: e.tensor_tensor(
                        out_tm[:, tt, :], out_tm[:, tt, :], x_tm[:, cb, tt, :], ALU.add),
                        reads=[("out", tt), xtok], writes=[("out", tt)])

            @block.sync
            def _(sync, ci=ci):
                p.op("q", sync, lambda e, ci=ci: e.dma_start(
                    out=_tok_tiles(x_out, ci * CH, NTT), in_=out_tm[:, :, :]),
                    reads=[("out", tt) for tt in range(NTT)],
                    writes=["x_out_hbm"], inc=16, dma="xout")
                if ci + 2 < NCH:
                    load_chunk(sync, ci + 2)

        front(0)
        for ci in range(NCH):
            if ci + 1 < NCH:
                front(ci + 1)
            back(ci)


def build_stage_c(T_=T, SEQ_=SEQ):
    return _std_wrap(decl_stage_c, emit_stage_c, T_, SEQ_)


def decl_stage_b(nc, pfx, T_, SEQ_, depth_=6, x_in=None, x_out=None):
    NT = SEQ_ // 128
    NB1 = 22
    d = lambda n, s, dt=F32, kind="ExternalInput": nc.dram_tensor(pfx + n, s, dt, kind=kind).ap()
    return {
        "x_in": x_in if x_in is not None else d("x_in", [T_, D]),
        "wqkv_in": d("wqkv_in", [depth_, D, 3072]),
        "wo_in": d("wo_in", [depth_, D, D]),
        "qs_in": d("qs_in", [depth_, 1, 64]),
        "ks_in": d("ks_in", [depth_, 1, 64]),
        "lng_in": d("lng_in", [depth_, DC, 128]),
        "lnb_in": d("lnb_in", [depth_, DC, 128]),
        "w1_in": d("w1_in", [depth_, D, 2 * NB1 * 128]),
        "b1_in": d("b1_in", [depth_, 2 * NB1, 128]),
        "w2_in": d("w2_in", [depth_, NB1 * 128, D]),
        "b2_in": d("b2_in", [depth_, 1, D]),
        "tab_cs": d("tab_cs", [128, NT, 64], BF16),
        "tab_ss": d("tab_ss", [128, NT, 64], BF16),
        "tab_ci": d("tab_ci", [128, NT, 64], BF16),
        "tab_si": d("tab_si", [128, NT, 64], BF16),
        "maskA": d("maskA", [128, 128], BF16),
        "maskC": d("maskC", [128, 128], BF16),
        "ident": d("ident", [128, 128], BF16),
        "identf": d("identf", [128, 128]),
        "onesf": d("onesf", [1, 128]),
        "onescb": d("onescb", [128, 1], BF16),
        "x_out": x_out if x_out is not None else d("x_out", [T_, D], F32, "ExternalOutput"),
    }


def emit_stage_b(nc, p, block, st_, pfx, A, T_=T, SEQ_=SEQ, depth_=6,
                 half_layers=None, debug_r=False):
    if half_layers is None:
        half_layers = 2 * depth_
    NT = SEQ_ // 128
    NW = SEQ_ // 256
    NSEQ = T_ // SEQ_
    NB1 = 22
    (x_in, wqkv_in, wo_in, qs_in, ks_in, lng_in, lnb_in, w1_in, b1_in, w2_in,
     b2_in, tab_cs_in, tab_ss_in, tab_ci_in, tab_si_in, maskA_in, maskC_in,
     ident, identf, onesf, onescb, x_out) = (
        A["x_in"], A["wqkv_in"], A["wo_in"], A["qs_in"], A["ks_in"], A["lng_in"],
        A["lnb_in"], A["w1_in"], A["b1_in"], A["w2_in"], A["b2_in"], A["tab_cs"],
        A["tab_ss"], A["tab_ci"], A["tab_si"], A["maskA"], A["maskC"], A["ident"],
        A["identf"], A["onesf"], A["onescb"], A["x_out"])

    if True:
        sb = lambda n, s, dt=F32: st_.enter_context(nc.sbuf_tensor(pfx + n, s, dt))
        r_tm = sb("r_tm", [128, NT, D])
        idn = sb("idn", [128, 128], BF16)
        idnf = sb("idnf", [128, 128])
        onesf_sb = sb("onesf_sb", [1, 128])
        onescb_sb = sb("onescb_sb", [128, 1], BF16)
        tabs = {k2: sb("tabsb_" + k2, [128, NT, 64], BF16) for k2 in ("cs", "ss", "ci", "si")}
        maskA = sb("mA", [128, 128], BF16)
        maskC = sb("mC", [128, 128], BF16)
        col_eps5 = sb("col_eps5", [128, 1])

        @block.sync
        def _(sync):
            for a, b2, tok in ((ident, idn, "idn"), (identf, idnf, "idnf"),
                               (onesf, onesf_sb, "onesf"),
                               (onescb, onescb_sb, "onescb"),
                               (maskA_in, maskA, "mA"), (maskC_in, maskC, "mC")):
                p.op("q", sync, lambda e, a=a, b2=b2: e.dma_start(out=b2[:], in_=a[:, :]),
                     writes=[tok], inc=16, dma=tok)
            for k2 in ("cs", "ss", "ci", "si"):
                p.op("q", sync, lambda e, k2=k2: e.dma_start(
                    out=tabs[k2][:], in_=[tab_cs_in, tab_ss_in, tab_ci_in, tab_si_in][
                        ["cs", "ss", "ci", "si"].index(k2)][:, :, :]),
                    writes=["tab" + k2], inc=16, dma="tab" + k2)

        @block.vector
        def _(v):
            p.op("v", v, lambda e: e.memset(col_eps5[:], 1e-5), writes=["eps5"])

        for sq in range(NSEQ):
            pro_stack = ExitStack()
            x_io = pro_stack.enter_context(nc.sbuf_tensor(pfx + "x_io_p%d" % sq, [128, 2, 2, D], F32))
            sq_scr = pro_stack.enter_context(nc.sbuf_tensor(pfx + "sqs_p%d" % sq, [128, D], F32))
            stat = pro_stack.enter_context(nc.sbuf_tensor(pfx + "stat_p%d" % sq, [128, 16], F32))

            def load_x(sync, sq, ci, key, x_io=x_io):
                p.op("q", sync, lambda e: e.dma_start(
                    out=x_io[:, ci % 2, :, :], in_=_tok_tiles(x_in, sq * SEQ_ + ci * 256, 2)),
                    writes=[(key, ci % 2)], inc=16, dma="%s%d" % (key, ci % 2))

            @block.sync
            def _(sync, sq=sq):
                load_x(sync, sq, 0, "xp")
                load_x(sync, sq, 1, "xp")

            for ci in range(NT // 2):
                buf = ci % 2

                ln_stats_tm(p, block, lambda tt, ci=ci, buf=buf, x_io=x_io: x_io[:, buf, tt, :],
                            [("xp", buf)] * 2, stat, [0, 8], sq_scr, 2, col_eps5, "eps5")

                @block.vector
                def _(v, ci=ci, buf=buf, x_io=x_io, stat=stat):
                    for tt in range(2):
                        c0 = tt * 8
                        tile = ci * 2 + tt
                        p.op("v", v, lambda e, tt=tt, c0=c0, tile=tile: e.tensor_scalar(
                            r_tm[:, tile, :], x_io[:, buf, tt, :],
                            stat[:, c0 + 1: c0 + 2], stat[:, c0 + 7: c0 + 8],
                            ALU.subtract, ALU.mult),
                            reads=[("xp", buf), ("st", c0, 1), ("st", c0, 7)],
                            writes=[("r", tile)])

                @block.sync
                def _(sync, sq=sq, ci=ci):
                    if ci + 2 < NT // 2:
                        load_x(sync, sq, ci + 2, "xp")

            p.barrier(block)
            pro_stack.close()

            for li in range(depth_):
                if 2 * li >= half_layers:
                    break
                _emit_attn_layer(nc, p, block, st_, li, sq, r_tm, idn, onesf_sb,
                                 tabs, maskA, maskC, wqkv_in, wo_in, qs_in, ks_in,
                                 NT, NW, pfx)
                if 2 * li + 1 >= half_layers:
                    break
                _emit_ffn_layer(nc, p, block, st_, li, sq, r_tm, idn, idnf, onesf_sb,
                                onescb_sb, col_eps5, lng_in, lnb_in, w1_in, b1_in,
                                w2_in, b2_in, NT, NB1, pfx)

            epi_stack = ExitStack()
            x_io = epi_stack.enter_context(nc.sbuf_tensor(pfx + "x_io_e%d" % sq, [128, 2, 2, D], F32))
            sq_scr = epi_stack.enter_context(nc.sbuf_tensor(pfx + "sqs_e%d" % sq, [128, D], F32))
            stat = epi_stack.enter_context(nc.sbuf_tensor(pfx + "stat_e%d" % sq, [128, 16], F32))

            def load_x(sync, sq, ci, key, x_io=x_io):
                p.op("q", sync, lambda e: e.dma_start(
                    out=x_io[:, ci % 2, :, :], in_=_tok_tiles(x_in, sq * SEQ_ + ci * 256, 2)),
                    writes=[(key, ci % 2)], inc=16, dma="%s%d" % (key, ci % 2))

            @block.sync
            def _(sync, sq=sq):
                load_x(sync, sq, 0, "xe")
                load_x(sync, sq, 1, "xe")

            for ci in range(NT // 2):
                buf = ci % 2

                if not debug_r:
                    ln_stats_tm(p, block, lambda tt, ci=ci: r_tm[:, ci * 2 + tt, :],
                                [("r", ci * 2 + tt) for tt in range(2)],
                                stat, [0, 8], sq_scr, 2, col_eps5, "eps5")

                @block.vector
                def _(v, sq=sq, ci=ci, buf=buf, x_io=x_io, stat=stat, sq_scr=sq_scr):
                    for tt in range(2):
                        c0 = tt * 8
                        tile = ci * 2 + tt
                        if debug_r:
                            p.op("v", v, lambda e, tt=tt, tile=tile: e.tensor_copy(
                                x_io[:, buf, tt, :], r_tm[:, tile, :]),
                                reads=[("r", tile), ("xe", buf)], writes=[("xe", buf)])
                        else:
                            p.op("v", v, lambda e, tt=tt, c0=c0, tile=tile: e.tensor_scalar(
                                sq_scr[:, :], r_tm[:, tile, :],
                                stat[:, c0 + 1: c0 + 2], stat[:, c0 + 7: c0 + 8],
                                ALU.subtract, ALU.mult),
                                reads=[("r", tile), ("st", c0, 1), ("st", c0, 7)],
                                writes=["sq_scr"])
                            p.op("v", v, lambda e, tt=tt: e.tensor_tensor(
                                x_io[:, buf, tt, :], x_io[:, buf, tt, :], sq_scr[:, :],
                                ALU.add),
                                reads=["sq_scr", ("xe", buf)], writes=[("xe", buf)])

                @block.sync
                def _(sync, sq=sq, ci=ci, buf=buf, x_io=x_io):
                    p.op("q", sync, lambda e: e.dma_start(
                        out=_tok_tiles(x_out, sq * SEQ_ + ci * 256, 2),
                        in_=x_io[:, buf, :, :]),
                        reads=[("xe", buf)], writes=["xoutb"], inc=16, dma="xoutb")
                    if ci + 2 < NT // 2:
                        load_x(sync, sq, ci + 2, "xe")

            p.barrier(block)
            epi_stack.close()


def build_stage_b(T_=T, SEQ_=SEQ, depth_=6, half_layers=None, debug_r=False):
    return _std_wrap(lambda nc, pfx, T2, S2: decl_stage_b(nc, pfx, T2, S2, depth_),
                     emit_stage_b, T_, SEQ_, depth_=depth_,
                     half_layers=half_layers, debug_r=debug_r)


def _emit_attn_layer(nc, p, block, st_, li, sq, r_tm, idn, onesf_sb, tabs,
                     maskA, maskC, wqkv_in, wo_in, qs_in, ks_in, NT, NW, pfx=""):
    tag = pfx + "_a%d_%d" % (li, sq)
    with ExitStack() as ph:
        sb = lambda n, s, dt=F32: ph.enter_context(nc.sbuf_tensor(n + tag, s, dt))
        ps = lambda n, s, dt=F32: ph.enter_context(nc.psum_tensor(n + tag, s, dt))
        wqkv_sb = sb("wqkv_sb", [128, DC, 3072], BF16)
        wo_sb = sb("wo_sb", [128, DC, D], BF16)
        qs_bc = sb("qs_bc", [128, 64])
        ks_bc = sb("ks_bc", [128, 64])
        qs_sg = sb("qs_sg", [128, 64])
        ks_sg = sb("ks_sg", [128, 64])
        qs_row = sb("qs_row", [1, 64])
        ks_row = sb("ks_row", [1, 64])
        xs_fm = sb("xs_fm", [128, 2, DC, 257], BF16)
        rcast = sb("rcast", [128, D], BF16)
        q_tm = sb("q_tm", [128, 2, D], BF16)
        k_tm = sb("k_tm", [128, 2, D], BF16)
        v_tm = sb("v_tm", [128, 3, 2, D], BF16)
        q_fm = sb("q_fm", [128, 2, DC, 256], BF16)
        k_fm = sb("k_fm", [128, 3, DC, 256], BF16)
        ao_fm = sb("ao_fm", [128, DC, 256], BF16)
        scr2 = sb("scr2", [128, D], BF16)
        scr3 = sb("scr3", [128, D], BF16)
        ss_t = sb("ss_t", [128, 16])
        rn_t = sb("rn_t", [128, 16])
        a1_t = sb("a1_t", [128, 64])
        a2_t = sb("a2_t", [128, 64])
        p_sb = sb("p_sb", [128, 6, 384], BF16)
        pT_sb = sb("pT_sb", [128, 6, 3, 128], BF16)
        sm1 = sb("sm1", [128, 6, 4])
        ptr = [ps("bptr0", [128, 4, 128], BF16), ps("bptr1", [128, 4, 128], BF16)]
        pq = [ps("bpq0", [128, 512])]
        psim = [ps("bpsim%d" % i2, [128, 384]) for i2 in range(4)]
        ppv = [ps("bppv0", [128, 2, 128])]

        @block.gpsimd
        def _(g):
            p.op("g", g, lambda e: e.dma_start(
                out=wqkv_sb[:], in_=_chunked(wqkv_in[li])), writes=["wqkv"], inc=16, dma="wqkv")
            p.op("g", g, lambda e: e.dma_start(
                out=wo_sb[:], in_=_chunked(wo_in[li])), writes=["wo"], inc=16, dma="wo")

        @block.sync
        def _(sync):
            p.op("q", sync, lambda e: e.dma_start(out=qs_row[:], in_=qs_in[li, :, :]),
                 writes=["qs_row"], inc=16, dma="qs_row")
            p.op("q", sync, lambda e: e.dma_start(out=ks_row[:], in_=ks_in[li, :, :]),
                 writes=["ks_row"], inc=16, dma="ks_row")

        col_ln8 = sb("col_ln8", [128, 1])

        @block.vector
        def _(v):
            p.op("v", v, lambda e: e.memset(col_ln8[:], 2.0794415416798357),
                 writes=["ln8"])

        for row, rtok, bc, btok, sg, sgtok in ((qs_row, "qs_row", qs_bc, "qs_bc", qs_sg, "qs_sg"),
                                               (ks_row, "ks_row", ks_bc, "ks_bc", ks_sg, "ks_sg")):
            @block.tensor
            def _(t, row=row, rtok=rtok):
                p.op("t", t, lambda e: e.matmul(pq[0][:, 0:64], onesf_sb[0:1, :], row[0:1, :],
                                                start=True, stop=True),
                     reads=["onesf", rtok], writes=[("bpq", 0)])

            @block.scalar
            def _(s, bc=bc, btok=btok):
                p.op("s", s, lambda e: e.copy(bc[:, :], pq[0][:, 0:64]),
                     reads=[("bpq", 0)], writes=[btok])

            @block.vector
            def _(v, bc=bc, btok=btok, sg=sg, sgtok=sgtok):
                p.op("v", v, lambda e: e.tensor_scalar_mul(sg[:, 0:32], bc[:, 32:64], -1.0),
                     reads=[btok], writes=[sgtok])
                p.op("v", v, lambda e: e.tensor_copy(sg[:, 32:64], bc[:, 0:32]),
                     reads=[btok], writes=[sgtok])

        tr_par = [0]
        pq_par = [0]
        pv_par = [0]
        unit_k = [0]

        def tr_pack4(srcs, dst_view, dtoks, rtoks):
            par = tr_par[0] % 2
            tr_par[0] += 1
            n = len(srcs)

            @block.tensor
            def _(t, srcs=srcs, par=par):
                def mmg(e):
                    for i2, src in enumerate(srcs):
                        ins = e.transpose(ptr[par][:, i2, :], src, idn[:])
                    return ins
                p.op("t", t, mmg, reads=rtoks + ["idn"], writes=[("bptr", par)])

            @block.scalar
            def _(s, dst_view=dst_view, par=par, n=n):
                p.op("s", s, lambda e: e.copy(dst_view, ptr[par][:, 0:n, :]),
                     reads=[("bptr", par)], writes=dtoks)

        def build_xs(w):
            wb = w % 2

            @block.vector
            def _(v, w=w, wb=wb):
                if w == 0:
                    p.op("v", v, lambda e: e.memset(xs_fm[:, wb, :, 0:1], 0.0),
                         reads=[("xs", w - 2)] if w >= 2 else [], writes=[("xscol", w)])
                else:
                    p.op("v", v, lambda e: e.tensor_copy(
                        xs_fm[:, wb, :, 0:1], xs_fm[:, 1 - wb, :, 256:257]),
                        reads=[("xs", w - 1)] + ([("xs", w - 2)] if w >= 2 else []),
                        writes=[("xscol", w)])

            for tt in range(2):
                tile = 2 * w + tt

                @block.vector
                def _(v, tile=tile):
                    p.op("v", v, lambda e: e.tensor_copy(rcast[:, :], r_tm[:, tile, :]),
                         reads=[("r", tile)], writes=["rcast"])

                for cg in range(2):
                    srcs = [rcast[:, (cg * 4 + i2) * 128:(cg * 4 + i2 + 1) * 128]
                            for i2 in range(4)]
                    tr_pack4(srcs,
                             xs_fm[:, wb, cg * 4:(cg + 1) * 4, 1 + tt * 128:1 + (tt + 1) * 128],
                             [("xs", w)], ["rcast"])

        def prep(w):
            wb = w % 2
            for o in range(6):
                for tt in range(2):
                    par = 0

                    @block.tensor
                    def _(t, w=w, wb=wb, o=o, tt=tt, par=par):
                        def mmg(e):
                            for c in range(DC):
                                off = 1 if c < 4 else 0
                                ins = e.matmul(pq[par][:, :],
                                               xs_fm[:, wb, c, off + tt * 128: off + (tt + 1) * 128],
                                               wqkv_sb[:, c, o * 512:(o + 1) * 512],
                                               start=(c == 0), stop=(c == DC - 1))
                            return ins
                        p.op("t", t, mmg, reads=[("xs", w), ("xscol", w), "wqkv"],
                             writes=[("bpq", par)])

                    @block.scalar
                    def _(s, w=w, wb=wb, o=o, tt=tt, par=par):
                        if o < 2:
                            p.op("s", s, lambda e: e.copy(
                                q_tm[:, tt, (o % 2) * 512:(o % 2 + 1) * 512], pq[par][:, :]),
                                reads=[("bpq", par)], writes=[("q_tm", tt)])
                        elif o < 4:
                            p.op("s", s, lambda e: e.copy(
                                k_tm[:, tt, (o % 2) * 512:(o % 2 + 1) * 512], pq[par][:, :]),
                                reads=[("bpq", par)], writes=[("k_tm", tt)])
                        else:
                            p.op("s", s, lambda e: e.copy(
                                v_tm[:, w % 3, tt, (o % 2) * 512:(o % 2 + 1) * 512], pq[par][:, :]),
                                reads=[("bpq", par)], writes=[("v", w % 3, tt)])

            for src, stok, bc, btok, sg, sgtok, tcs, tss in (
                    (q_tm, "q_tm", qs_bc, "qs_bc", qs_sg, "qs_sg", "cs", "ss"),
                    (k_tm, "k_tm", ks_bc, "ks_bc", ks_sg, "ks_sg", "ci", "si")):
                for tt in range(2):
                    tg = 2 * w + tt

                    @block.scalar
                    def _(s, src=src, stok=stok, tt=tt):
                        p.op("s", s, lambda e: e.activation(
                            scr2[:, :], src[:, tt, :], AF.Square),
                            reads=[(stok, tt)], writes=["scr2"])

                    @block.vector
                    def _(v, tt=tt):
                        p.op("v", v, lambda e: e.tensor_reduce(
                            ss_t[:, :], scr2[:, :].rearrange("p (h d) -> p h d", d=64),
                            AX.X, ALU.add),
                            reads=["scr2"], writes=["ss_t"])
                        p.op("v", v, lambda e: e.tensor_scalar_max(
                            ss_t[:, :], ss_t[:, :], 1e-24),
                            reads=["ss_t"], writes=["ss_t"])

                    @block.scalar
                    def _(s, stok=stok):
                        p.op("s", s, lambda e: e.activation(rn_t[:, :], ss_t[:, :], AF.Ln),
                             reads=["ss_t"], writes=["rn_t"])
                        if stok == "q_tm":
                            p.op("s", s, lambda e: e.activation(
                                rn_t[:, :], rn_t[:, :], AF.Exp, scale=-0.5,
                                bias=col_ln8[:, 0:1]),
                                reads=["rn_t", "ln8"], writes=["rn_t"])
                        else:
                            p.op("s", s, lambda e: e.activation(
                                rn_t[:, :], rn_t[:, :], AF.Exp, scale=-0.5),
                                reads=["rn_t"], writes=["rn_t"])

                    @block.vector
                    def _(v, tg=tg, bc=bc, btok=btok, sg=sg, sgtok=sgtok, tcs=tcs, tss=tss):
                        p.op("v", v, lambda e: e.tensor_tensor(
                            a1_t[:, :], tabs[tcs][:, tg, :], bc[:, :], ALU.mult),
                            reads=["tab" + tcs, btok], writes=["a1_t"])
                        p.op("v", v, lambda e: e.tensor_tensor(
                            a2_t[:, :], tabs[tss][:, tg, :], sg[:, :], ALU.mult),
                            reads=["tab" + tss, sgtok], writes=["a2_t"])

                    h64 = lambda buf_, ttt: buf_[:, ttt, :].rearrange("p (h d) -> p h d", d=64)

                    @block.gpsimd
                    def _(g, src=src, stok=stok, tt=tt, h64=h64):
                        p.op("g", g, lambda e: e.tensor_tensor(
                            scr2[:, :].rearrange("p (h d) -> p h d", d=64), h64(src, tt),
                            a1_t[:, :].unsqueeze(1).broadcast_to([128, 16, 64]), ALU.mult),
                            reads=[(stok, tt), "a1_t"], writes=["scr2"])
                        p.op("g", g, lambda e: e.tensor_tensor(
                            scr3[:, :].rearrange("p (h d) -> p h d", d=64)[:, :, 0:32],
                            h64(src, tt)[:, :, 32:64],
                            a2_t[:, 0:32].unsqueeze(1).broadcast_to([128, 16, 32]), ALU.mult),
                            reads=[(stok, tt), "a2_t"], writes=["scr3"])
                        p.op("g", g, lambda e: e.tensor_tensor(
                            scr3[:, :].rearrange("p (h d) -> p h d", d=64)[:, :, 32:64],
                            h64(src, tt)[:, :, 0:32],
                            a2_t[:, 32:64].unsqueeze(1).broadcast_to([128, 16, 32]), ALU.mult),
                            reads=[(stok, tt), "a2_t"], writes=["scr3"])
                        p.op("g", g, lambda e: e.tensor_tensor(
                            scr2[:, :], scr2[:, :], scr3[:, :], ALU.add),
                            reads=["scr2", "scr3"], writes=["scr2"])

                    @block.vector
                    def _(v, src=src, stok=stok, tt=tt):
                        p.op("v", v, lambda e: e.tensor_tensor(
                            src[:, tt, :].rearrange("p (h d) -> p h d", d=64),
                            scr2[:, :].rearrange("p (h d) -> p h d", d=64),
                            rn_t[:, :].unsqueeze(2).broadcast_to([128, 16, 64]), ALU.mult),
                            reads=["scr2", "rn_t", (stok, tt)],
                            writes=[(stok + "r", tt)])

        def prep_tr(w):
            wb = w % 2
            for src, stok, dstf, dtokf in ((q_tm, "q_tm", q_fm, "q_fm"),
                                           (k_tm, "k_tm", k_fm, "k_fm")):
                wi = wb if dtokf == "q_fm" else w % 3
                for tt in range(2):
                    for cg in range(2):
                        srcs = [src[:, tt, (cg * 4 + i2) * 128:(cg * 4 + i2 + 1) * 128]
                                for i2 in range(4)]
                        tr_pack4(srcs,
                                 dstf[:, wi, cg * 4:(cg + 1) * 4, tt * 128:(tt + 1) * 128],
                                 [(dtokf, wi)], [(stok + "r", tt)])

        def head_sims(w, u):
            wb = w % 2
            tt, h, subs, valid = u
            hp, hc = h % 2, h // 2
            k = unit_k[0]
            spar = k % 4

            @block.tensor
            def _(t, wb=wb, tt=tt, hp=hp, hc=hc, spar=spar, subs=subs):
                def mmg(e):
                    ins = None
                    for ks, s_ in enumerate(subs):
                        if s_ is None:
                            continue
                        kb2, kt = s_
                        m = [maskA, None, maskC][ks]
                        ins = e.matmul(
                            psim[spar][:, ks * 128:(ks + 1) * 128],
                            q_fm[hp * 64:(hp + 1) * 64, wb, hc, tt * 128:(tt + 1) * 128],
                            k_fm[hp * 64:(hp + 1) * 64, kb2, hc, kt * 128:(kt + 1) * 128],
                            start=True, stop=(m is None),
                            tile_position=(hp * 64, 0))
                        if m is not None:
                            ins = e.matmul(psim[spar][:, ks * 128:(ks + 1) * 128],
                                           idn[:, :], m[:, :], start=False, stop=True)
                    return ins
                p.op("t", t, mmg, reads=[("q_fm", wb), ("k_fm", 0), ("k_fm", 1),
                                         ("k_fm", 2), "mA", "mC", "idn"],
                     writes=[("psim", spar)])

        def head_stats(k, u):
            tt, h, subs, valid = u
            spar = k % 4
            pi = k % 6

            @block.vector
            def _(v, spar=spar, valid=valid, pi=pi):
                vs = slice(valid[0] * 128, 384)
                p.op("v", v, lambda e, vs=vs: e.reduce_max(
                    sm1[:, pi, 1:2], psim[spar][:, vs], axis=AX.X, negate=True),
                    reads=[("psim", spar)], writes=[("negm", pi)])

            @block.scalar
            def _(s, valid=valid, spar=spar, pi=pi):
                vs = slice(valid[0] * 128, 384)
                p.op("s", s, lambda e, vs=vs: e.activation(
                    p_sb[:, pi, vs], psim[spar][:, vs], AF.Exp,
                    bias=sm1[:, pi, 1:2], accum_out=sm1[:, pi, 2:3]),
                    reads=[("psim", spar), ("negm", pi)],
                    writes=[("p", pi), ("sumex", pi)])

        def head_out(w, k, u, gpos, gfirst, glast, hc0):
            tt, h, subs, valid = u
            hp, hc = h % 2, h // 2
            pi = k % 6

            @block.vector
            def _(v, valid=valid, pi=pi):
                vs = slice(valid[0] * 128, 384)
                p.op("v", v, lambda e: e.reciprocal(sm1[:, pi, 3:4], sm1[:, pi, 2:3]),
                     reads=[("sumex", pi)], writes=[("rsum", pi)])
                p.op("v", v, lambda e, vs=vs: e.tensor_scalar_mul(
                    p_sb[:, pi, vs], p_sb[:, pi, vs], sm1[:, pi, 3:4]),
                    reads=[("p", pi), ("rsum", pi)], writes=[("p", pi)])

            srcs = [p_sb[:, pi, ks * 128:(ks + 1) * 128] for ks in valid]
            par = tr_par[0] % 2
            tr_par[0] += 1

            @block.tensor
            def _(t, srcs=srcs, par=par):
                def mmg(e):
                    for i2, src in enumerate(srcs):
                        ins = e.transpose(ptr[par][:, i2, :], src, idn[:])
                    return ins
                p.op("t", t, mmg, reads=[("p", pi), "idn"], writes=[("bptr", par)])

            @block.scalar
            def _(s, par=par, pi=pi, valid=valid):
                p.op("s", s, lambda e: e.copy(
                    pT_sb[:, pi, 0:len(valid), :], ptr[par][:, 0:len(valid), :]),
                    reads=[("bptr", par)], writes=[("pT", pi)])

            if gfirst:
                pv_par[0] += 1
            pvp = 0
            slot = hc - hc0

            @block.tensor
            def _(t, h=h, hp=hp, slot=slot, subs=subs, valid=valid, pvp=pvp, pi=pi):
                def mmg(e):
                    first = True
                    for i2, ks in enumerate(valid):
                        kb2, kt = subs[ks]
                        ins = e.matmul(ppv[pvp][hp * 64:(hp + 1) * 64, slot, :],
                                       v_tm[:, kb2, kt, h * 64:(h + 1) * 64],
                                       pT_sb[:, pi, i2, :],
                                       start=first, stop=(i2 == len(valid) - 1),
                                       tile_position=(0, hp * 64))
                        first = False
                    return ins
                p.op("t", t, mmg,
                     reads=[("pT", pi)] +
                           [("v", i, j) for i in range(3) for j in range(2)],
                     writes=[("ppv", pvp)])

            if glast:
                @block.scalar
                def _(s, tt=tt, hc0=hc0, pvp=pvp):
                    p.op("s", s, lambda e: e.copy(
                        ao_fm[:, hc0:hc0 + 2, tt * 128:(tt + 1) * 128],
                        ppv[pvp][:, :, :]),
                        reads=[("ppv", pvp)], writes=["ao_fm"])

        def attn_units(w):
            p0 = w % 3
            pm1 = (w - 1) % 3
            units = []
            for tt in range(2):
                if tt == 0:
                    subs = [(pm1, 0), (pm1, 1), (p0, 0)]
                else:
                    subs = [(pm1, 1), (p0, 0), (p0, 1)]
                if w == 0:
                    subs = [None if s_[0] == pm1 else s_ for s_ in subs]
                valid = [i for i, s_ in enumerate(subs) if s_ is not None]
                for h in range(16):
                    units.append((tt, h, subs, valid))
            return units

        def wo_update(w):
            for tt in range(2):
                tile = 2 * w + tt
                for nb in range(2):
                    par = 0

                    @block.tensor
                    def _(t, tt=tt, nb=nb, par=par):
                        def mmg(e):
                            for c in range(DC):
                                ins = e.matmul(pq[par][:, :],
                                               ao_fm[:, c, tt * 128:(tt + 1) * 128],
                                               wo_sb[:, c, nb * 512:(nb + 1) * 512],
                                               start=(c == 0), stop=(c == DC - 1))
                            return ins
                        p.op("t", t, mmg, reads=["ao_fm", "wo"], writes=[("bpq", par)])

                    @block.vector
                    def _(v, tile=tile, nb=nb, par=par):
                        p.op("v", v, lambda e: e.tensor_tensor(
                            r_tm[:, tile, nb * 512:(nb + 1) * 512],
                            r_tm[:, tile, nb * 512:(nb + 1) * 512],
                            pq[par][:, :], ALU.add),
                            reads=[("bpq", par), ("r", tile)], writes=[("r", tile)])

        build_xs(0)
        prep(0)
        prep_tr(0)
        for w in range(NW):
            units = attn_units(w)
            n_u = len(units)
            k0 = unit_k[0]
            for j in range(n_u + 5):
                if j == 2 and w + 1 < NW:
                    build_xs(w + 1)
                    prep(w + 1)
                if j == n_u // 2 and w + 1 < NW:
                    prep_tr(w + 1)
                if j < n_u:
                    head_sims(w, units[j])
                    unit_k[0] += 1
                if 0 <= j - 5 < n_u:
                    jj = j - 5
                    gpos = jj % 4
                    hc0 = units[jj - gpos][1] // 2
                    head_out(w, k0 + jj, units[jj], gpos, gpos == 0, gpos == 3, hc0)
                if 0 <= j - 1 < n_u:
                    head_stats(k0 + j - 1, units[j - 1])
            wo_update(w)

        p.barrier(block)

def _emit_ffn_layer(nc, p, block, st_, li, sq, r_tm, idn, idnf, onesf_sb, onescb_sb,
                    col_eps5, lng_in, lnb_in, w1_in, b1_in, w2_in, b2_in, NT, NB1, pfx=""):
    NCF = NT // 4
    tag = pfx + "_f%d_%d" % (li, sq)
    with ExitStack() as ph:
        sb = lambda n, s, dt=F32: ph.enter_context(nc.sbuf_tensor(n + tag, s, dt))
        ps = lambda n, s, dt=F32: ph.enter_context(nc.psum_tensor(n + tag, s, dt))
        w2_sb = sb("w2_sb", [128, NB1, D], BF16)
        w1blk = sb("w1blk", [128, 2, DC, 256], BF16)
        act_fm = sb("act_fm", [128, NB1, 512], BF16)
        xs2_fm = sb("xs2_fm", [128, DC, 513], BF16)
        sqf = sb("sqf", [128, DC, 513], BF16)
        xn_fm = sb("xn_fm", [128, DC, 512], BF16)
        rcast = sb("rcast2", [128, D], BF16)
        gb_s = sb("gb_s", [128, 2, 512], BF16)
        m_bc = sb("m_bc", [128, 512])
        rs_bc = sb("rs_bc", [128, 512])
        b2_bc = sb("b2_bc", [128, D])
        scr_w = sb("scr_w", [128, 512])
        g_cols = sb("g_cols", [128, DC])
        b_cols = sb("b_cols", [128, DC])
        b1_cols = sb("b1_cols", [128, 2 * NB1])
        gtmp = sb("gtmp", [DC, 128])
        btmp = sb("btmp", [DC, 128])
        b1tmp = sb("b1tmp", [2 * NB1, 128])
        b2row = sb("b2row", [1, D])
        srow = sb("srow", [1, 2 * 512])
        rrow = sb("rrow", [1, 512])
        ptr2 = [ps("fptr0", [128, 4, 128], BF16)]
        pab = [ps("fpa0", [128, 512]), ps("fpb0", [128, 512]),
               ps("fpa1", [128, 512]), ps("fpb1", [128, 512])]
        pbc = ps("fpbc", [128, 512])
        pst = pbc[0:1, :]
        pyw = [ps("fpy0", [128, 512]), ps("fpy1", [128, 512])]

        @block.gpsimd
        def _(g):
            p.op("g", g, lambda e: e.dma_start(
                out=w2_sb[:], in_=w2_in[li].rearrange("(kc kp) n -> kp kc n", kp=128)),
                writes=["w2sb"], inc=16, dma="w2sb")

        @block.sync
        def _(sync):
            p.op("q", sync, lambda e: e.dma_start(out=gtmp[:], in_=lng_in[li, :, :]),
                 writes=["gtmp"], inc=16, dma="gtmp")
            p.op("q", sync, lambda e: e.dma_start(out=btmp[:], in_=lnb_in[li, :, :]),
                 writes=["btmp"], inc=16, dma="btmp")
            p.op("q", sync, lambda e: e.dma_start(out=b1tmp[:], in_=b1_in[li, :, :]),
                 writes=["b1tmp"], inc=16, dma="b1tmp")
            p.op("q", sync, lambda e: e.dma_start(out=b2row[:], in_=b2_in[li, :, :]),
                 writes=["b2row"], inc=16, dma="b2row")

        for src, stok, dst, dtok, n in ((gtmp, "gtmp", g_cols, "g_cols", DC),
                                        (btmp, "btmp", b_cols, "b_cols", DC),
                                        (b1tmp, "b1tmp", b1_cols, "b1_cols", 2 * NB1)):
            @block.tensor
            def _(t, src=src, stok=stok, n=n):
                p.op("t", t, lambda e: e.transpose(
                    pbc[:, 0:n], src[:, :], idnf[0:n, 0:n]),
                    reads=[stok, "idnf"], writes=["fpbc"])

            @block.scalar
            def _(s, dst=dst, dtok=dtok, n=n):
                p.op("s", s, lambda e: e.copy(dst[:, 0:n], pbc[:, 0:n]),
                     reads=["fpbc"], writes=[dtok])

        for nb in range(2):
            @block.tensor
            def _(t, nb=nb):
                p.op("t", t, lambda e: e.matmul(
                    pbc[:, :], onesf_sb[0:1, :], b2row[0:1, nb * 512:(nb + 1) * 512],
                    start=True, stop=True),
                    reads=["onesf", "b2row"], writes=["fpbc"])

            @block.scalar
            def _(s, nb=nb):
                p.op("s", s, lambda e: e.copy(b2_bc[:, nb * 512:(nb + 1) * 512], pbc[:, :]),
                     reads=["fpbc"], writes=["b2_bc"])

        tr_par = [0]
        ab_seq = [0]
        py_par = [0]

        def build_xs2(cf):
            @block.vector
            def _(v, cf=cf):
                if cf == 0:
                    p.op("v", v, lambda e: e.memset(xs2_fm[:, :, 0:1], 0.0),
                         reads=["xs2"], writes=["xs2"])
                else:
                    p.op("v", v, lambda e: e.tensor_copy(
                        xs2_fm[:, :, 0:1], xs2_fm[:, :, 512:513]),
                        reads=["xs2"], writes=["xs2"])

            for tt in range(4):
                tile = 4 * cf + tt

                @block.vector
                def _(v, tile=tile):
                    p.op("v", v, lambda e: e.tensor_copy(rcast[:, :], r_tm[:, tile, :]),
                         reads=[("r", tile)], writes=["rcast2"])

                for cg in range(2):
                    @block.tensor
                    def _(t, cg=cg):
                        def mmg(e):
                            for i2 in range(4):
                                c = cg * 4 + i2
                                ins = e.transpose(ptr2[0][:, i2, :],
                                                  rcast[:, c * 128:(c + 1) * 128], idn[:])
                            return ins
                        p.op("t", t, mmg, reads=["rcast2", "idn"], writes=["fptr0"])

                    @block.scalar
                    def _(s, tt=tt, cg=cg):
                        p.op("s", s, lambda e: e.copy(
                            xs2_fm[:, cg * 4:(cg + 1) * 4, 1 + tt * 128:1 + (tt + 1) * 128],
                            ptr2[0][:, :, :]),
                            reads=["fptr0"], writes=["xs2"])

        build_xs2(0)

        for cf in range(NCF):
            def shslice(c, n=512):
                off = 1 if c < 4 else 0
                return slice(off, off + n)

            @block.scalar
            def _(s, cf=cf):
                p.op("s", s, lambda e: e.activation(
                    sqf[:, :, :], xs2_fm[:, :, :], AF.Square),
                    reads=["xs2"], writes=["sqf"])

            for which, srcbuf, rowi in (("sum", None, 0), ("sq", None, 1)):
                @block.tensor
                def _(t, which=which, cf=cf, rowi=rowi):
                    def mmg(e):
                        for c in range(DC):
                            src = (xs2_fm[:, c, shslice(c)] if which == "sum"
                                   else sqf[:, c, shslice(c)])
                            ins = e.matmul(pst[0:1, :], onescb_sb[:, 0:1],
                                           src, start=(c == 0), stop=(c == DC - 1))
                        return ins
                    p.op("t", t, mmg,
                         reads=["xs2", "sqf", "onescb"],
                         writes=["fpbc"])

                @block.vector
                def _(v, rowi=rowi):
                    p.op("v", v, lambda e: e.tensor_copy(
                        srow[0:1, rowi * 512:(rowi + 1) * 512], pst[0:1, :]),
                        reads=["fpbc"], writes=[("srow", rowi)])

            @block.vector
            def _(v):
                p.op("v", v, lambda e: e.tensor_scalar_mul(
                    srow[0:1, 0:512], srow[0:1, 0:512], 1.0 / D),
                    reads=[("srow", 0)], writes=[("srow", 0)])
                p.op("v", v, lambda e: e.tensor_tensor(
                    rrow[0:1, :], srow[0:1, 0:512], srow[0:1, 0:512], ALU.mult),
                    reads=[("srow", 0)], writes=["rrow"])
                p.op("v", v, lambda e: e.scalar_tensor_tensor(
                    srow[0:1, 512:1024], srow[0:1, 512:1024], 1.0 / D, rrow[0:1, :],
                    ALU.mult, ALU.subtract),
                    reads=[("srow", 1), "rrow"], writes=[("srow", 1)])

            @block.scalar
            def _(s):
                p.op("s", s, lambda e: e.activation(
                    rrow[0:1, :], srow[0:1, 512:1024], AF.Ln, bias=col_eps5[0:1, 0:1]),
                    reads=[("srow", 1), "eps5"], writes=["rrow"])
                p.op("s", s, lambda e: e.activation(
                    rrow[0:1, :], rrow[0:1, :], AF.Exp, scale=-0.5),
                    reads=["rrow"], writes=["rrow"])

            for src_row, rtok, dst, dtok in ((srow[0:1, 0:512], ("srow", 0), m_bc, "m_bc"),
                                             (rrow[0:1, :], "rrow", rs_bc, "rs_bc")):
                @block.tensor
                def _(t, src_row=src_row, rtok=rtok):
                    p.op("t", t, lambda e: e.matmul(pbc[:, :], onesf_sb[0:1, :], src_row,
                                                    start=True, stop=True),
                         reads=["onesf", rtok], writes=["fpbc"])

                @block.scalar
                def _(s, dst=dst, dtok=dtok):
                    p.op("s", s, lambda e: e.copy(dst[:, :], pbc[:, :]),
                         reads=["fpbc"], writes=[dtok])

            for c in range(DC):
                @block.gpsimd
                def _(g, cf=cf, c=c):
                    p.op("g", g, lambda e: e.tensor_tensor(
                        scr_w[:, :], xs2_fm[:, c, shslice(c)], m_bc[:, :], ALU.subtract),
                        reads=["xs2", "m_bc"], writes=["scr_w"])
                    p.op("g", g, lambda e: e.tensor_tensor(
                        scr_w[:, :], scr_w[:, :], rs_bc[:, :], ALU.mult),
                        reads=["scr_w", "rs_bc"], writes=["scr_w"])

                @block.vector
                def _(v, c=c):
                    p.op("v", v, lambda e: e.tensor_scalar(
                        xn_fm[:, c, :], scr_w[:, :], g_cols[:, c:c + 1], b_cols[:, c:c + 1],
                        ALU.mult, ALU.add),
                        reads=["scr_w", "g_cols", "b_cols"], writes=["xn_fm"])

            for i in range(NB1):
                wpar = i % 2

                @block.gpsimd
                def _(g, li=li, i=i, wpar=wpar):
                    p.op("g", g, lambda e: e.dma_start(
                        out=w1blk[:, wpar, :, :],
                        in_=w1_in[li, :, i * 256:(i + 1) * 256].rearrange(
                            "(kc kp) n -> kp kc n", kp=128)),
                        writes=[("w1b", wpar)], inc=16, dma="w1b%d" % wpar)

                @block.tensor
                def _(t, i=i, wpar=wpar):
                    def mma(e):
                        for c in range(DC):
                            ins = e.matmul(pab[2 * wpar][:, :], w1blk[:, wpar, c, 0:128],
                                           xn_fm[:, c, :], start=(c == 0), stop=(c == DC - 1))
                        return ins
                    p.op("t", t, mma, reads=[("w1b", wpar), "xn_fm"], writes=[("fpa", wpar)])

                    def mmb(e):
                        for c in range(DC):
                            ins = e.matmul(pab[2 * wpar + 1][:, :], w1blk[:, wpar, c, 128:256],
                                           xn_fm[:, c, :], start=(c == 0), stop=(c == DC - 1))
                        return ins
                    p.op("t", t, mmb, reads=[("w1b", wpar), "xn_fm"], writes=[("fpb", wpar)])

                @block.scalar
                def _(s, i=i, wpar=wpar):
                    p.op("s", s, lambda e: e.activation(
                        gb_s[:, wpar, :], pab[2 * wpar + 1][:, :], AF.Gelu,
                        bias=b1_cols[:, 2 * i + 1:2 * i + 2]),
                        reads=[("fpb", wpar), "b1_cols"], writes=[("gb_s", wpar)])

                @block.vector
                def _(v, i=i, wpar=wpar):
                    p.op("v", v, lambda e: e.scalar_tensor_tensor(
                        act_fm[:, i, :], pab[2 * wpar][:, :], b1_cols[:, 2 * i:2 * i + 1],
                        gb_s[:, wpar, :], ALU.add, ALU.mult),
                        reads=[("fpa", wpar), ("gb_s", wpar), "b1_cols"], writes=[("act", i)])

            for tt in range(4):
                tile = 4 * cf + tt
                for nb in range(2):
                    par = py_par[0] % 2
                    py_par[0] += 1

                    @block.tensor
                    def _(t, tt=tt, nb=nb, par=par):
                        def mmg(e):
                            for kc in range(NB1):
                                ins = e.matmul(pyw[par][:, :], act_fm[:, kc, tt * 128:(tt + 1) * 128],
                                               w2_sb[:, kc, nb * 512:(nb + 1) * 512],
                                               start=(kc == 0), stop=(kc == NB1 - 1))
                            return ins
                        p.op("t", t, mmg, reads=[("act", kc2) for kc2 in range(NB1)] + ["w2sb"],
                             writes=[("fpy", par)])

                    @block.vector
                    def _(v, tile=tile, nb=nb, par=par):
                        p.op("v", v, lambda e: e.tensor_tensor(
                            scr_w[:, :], pyw[par][:, :], b2_bc[:, nb * 512:(nb + 1) * 512],
                            ALU.add),
                            reads=[("fpy", par), "b2_bc"], writes=["scr_w"])
                        p.op("v", v, lambda e: e.tensor_tensor(
                            r_tm[:, tile, nb * 512:(nb + 1) * 512],
                            r_tm[:, tile, nb * 512:(nb + 1) * 512],
                            scr_w[:, :], ALU.add),
                            reads=["scr_w", ("r", tile)], writes=[("r", tile)])

            if cf + 1 < NCF:
                build_xs2(cf + 1)

        p.barrier(block)


def _stage_b_consts(SEQ_):
    dim_head, window = 64, 256
    half = np.arange(0, dim_head, 2, dtype=np.float32)
    inv_freq = (1.0 / 10000.0 ** (half / dim_head)).astype(np.float32)
    t_ = np.arange(SEQ_, dtype=np.float32)
    freqs = t_[:, None] * inv_freq[None, :]
    freqs = np.concatenate([freqs, freqs], -1)
    base_scale = (half + 0.4 * dim_head) / (1.4 * dim_head)
    power = (t_ - SEQ_ // 2) / (window // 2)
    scale = (base_scale[None, :] ** power[:, None]).astype(np.float32)
    scale = np.concatenate([scale, scale], -1)
    cos, sin = np.cos(freqs), np.sin(freqs)
    NT = SEQ_ // 128
    shp = lambda a: np.ascontiguousarray(
        a.reshape(NT, 128, 64).transpose(1, 0, 2).astype(ml_dtypes.bfloat16))
    u = np.arange(128)[:, None]
    v = np.arange(128)[None, :]
    maskA = np.where(u <= v, 0.0, -3.0e38).astype(ml_dtypes.bfloat16)
    maskC = np.where(v <= u, 0.0, -3.0e38).astype(ml_dtypes.bfloat16)
    return {
        "tab_cs": shp(cos * scale), "tab_ss": shp(sin * scale),
        "tab_ci": shp(cos / scale), "tab_si": shp(sin / scale),
        "maskA": maskA, "maskC": maskC,
        "onescb": np.ones((128, 1), dtype=ml_dtypes.bfloat16),
        "onesf": np.ones((1, 128), np.float32),
    }


def _stage_b_weights(Wqkv, q_scale, k_scale, Wo, ln_g, ln_b, W1, b1, W2, b2):
    dep = Wqkv.shape[0]
    NB1 = 22
    a = W1[:, :, :2730]
    g = W1[:, :, 2730:]
    ba = b1[:, :2730]
    bg = b1[:, 2730:]
    W1P = np.zeros((dep, D, 2 * NB1 * 128), dtype=np.float32)
    b1P = np.zeros((dep, 2 * NB1, 128), dtype=np.float32)
    for i in range(NB1):
        lo, hi = i * 128, min((i + 1) * 128, 2730)
        n = hi - lo
        W1P[:, :, i * 256: i * 256 + n] = a[:, :, lo:hi]
        W1P[:, :, i * 256 + 128: i * 256 + 128 + n] = g[:, :, lo:hi]
        b1P[:, 2 * i, :n] = ba[:, lo:hi]
        b1P[:, 2 * i + 1, :n] = bg[:, lo:hi]
    W2P = np.zeros((dep, NB1 * 128, D), dtype=np.float32)
    W2P[:, :2730, :] = W2
    return {
        "wqkv_in": np.ascontiguousarray(Wqkv), "wo_in": np.ascontiguousarray(Wo),
        "qs_in": np.ascontiguousarray(q_scale.reshape(dep, 1, 64)),
        "ks_in": np.ascontiguousarray(k_scale.reshape(dep, 1, 64)),
        "lng_in": np.ascontiguousarray(ln_g.reshape(dep, DC, 128)),
        "lnb_in": np.ascontiguousarray(ln_b.reshape(dep, DC, 128)),
        "w1_in": W1P, "b1_in": b1P, "w2_in": W2P,
        "b2_in": np.ascontiguousarray(b2.reshape(dep, 1, D)),
    }


def run_stage_b(x_full, Wqkv, q_scale, k_scale, Wo, ln_g, ln_b, W1, b1, W2, b2,
                half_layers=None, debug_r=False):
    nc = build_stage_b(half_layers=half_layers, debug_r=debug_r)
    shards = _shard_x(x_full)
    wts = _stage_b_weights(Wqkv, q_scale, k_scale, Wo, ln_g, ln_b, W1, b1, W2, b2)
    consts = {**_stage_b_consts(SEQ), **_consts()}
    del consts["arange"]
    in_maps = [{"x_in": shards[c], **wts, **consts} for c in range(NCORES)]
    res = run_bass_kernel_spmd(nc, in_maps, core_ids=list(range(NCORES)))
    return np.stack([res.results[c]["x_out"].reshape(SPC, SEQ, D) for c in range(NCORES)]
                    ).reshape(NCORES * SPC, SEQ, D)


def run_stage_c(x_full, wqkv):
    nc = build_stage_c()
    shards = _shard_x(x_full)
    wr = np.ascontiguousarray(wqkv.reshape(D, 16, 192))
    w1q = np.ascontiguousarray(wr[:, :, 0:64].reshape(D, D))
    w1k = np.ascontiguousarray(wr[:, :, 64:128].reshape(D, D))
    w1v = np.ascontiguousarray(wr[:, :, 128:192].reshape(D, D))
    consts = _consts()
    in_maps = [{"x_in": shards[c], "w_q": w1q, "w_k": w1k, "w_v": w1v, **consts}
               for c in range(NCORES)]
    res = run_bass_kernel_spmd(nc, in_maps, core_ids=list(range(NCORES)))
    return np.stack([res.results[c]["x_out"].reshape(SPC, SEQ, D) for c in range(NCORES)]
                    ).reshape(NCORES * SPC, SEQ, D)


def decl_stage_d(nc, pfx, T_, SEQ_=None, x_in=None, x_out=None):
    NE = 16
    d = lambda n, s, dt=F32, kind="ExternalInput": nc.dram_tensor(pfx + n, s, dt, kind=kind).ap()
    return {
        "x_in": x_in if x_in is not None else d("x_in", [T_, D]),
        "ns_in": d("ns_in", [1, D]),
        "wg_in": d("wg_in", [D, NE]), "bg_in": d("bg_in", [1, NE]),
        "w1_in": d("w1_in", [NE, D, 2048]), "b1_in": d("b1_in", [NE, 16, 128]),
        "w2_in": d("w2_in", [NE, D, D]), "b2_in": d("b2_in", [NE, D]),
        "ident": d("ident", [128, 128], BF16), "identf": d("identf", [128, 128]),
        "onesf": d("onesf", [1, 128]), "esel": d("esel", [NE, NE * 128]),
        "x_out": x_out if x_out is not None else d("x_out", [T_, D], F32, "ExternalOutput"),
        "z_hbm": d("z_hbm", [T_, D], F32, "Internal"),
        "xn2_hbm": d("xn2_hbm", [DC, 128, T_], BF16, "Internal"),
    }


def emit_stage_d(nc, p, block, st_, pfx, A, T_=T, SEQ_=SEQ):
    NE = 16
    (x_in, ns_in, wg_in, bg_in, w1_in, b1_in, w2_in, b2_in, ident, identf,
     onesf, esel, x_out, z_hbm, xn2_hbm) = (
        A["x_in"], A["ns_in"], A["wg_in"], A["bg_in"], A["w1_in"], A["b1_in"],
        A["w2_in"], A["b2_in"], A["ident"], A["identf"], A["onesf"], A["esel"],
        A["x_out"], A["z_hbm"], A["xn2_hbm"])

    CH = 256
    NCH = T_ // CH
    NTT = CH // 128
    TOK = 256
    NSLAB = T_ // TOK

    if True:
        sb = lambda n, s, dt=F32: st_.enter_context(nc.sbuf_tensor(pfx + n, s, dt))
        ps = lambda n, s, dt=F32: st_.enter_context(nc.psum_tensor(pfx + n, s, dt))
        idn = sb("idn", [128, 128], BF16)
        idnf = sb("idnf", [128, 128], F32)
        onesf_sb = sb("onesf_sb", [1, 128], F32)
        esel_sb = sb("esel_sb", [NE, NE * 128], F32)
        ns_row = sb("ns_row", [1, D], F32)
        ns_bc = sb("ns_bc", [128, D], F32)
        wgsb = sb("wgsb", [128, DC, NE], BF16)
        bg_row = sb("bg_row", [1, NE], F32)
        bg_bc = sb("bg_bc", [128, NE], F32)
        gates_sb = sb("gates_sb", [128, T_ // 128, NE])
        gatesT = sb("gatesT", [NE, T_], F32)
        t_acc = sb("t_acc", [128, DC, T_], BF16)
        b2sb = sb("b2sb", [NE, D], F32)
        col_eps5 = sb("col_eps5", [128, 1])
        sq_scr = sb("sq_scr", [128, D])
        stat = sb("stat", [128, 8 * NTT])

        @block.sync
        def _(sync):
            for ap_hbm, ap_sb, tok in ((ident, idn, "idn"), (identf, idnf, "idnf"),
                                       (onesf, onesf_sb, "onesf"), (esel, esel_sb, "esel"),
                                       (ns_in, ns_row, "ns_row"), (bg_in, bg_row, "bg_row"),
                                       (b2_in, b2sb, "b2sb")):
                p.op("q", sync, lambda e, a=ap_hbm, b=ap_sb: e.dma_start(out=b[:], in_=a[:, :] if a.ndim == 2 else a[:, :, :]),
                     writes=[tok], inc=16, dma=tok)

        @block.gpsimd
        def _(g):
            p.op("g", g, lambda e: e.dma_start(out=wgsb[:], in_=_chunked(wg_in)),
                 writes=["wg"], inc=16, dma="wg")

        @block.vector
        def _(v):
            p.op("v", v, lambda e: e.memset(col_eps5[:], 1e-5), writes=["eps5"])

        with ExitStack() as ph1:
            ps1 = lambda n, s, dt=F32: ph1.enter_context(nc.psum_tensor(pfx + n, s, dt))
            sb1 = lambda n, s, dt=F32: ph1.enter_context(nc.sbuf_tensor(pfx + n, s, dt))
            ptr0 = ps1("ptr0", [128, 128], BF16)
            pgt = ps1("pgt", [16, 128], F32)
            pa0 = ps1("pa0", [128, 512], F32)
            pa1 = ps1("pa1", [128, 512], F32)
            x_tm = sb1("x_tm", [128, 2, NTT, D])
            z_tm = sb1("z_tm", [128, NTT, D])
            xn2_tm = sb1("xn2_tm", [128, NTT, D], BF16)
            xn2_fm = sb1("xn2_fm", [128, DC, CH], BF16)
            lg = sb1("lg", [128, NTT, NE])
            cmp_scr = sb1("cmp_scr", [128, NE * NE])
            cnt = sb1("cnt", [128, NE])
            sel = sb1("sel", [128, NE])
            gsml = sb1("gsml", [128, 6])

            for nb in range(2):
                @block.tensor
                def _(t, nb=nb):
                    p.op("t", t, lambda e: e.matmul(
                        pa0[:, 0:512], onesf_sb[0:1, :], ns_row[0:1, nb * 512:(nb + 1) * 512],
                        start=True, stop=True),
                        reads=["onesf", "ns_row"], writes=[("pa0",)])

                @block.scalar
                def _(s, nb=nb):
                    p.op("s", s, lambda e: e.copy(ns_bc[:, nb * 512:(nb + 1) * 512], pa0[:, 0:512]),
                         reads=[("pa0",)], writes=["ns_bc"])

            @block.tensor
            def _(t):
                p.op("t", t, lambda e: e.matmul(pa1[:, 0:NE], onesf_sb[0:1, :], bg_row[0:1, :],
                                                start=True, stop=True),
                     reads=["onesf", "bg_row"], writes=[("pa1",)])

            @block.scalar
            def _(s):
                p.op("s", s, lambda e: e.copy(bg_bc[:, :], pa1[:, 0:NE]),
                     reads=[("pa1",)], writes=["bg_bc"])

            def load_chunk(sync, ci):
                p.op("q", sync, lambda e, ci=ci: e.dma_start(
                    out=x_tm[:, ci % 2, :, :], in_=_tok_tiles(x_in, ci * CH, NTT)),
                    writes=[("x", ci % 2)], inc=16, dma="x%d" % (ci % 2))

            @block.sync
            def _(sync):
                load_chunk(sync, 0)
                load_chunk(sync, 1)

            ev_par = [0]
            pa_par = [0]
            for ci in range(NCH):
                buf = ci % 2
                xtok = ("x", buf)

                ln_stats_tm(p, block, lambda tt, buf=buf, x_tm=x_tm: x_tm[:, buf, tt, :], [xtok] * NTT,
                            stat, [0, 8], sq_scr, NTT, col_eps5, "eps5")

                @block.vector
                def _(v):
                    for tt in range(NTT):
                        c0 = tt * 8
                        p.op("v", v, lambda e, tt=tt, c0=c0: e.tensor_scalar(
                            z_tm[:, tt, :], x_tm[:, buf, tt, :],
                            stat[:, c0 + 1: c0 + 2], stat[:, c0 + 7: c0 + 8],
                            ALU.subtract, ALU.mult),
                            reads=[xtok, ("st", c0, 1), ("st", c0, 7)],
                            writes=[("z", tt)])

                @block.sync
                def _(sync):
                    p.op("q", sync, lambda e, ci=ci: e.dma_start(
                        out=_tok_tiles(z_hbm, ci * CH, NTT), in_=z_tm[:, :, :]),
                        reads=[("z", tt) for tt in range(NTT)], writes=["z_hbm"],
                        inc=16, dma="zout")

                @block.scalar
                def _(s):
                    for tt in range(NTT):
                        c0 = tt * 8
                        p.op("s", s, lambda e, tt=tt, c0=c0: e.activation(
                            sq_scr[:, :], z_tm[:, tt, :], AF.Square,
                            accum_out=stat[:, c0 + 2: c0 + 3]),
                            reads=[("z", tt)], writes=["sq_scr", ("st", c0, 2)])
                        p.op("s", s, lambda e, c0=c0: e.activation(
                            stat[:, c0 + 6: c0 + 7], stat[:, c0 + 2: c0 + 3], AF.Ln,
                            bias=col_eps5[:, 0:1], scale=1.0 / D),
                            reads=[("st", c0, 2), "eps5"], writes=[("st", c0, 6)])
                        p.op("s", s, lambda e, c0=c0: e.activation(
                            stat[:, c0 + 7: c0 + 8], stat[:, c0 + 6: c0 + 7], AF.Exp, scale=-0.5),
                            reads=[("st", c0, 6)], writes=[("st", c0, 7)])

                @block.vector
                def _(v):
                    for tt in range(NTT):
                        c0 = tt * 8
                        p.op("v", v, lambda e, tt=tt: e.tensor_tensor(
                            z_tm[:, tt, :], z_tm[:, tt, :], ns_bc[:, :], ALU.mult),
                            reads=[("z", tt), "ns_bc"], writes=[("z", tt)])
                        p.op("v", v, lambda e, tt=tt, c0=c0: e.tensor_scalar_mul(
                            xn2_tm[:, tt, :], z_tm[:, tt, :], stat[:, c0 + 7: c0 + 8]),
                            reads=[("z", tt), ("st", c0, 7)], writes=[("xn2", tt)])


                for tt in range(NTT):
                    for c in range(DC):
                        @block.tensor
                        def _(t, tt=tt, c=c):
                            p.op("t", t, lambda e: e.transpose(
                                ptr0[:, :], xn2_tm[:, tt, c * 128:(c + 1) * 128], idn[:]),
                                reads=[("xn2", tt), "idn"], writes=["ptr0"])

                        @block.scalar
                        def _(s, tt=tt, c=c):
                            p.op("s", s, lambda e: e.copy(
                                xn2_fm[:, c, tt * 128:(tt + 1) * 128], ptr0[:, :]),
                                reads=["ptr0"], writes=["xn2_fm"])

                @block.sync
                def _(sync):
                    p.op("q", sync, lambda e, ci=ci: e.dma_start(
                        out=xn2_hbm[:, :, ci * CH:(ci + 1) * CH].rearrange("kc p t -> p kc t"),
                        in_=xn2_fm[:, :, :]),
                        reads=["xn2_fm"], writes=["xn2_hbm"], inc=16, dma="xn2out")

                for tt in range(NTT):
                    tg = ci * NTT + tt
                    par = pa_par[0] % 2
                    pa_par[0] += 1
                    pa_ = [pa0, pa1][par]
                    ptok = ("pa0",) if par == 0 else ("pa1",)

                    @block.tensor
                    def _(t, tt=tt, pa_=pa_, ptok=ptok):
                        def mmg(e):
                            for c in range(DC):
                                ins = e.matmul(pa_[:, 0:NE], xn2_fm[:, c, tt * 128:(tt + 1) * 128],
                                               wgsb[:, c, :], start=(c == 0), stop=(c == DC - 1))
                            return ins
                        p.op("t", t, mmg, reads=["xn2_fm", "wg"], writes=[ptok])

                    @block.vector
                    def _(v, tt=tt, tg=tg, pa_=pa_, ptok=ptok):
                        p.op("v", v, lambda e: e.tensor_tensor(
                            lg[:, tt, :], pa_[:, 0:NE], bg_bc[:, :], ALU.add),
                            reads=[ptok, "bg_bc"], writes=[("lg", tt)])
                        p.op("v", v, lambda e: e.tensor_tensor(
                            cmp_scr[:, :].rearrange("p (a b) -> p a b", b=NE),
                            lg[:, tt, :].unsqueeze(1).broadcast_to([128, NE, NE]),
                            lg[:, tt, :].unsqueeze(2).broadcast_to([128, NE, NE]),
                            ALU.is_gt),
                            reads=[("lg", tt)], writes=["cmp"])
                        p.op("v", v, lambda e: e.tensor_reduce(
                            cnt[:, :], cmp_scr[:, :].rearrange("p (a b) -> p a b", a=NE),
                            AX.X, ALU.add),
                            reads=["cmp"], writes=["cnt"])
                        p.op("v", v, lambda e: e.tensor_single_scalar(
                            sel[:, :], cnt[:, :], 4.0, ALU.is_lt),
                            reads=["cnt"], writes=["sel"])
                        p.op("v", v, lambda e: e.reduce_max(
                            gsml[:, 0:1], lg[:, tt, :], axis=AX.X),
                            reads=[("lg", tt)], writes=["mx"])
                        p.op("v", v, lambda e: e.tensor_scalar(
                            gates_sb[:, tg, :], lg[:, tt, :], gsml[:, 0:1], None,
                            ALU.subtract),
                            reads=[("lg", tt), "mx"], writes=[("gates", tg)])

                    @block.scalar
                    def _(s, tg=tg):
                        p.op("s", s, lambda e: e.activation(
                            gates_sb[:, tg, :], gates_sb[:, tg, :], AF.Exp),
                            reads=[("gates", tg)], writes=[("gates", tg)])

                    @block.vector
                    def _(v, tg=tg):
                        p.op("v", v, lambda e: e.tensor_tensor(
                            gates_sb[:, tg, :], gates_sb[:, tg, :], sel[:, :], ALU.mult),
                            reads=[("gates", tg), "sel"], writes=[("gates", tg)])
                        p.op("v", v, lambda e: e.reduce_sum(
                            gsml[:, 1:2], gates_sb[:, tg, :], axis=AX.X),
                            reads=[("gates", tg)], writes=["dn"])
                        p.op("v", v, lambda e: e.reciprocal(
                            gsml[:, 2:3], gsml[:, 1:2]),
                            reads=["dn"], writes=["rdn"])
                        p.op("v", v, lambda e: e.tensor_scalar_mul(
                            gates_sb[:, tg, :], gates_sb[:, tg, :], gsml[:, 2:3]),
                            reads=[("gates", tg), "rdn"], writes=[("gates", tg)])

                    @block.tensor
                    def _(t, tg=tg):
                        p.op("t", t, lambda e: e.transpose(
                            pgt[:, :], gates_sb[:, tg, :], idnf[:]),
                            reads=[("gates", tg), "idnf"], writes=["pgt"])

                    @block.scalar
                    def _(s, tg=tg):
                        p.op("s", s, lambda e: e.copy(
                            gatesT[:, tg * 128:(tg + 1) * 128], pgt[:, :]),
                            reads=["pgt"], writes=["gatesT"])

                @block.sync
                def _(sync):
                    if ci + 2 < NCH:
                        load_chunk(sync, ci + 2)

            p.barrier(block)

        with ExitStack() as ph2:
            ps2 = lambda n, s, dt=F32: ph2.enter_context(nc.psum_tensor(pfx + n, s, dt))
            sb2 = lambda n, s, dt=F32: ph2.enter_context(nc.sbuf_tensor(pfx + n, s, dt))
            pa0 = ps2("p2a0", [128, TOK], F32)
            pa1 = ps2("p2a1", [128, TOK], F32)
            py0 = ps2("p2y0", [128, TOK], F32)
            py1 = ps2("p2y1", [128, TOK], F32)
            pgb = ps2("pgb", [128, 512], F32)
            w1sb = sb2("w1sb", [128, DC, 2048], BF16)
            w2sb = sb2("w2sb", [128, DC, D], BF16)
            b1r = sb2("b1r", [16, 128], F32)
            b1col = sb2("b1col", [128, 16], F32)
            gate_bc = sb2("gate_bc", [128, T_], F32)
            xslab = sb2("xslab", [128, 2, DC, TOK], BF16)
            a_s = sb2("a_s", [128, 2, TOK], F32)
            b_s = sb2("b_s", [128, 2, TOK], F32)
            g_s = sb2("g_s", [128, 2, TOK], F32)
            v1_s = sb2("v1_s", [128, 2, TOK], F32)
            ss_s = sb2("ss_s", [128, DC, TOK], BF16)

            def load_slab(sync, e, sl):
                p.op("q", sync, lambda ee, e=e, sl=sl: ee.dma_start(
                    out=xslab[:, sl % 2, :, :],
                    in_=xn2_hbm[:, :, sl * TOK:(sl + 1) * TOK].rearrange("kc p t -> p kc t")),
                    reads=["xn2_hbm"], writes=[("xs", sl % 2)], inc=16, dma="xs%d" % (sl % 2))

            for e_i in range(NE):
                @block.gpsimd
                def _(g, e_i=e_i):
                    p.op("g", g, lambda e: e.dma_start(
                        out=w1sb[:], in_=w1_in[e_i].rearrange("(kc kp) n -> kp kc n", kp=128)),
                        writes=["w1"], inc=16, dma="w1")
                    p.op("g", g, lambda e: e.dma_start(
                        out=w2sb[:], in_=w2_in[e_i].rearrange("(kc kp) n -> kp kc n", kp=128)),
                        writes=["w2"], inc=16, dma="w2")

                @block.sync
                def _(sync, e_i=e_i):
                    p.op("q", sync, lambda e: e.dma_start(out=b1r[:], in_=b1_in[e_i, :, :]),
                         writes=["b1r"], inc=16, dma="b1r")
                    if e_i == 0:
                        load_slab(sync, e_i, 0)
                        load_slab(sync, e_i, 1)

                @block.tensor
                def _(t):
                    p.op("t", t, lambda e: e.transpose(pa0[:, 0:16], b1r[:, :], idnf[0:16, 0:16]),
                         reads=["b1r", "idnf"], writes=[("p2a0",)])

                @block.scalar
                def _(s):
                    p.op("s", s, lambda e: e.copy(b1col[:, :], pa0[:, 0:16]),
                         reads=[("p2a0",)], writes=["b1col"])

                for gslab in range(T_ // 512):
                    @block.tensor
                    def _(t, e_i=e_i, gslab=gslab):
                        p.op("t", t, lambda e: e.matmul(
                            pgb[:, :], esel_sb[:, e_i * 128:(e_i + 1) * 128],
                            gatesT[:, gslab * 512:(gslab + 1) * 512], start=True, stop=True),
                            reads=["esel", "gatesT"], writes=["pgb"])

                    @block.scalar
                    def _(s, gslab=gslab):
                        p.op("s", s, lambda e: e.copy(
                            gate_bc[:, gslab * 512:(gslab + 1) * 512], pgb[:, :]),
                            reads=["pgb"], writes=["gate_bc"])

                for sl in range(NSLAB):
                    sbuf_ = sl % 2
                    xs_tok = ("xs", sbuf_)

                    for j in range(DC):
                        @block.tensor
                        def _(t, j=j, sbuf_=sbuf_):
                            def mma(e):
                                for kc in range(DC):
                                    ins = e.matmul(pa0[:, :], w1sb[:, kc, j * 128:(j + 1) * 128],
                                                   xslab[:, sbuf_, kc, :],
                                                   start=(kc == 0), stop=(kc == DC - 1))
                                return ins
                            p.op("t", t, mma, reads=["w1", xs_tok], writes=[("p2a0",)])

                            def mmb(e):
                                for kc in range(DC):
                                    ins = e.matmul(pa1[:, :], w1sb[:, kc, 1024 + j * 128:1024 + (j + 1) * 128],
                                                   xslab[:, sbuf_, kc, :],
                                                   start=(kc == 0), stop=(kc == DC - 1))
                                return ins
                            p.op("t", t, mmb, reads=["w1", xs_tok], writes=[("p2a1",)])

                        jp = j % 2

                        @block.vector
                        def _(v, j=j, jp=jp):
                            p.op("v", v, lambda e: e.tensor_scalar(
                                a_s[:, jp, :], pa0[:, :], b1col[:, j:j + 1], 7.0, ALU.add, ALU.min),
                                reads=[("p2a0",), "b1col"], writes=[("a_s", jp)])
                            p.op("v", v, lambda e: e.tensor_scalar(
                                b_s[:, jp, :], pa1[:, :], b1col[:, 8 + j:9 + j], -7.0, ALU.add, ALU.max),
                                reads=[("p2a1",), "b1col"], writes=[("b_s", jp)])
                            p.op("v", v, lambda e: e.tensor_scalar_min(
                                b_s[:, jp, :], b_s[:, jp, :], 7.0),
                                reads=[("b_s", jp)], writes=[("b_s", jp)])

                        @block.scalar
                        def _(s, j=j, jp=jp):
                            p.op("s", s, lambda e: e.activation(
                                g_s[:, jp, :], a_s[:, jp, :], AF.Sigmoid, scale=1.702),
                                reads=[("a_s", jp)], writes=[("g_s", jp)])

                        @block.vector
                        def _(v, j=j, jp=jp):
                            p.op("v", v, lambda e: e.scalar_tensor_tensor(
                                v1_s[:, jp, :], b_s[:, jp, :], 1.0, g_s[:, jp, :], ALU.add, ALU.mult),
                                reads=[("b_s", jp), ("g_s", jp)], writes=[("v1_s", jp)])

                        @block.gpsimd
                        def _(g, j=j, sl=sl, jp=jp):
                            p.op("g", g, lambda e: e.tensor_tensor(
                                v1_s[:, jp, :], v1_s[:, jp, :], a_s[:, jp, :], ALU.mult),
                                reads=[("v1_s", jp), ("a_s", jp)], writes=[("v1_s", jp)])
                            p.op("g", g, lambda e, sl=sl, j=j: e.tensor_tensor(
                                ss_s[:, j, :], v1_s[:, jp, :], gate_bc[:, sl * TOK:(sl + 1) * TOK],
                                ALU.mult),
                                reads=[("v1_s", jp), "gate_bc"], writes=[("ss", j)])

                    for oc2 in range(DC):
                        ypar = oc2 % 2
                        py_ = [py0, py1][ypar]
                        ytok = ("p2y", ypar)

                        @block.tensor
                        def _(t, oc2=oc2, py_=py_, ytok=ytok):
                            def mmy(e):
                                for j in range(DC):
                                    ins = e.matmul(py_[:, :], w2sb[:, j, oc2 * 128:(oc2 + 1) * 128],
                                                   ss_s[:, j, :], start=(j == 0), stop=(j == DC - 1))
                                return ins
                            p.op("t", t, mmy,
                                 reads=["w2"] + [("ss", j) for j in range(DC)],
                                 writes=[ytok])

                        @block.vector
                        def _(v, e_i=e_i, sl=sl, oc2=oc2, py_=py_, ytok=ytok):
                            if e_i == 0:
                                p.op("v", v, lambda e: e.tensor_copy(
                                    t_acc[:, oc2, sl * TOK:(sl + 1) * TOK], py_[:, :]),
                                    reads=[ytok], writes=[("tacc", sl, oc2)])
                            else:
                                p.op("v", v, lambda e: e.tensor_tensor(
                                    t_acc[:, oc2, sl * TOK:(sl + 1) * TOK],
                                    t_acc[:, oc2, sl * TOK:(sl + 1) * TOK],
                                    py_[:, :], ALU.add),
                                    reads=[ytok, ("tacc", sl, oc2)],
                                    writes=[("tacc", sl, oc2)])

                    @block.sync
                    def _(sync, e_i=e_i, sl=sl):
                        nxt = sl + 2
                        if nxt < NSLAB:
                            load_slab(sync, e_i, nxt)
                        elif e_i + 1 < NE:
                            load_slab(sync, e_i + 1, nxt - NSLAB)

            p.barrier(block)

        with ExitStack() as ph3:
            ps3 = lambda n, s, dt=F32: ph3.enter_context(nc.psum_tensor(pfx + n, s, dt))
            sb3 = lambda n, s, dt=F32: ph3.enter_context(nc.sbuf_tensor(pfx + n, s, dt))
            ptr0 = ps3("p3tr", [128, 128], BF16)
            pb2 = [ps3("pb20", [128, 512], F32), ps3("pb21", [128, 512], F32)]
            x_tm = sb3("x3_tm", [128, 2, NTT, D])
            z3_tm = sb3("z3_tm", [128, 2, NTT, D])
            u_tm = sb3("u_tm", [128, NTT, D])
            out_tm = sb3("out3_tm", [128, NTT, D])

            def load_x3(sync, ci):
                p.op("q", sync, lambda e, ci=ci: e.dma_start(
                    out=x_tm[:, ci % 2, :, :], in_=_tok_tiles(x_in, ci * CH, NTT)),
                    writes=[("x3", ci % 2)], inc=16, dma="x3%d" % (ci % 2))
                p.op("q", sync, lambda e, ci=ci: e.dma_start(
                    out=z3_tm[:, ci % 2, :, :], in_=_tok_tiles(z_hbm, ci * CH, NTT)),
                    reads=["z_hbm"], writes=[("z3", ci % 2)], inc=16, dma="z3%d" % (ci % 2))

            @block.sync
            def _(sync):
                load_x3(sync, 0)
                load_x3(sync, 1)

            for ci in range(NCH):
                buf = ci % 2
                for tt in range(NTT):
                    tg = ci * NTT + tt

                    @block.tensor
                    def _(t, tg=tg):
                        p.op("t", t, lambda e: e.matmul(
                            pb2[0][:, 0:512], gatesT[:, tg * 128:(tg + 1) * 128],
                            b2sb[:, 0:512], start=True, stop=True),
                            reads=["gatesT", "b2sb"], writes=[("pb2", 0)])
                        p.op("t", t, lambda e: e.matmul(
                            pb2[1][:, 0:512], gatesT[:, tg * 128:(tg + 1) * 128],
                            b2sb[:, 512:1024], start=True, stop=True),
                            reads=["gatesT", "b2sb"], writes=[("pb2", 1)])

                    for c in range(DC):
                        @block.tensor
                        def _(t, tg=tg, c=c):
                            p.op("t", t, lambda e: e.transpose(
                                ptr0[:, :], t_acc[:, c, tg * 128:(tg + 1) * 128], idn[:]),
                                reads=[("tacc", tg // (TOK // 128), c), "idn"],
                                writes=["p3tr"])

                        @block.scalar
                        def _(s, tt=tt, c=c):
                            p.op("s", s, lambda e: e.copy(
                                u_tm[:, tt, c * 128:(c + 1) * 128], ptr0[:, :]),
                                reads=["p3tr"], writes=[("u", tt)])

                    @block.vector
                    def _(v, tt=tt, buf=buf):
                        p.op("v", v, lambda e: e.tensor_tensor(
                            u_tm[:, tt, :], u_tm[:, tt, :], z3_tm[:, buf, tt, :], ALU.add),
                            reads=[("u", tt), ("z3", buf)], writes=[("u", tt)])
                        p.op("v", v, lambda e: e.tensor_tensor(
                            u_tm[:, tt, 0:512], u_tm[:, tt, 0:512], pb2[0][:, 0:512], ALU.add),
                            reads=[("u", tt), ("pb2", 0)], writes=[("u", tt)])
                        p.op("v", v, lambda e: e.tensor_tensor(
                            u_tm[:, tt, 512:1024], u_tm[:, tt, 512:1024], pb2[1][:, 0:512], ALU.add),
                            reads=[("u", tt), ("pb2", 1)], writes=[("u", tt)])

                ln_stats_tm(p, block, lambda tt: u_tm[:, tt, :],
                            [("u", tt) for tt in range(NTT)],
                            stat, [0, 8], sq_scr, NTT, col_eps5, "eps5")

                @block.vector
                def _(v, ci=ci, buf=buf):
                    for tt in range(NTT):
                        c0 = tt * 8
                        p.op("v", v, lambda e, tt=tt, c0=c0: e.tensor_scalar(
                            out_tm[:, tt, :], u_tm[:, tt, :],
                            stat[:, c0 + 1: c0 + 2], stat[:, c0 + 7: c0 + 8],
                            ALU.subtract, ALU.mult),
                            reads=[("u", tt), ("st", c0, 1), ("st", c0, 7)],
                            writes=[("o3", tt)])
                        p.op("v", v, lambda e, tt=tt: e.tensor_tensor(
                            out_tm[:, tt, :], out_tm[:, tt, :], x_tm[:, buf, tt, :], ALU.add),
                            reads=[("o3", tt), ("x3", buf)], writes=[("o3", tt)])

                @block.sync
                def _(sync, ci=ci):
                    p.op("q", sync, lambda e, ci=ci: e.dma_start(
                        out=_tok_tiles(x_out, ci * CH, NTT), in_=out_tm[:, :, :]),
                        reads=[("o3", tt) for tt in range(NTT)],
                        writes=["x_out_hbm"], inc=16, dma="xo3")
                    if ci + 2 < NCH:
                        load_x3(sync, ci + 2)


def build_stage_d(T_=T, SEQ_=SEQ):
    return _std_wrap(decl_stage_d, emit_stage_d, T_, SEQ_)


def _moe_prep(w1, b1, w2, b2):
    w1p = np.concatenate([w1[:, :, 0::2], w1[:, :, 1::2]], axis=2)
    b1p = np.concatenate([b1[:, 0::2], b1[:, 1::2]], axis=1)
    b1t = np.ascontiguousarray(b1p.reshape(16, 16, 128))
    return (np.ascontiguousarray(w1p), b1t,
            np.ascontiguousarray(w2), np.ascontiguousarray(b2))


def run_stage_d(x_full, norm_scale, wg, bg, w1, b1, w2, b2):
    nc = build_stage_d()
    shards = _shard_x(x_full)
    w1p, b1t, w2c, b2c = _moe_prep(w1, b1, w2, b2)
    esel = np.zeros((16, 16 * 128), dtype=np.float32)
    for e in range(16):
        esel[e, e * 128:(e + 1) * 128] = 1.0
    consts = _consts()
    in_maps = [{"x_in": shards[c], "ns_in": norm_scale.reshape(1, D), "wg_in": wg,
                "bg_in": bg.reshape(1, 16), "w1_in": w1p, "b1_in": b1t,
                "w2_in": w2c, "b2_in": b2c, "onesf": np.ones((1, 128), np.float32),
                "esel": esel, **consts}
               for c in range(NCORES)]
    res = run_bass_kernel_spmd(nc, in_maps, core_ids=list(range(NCORES)))
    return np.stack([res.results[c]["x_out"].reshape(SPC, SEQ, D) for c in range(NCORES)]
                    ).reshape(NCORES * SPC, SEQ, D)


def _consts():
    return {
        "ident": np.eye(128, dtype=ml_dtypes.bfloat16),
        "identf": np.eye(128, dtype=np.float32),
        "arange": np.arange(128, dtype=np.float32).reshape(128, 1),
    }


def _shard_x(x):
    x = np.ascontiguousarray(x, dtype=np.float32)
    return [x[c * SPC:(c + 1) * SPC].reshape(T, D) for c in range(NCORES)]


def run_stage_a(x_full, Wq, Wk, Wv, Wc):
    nc = build_stage_a()
    shards = _shard_x(x_full)
    consts = _consts()
    in_maps = [{"x_in": shards[c], "w_q": Wq, "w_k": Wk, "w_v": Wv, "w_c": Wc, **consts}
               for c in range(NCORES)]
    res = run_bass_kernel_spmd(nc, in_maps, core_ids=list(range(NCORES)))
    return np.stack([res.results[c]["x_out"].reshape(SPC, SEQ, D) for c in range(NCORES)]
                    ).reshape(NCORES * SPC, SEQ, D)


def build_fused(T_=T, SEQ_=SEQ):
    nc = bass.Bass("TRN2", target_bir_lowering=False, debug=False)
    x1 = nc.dram_tensor("x1_hbm", [T_, D], F32, kind="Internal").ap()
    x2 = nc.dram_tensor("x2_hbm", [T_, D], F32, kind="Internal").ap()
    x3 = nc.dram_tensor("x3_hbm", [T_, D], F32, kind="Internal").ap()
    A = decl_stage_a(nc, "a_", T_, x_out=x1)
    B = decl_stage_b(nc, "b_", T_, SEQ_, x_in=x1, x_out=x2)
    C = decl_stage_c(nc, "c_", T_, x_in=x2, x_out=x3)
    Dd = decl_stage_d(nc, "d_", T_, x_in=x3)

    with ExitStack() as st_:
        p = Prog(nc, st_)
        block = FakeBlock()
        with ExitStack() as sA:
            emit_stage_a(nc, p, block, sA, "a_", A, T_, SEQ_)
            p.barrier(block)
        with ExitStack() as sB:
            emit_stage_b(nc, p, block, sB, "b_", B, T_, SEQ_)
            p.barrier(block)
        with ExitStack() as sC:
            emit_stage_c(nc, p, block, sC, "c_", C, T_, SEQ_)
            p.barrier(block)
        with ExitStack() as sD:
            emit_stage_d(nc, p, block, sD, "d_", Dd, T_, SEQ_)
        p.final_wait()
        real_block = st_.enter_context(nc.Block())
        p.flush(real_block)
    return nc


def _fused_in_maps(inputs):
    shards = _shard_x(inputs["L0_attn_X"])
    cst = _consts()
    wr = np.ascontiguousarray(inputs["L1_attn_wqkv"].reshape(D, 16, 192))
    bwts = _stage_b_weights(inputs["L0_ffn_Wqkv"], inputs["L0_ffn_q_scale"],
                            inputs["L0_ffn_k_scale"], inputs["L0_ffn_Wo"],
                            inputs["L0_ffn_ln_g"], inputs["L0_ffn_ln_b"],
                            inputs["L0_ffn_W1"], inputs["L0_ffn_b1"],
                            inputs["L0_ffn_W2"], inputs["L0_ffn_b2"])
    bcst = _stage_b_consts(SEQ)
    w1p, b1t, w2c, b2c = _moe_prep(inputs["L1_ffn_w1"], inputs["L1_ffn_b1"],
                                   inputs["L1_ffn_w2"], inputs["L1_ffn_b2"])
    esel = np.zeros((16, 16 * 128), dtype=np.float32)
    for e in range(16):
        esel[e, e * 128:(e + 1) * 128] = 1.0
    shared = {}
    shared.update({"a_w_q": inputs["L0_attn_Wq"], "a_w_k": inputs["L0_attn_Wk"],
                   "a_w_v": inputs["L0_attn_Wv"], "a_w_c": inputs["L0_attn_Wc"],
                   "a_ident": cst["ident"]})
    shared.update({"b_" + k: v for k, v in bwts.items()})
    shared.update({"b_tab_cs": bcst["tab_cs"], "b_tab_ss": bcst["tab_ss"],
                   "b_tab_ci": bcst["tab_ci"], "b_tab_si": bcst["tab_si"],
                   "b_maskA": bcst["maskA"], "b_maskC": bcst["maskC"],
                   "b_ident": cst["ident"], "b_identf": cst["identf"],
                   "b_onesf": bcst["onesf"], "b_onescb": bcst["onescb"]})
    shared.update({"c_w_q": np.ascontiguousarray(wr[:, :, 0:64].reshape(D, D)),
                   "c_w_k": np.ascontiguousarray(wr[:, :, 64:128].reshape(D, D)),
                   "c_w_v": np.ascontiguousarray(wr[:, :, 128:192].reshape(D, D)),
                   "c_ident": cst["ident"], "c_identf": cst["identf"],
                   "c_arange": cst["arange"]})
    shared.update({"d_ns_in": inputs["L1_ffn_norm_scale"].reshape(1, D),
                   "d_wg_in": inputs["L1_ffn_wg"],
                   "d_bg_in": inputs["L1_ffn_bg"].reshape(1, 16),
                   "d_w1_in": w1p, "d_b1_in": b1t, "d_w2_in": w2c, "d_b2_in": b2c,
                   "d_ident": cst["ident"], "d_identf": cst["identf"],
                   "d_onesf": bcst["onesf"], "d_esel": esel})
    return [{"a_x_in": shards[c], **shared} for c in range(NCORES)]


def kernel(**inputs):
    nc = build_fused()
    in_maps = _fused_in_maps(inputs)
    res = run_bass_kernel_spmd(nc, in_maps, core_ids=list(range(NCORES)))
    return np.stack([res.results[c]["d_x_out"].reshape(SPC, SEQ, D) for c in range(NCORES)]
                    ).reshape(NCORES * SPC, SEQ, D)


def kernel_multilaunch(**inputs):
    x = run_stage_a(inputs["L0_attn_X"], inputs["L0_attn_Wq"], inputs["L0_attn_Wk"],
                    inputs["L0_attn_Wv"], inputs["L0_attn_Wc"])
    x = run_stage_b(x, inputs["L0_ffn_Wqkv"], inputs["L0_ffn_q_scale"],
                    inputs["L0_ffn_k_scale"], inputs["L0_ffn_Wo"], inputs["L0_ffn_ln_g"],
                    inputs["L0_ffn_ln_b"], inputs["L0_ffn_W1"], inputs["L0_ffn_b1"],
                    inputs["L0_ffn_W2"], inputs["L0_ffn_b2"])
    x = run_stage_c(x, inputs["L1_attn_wqkv"])
    x = run_stage_d(x, inputs["L1_ffn_norm_scale"], inputs["L1_ffn_wg"], inputs["L1_ffn_bg"],
                    inputs["L1_ffn_w1"], inputs["L1_ffn_b1"], inputs["L1_ffn_w2"],
                    inputs["L1_ffn_b2"])
    return x


# version 09 (final)
# speedup vs baseline: 2.0838x; 1.0371x over previous
import numpy as np
import ml_dtypes
from contextlib import ExitStack

import concourse.bass as bass
import concourse.mybir as mybir
from concourse.bass_utils import run_bass_kernel_spmd

F32 = mybir.dt.float32
BF16 = mybir.dt.bfloat16
AF = mybir.ActivationFunctionType
ALU = mybir.AluOpType
AX = mybir.AxisListType

NCORES = 8
SEQ = 2048
SPC = 2
T = SEQ * SPC
D = 1024
DC = D // 128

ENGS = ("t", "v", "s", "g", "q")


class Sem:
    def __init__(self, handle):
        self.h = handle
        self.n = 0

    def inc(self, instr, k=1):
        instr.then_inc(self.h, k)
        self.n += k
        return self.n


def _freeze(fn):
    import types
    if getattr(fn, "__closure__", None) is None:
        return fn
    cells = []
    for c in fn.__closure__:
        try:
            cells.append(types.CellType(c.cell_contents))
        except ValueError:
            cells.append(c)
    out = types.FunctionType(fn.__code__, fn.__globals__, fn.__name__,
                             fn.__defaults__, tuple(cells))
    out.__kwdefaults__ = fn.__kwdefaults__
    return out


class FakeBlock:

    def tensor(self, f):
        f(None)

    def vector(self, f):
        f(None)

    def scalar(self, f):
        f(None)

    def gpsimd(self, f):
        f(None)

    def sync(self, f):
        f(None)


class Prog:

    def __init__(self, nc, stack):
        self.nc = nc
        self.stack = stack
        self.sems = {e: Sem(stack.enter_context(nc.semaphore("sem_" + e))) for e in ENGS}
        self.dma_sems = {}
        self.writer = {}
        self.readers = {}
        self.hwm = {e: {} for e in ENGS}
        self.thunks = {e: [] for e in ENGS}

    def dma_sem(self, key):
        if key not in self.dma_sems:
            self.dma_sems[key] = Sem(self.stack.enter_context(
                self.nc.semaphore("dma_" + str(key))))
        return self.dma_sems[key]

    def op(self, ek, eng, emit_fn, reads=(), writes=(), inc=1, dma=None):
        need = {}
        for tok in reads:
            w = self.writer.get(tok)
            if w is not None:
                sem, m, _ = w
                need[sem] = max(need.get(sem, 0), m)
        for tok in writes:
            w = self.writer.get(tok)
            if w is not None:
                sem, m, wek = w
                if not (ek == "t" and wek == "t"):
                    need[sem] = max(need.get(sem, 0), m)
            for sem, m in self.readers.get(tok, {}).items():
                need[sem] = max(need.get(sem, 0), m)
        waits = []
        for sem, m in need.items():
            if ek == "t" and sem is self.sems["t"]:
                continue
            if m > self.hwm[ek].get(sem, 0):
                waits.append((sem.h, m))
                self.hwm[ek][sem] = m
        my_sem = self.dma_sem(dma) if dma is not None else self.sems[ek]
        my_sem.n += inc
        mil = my_sem.n
        self.thunks[ek].append((waits, _freeze(emit_fn), my_sem.h, inc))
        for tok in writes:
            self.writer[tok] = (my_sem, mil, ek)
            self.readers[tok] = {}
        for tok in reads:
            d = self.readers.setdefault(tok, {})
            d[my_sem] = max(d.get(my_sem, 0), mil)
        return mil

    def barrier(self, block=None):
        snap = {sem: sem.n for sem in
                list(self.sems.values()) + list(self.dma_sems.values())}
        for ek in ENGS:
            waits = []
            for sem, n in snap.items():
                if sem is self.sems[ek]:
                    continue
                if n > self.hwm[ek].get(sem, 0):
                    waits.append((sem.h, n))
                    self.hwm[ek][sem] = n
            if waits:
                self.thunks[ek].append((waits, None, None, 0))

    def final_wait(self, eng=None, ek="q"):
        waits = []
        for sem in list(self.sems.values()) + list(self.dma_sems.values()):
            if sem is self.sems[ek]:
                continue
            if sem.n > self.hwm[ek].get(sem, 0):
                waits.append((sem.h, sem.n))
        if self.sems[ek].n > 0:
            waits.append((self.sems[ek].h, self.sems[ek].n))
        self.thunks[ek].append((waits, None, None, 0))

    def flush(self, real_block):
        meths = {"t": real_block.tensor, "v": real_block.vector,
                 "s": real_block.scalar, "g": real_block.gpsimd,
                 "q": real_block.sync}
        for ek in ENGS:
            def body(eng, ek=ek):
                for waits, emit_fn, sem_h, inc in self.thunks[ek]:
                    for h, m in waits:
                        eng.wait_ge(h, m)
                    if emit_fn is not None:
                        ins = emit_fn(eng)
                        ins.then_inc(sem_h, inc)
            meths[ek](body)
            self.thunks[ek] = []


def _chunked(w_hbm):
    return w_hbm.rearrange("(kc kp) n -> kp kc n", kp=128)


def _tok_tiles(dram_2d, t0, n_tt):
    return dram_2d[t0:t0 + 128 * n_tt, :].rearrange("(tt p) d -> p tt d", p=128)


def ln_stats_tm(p, block, x_ap_fn, x_toks, stat, cols, sq_scr, ntt, eps_col,
                eps_tok, width=D, stok=("st",), sqtok="sq_scr"):
    if not isinstance(stok, tuple):
        stok = (stok,)
    ST = lambda c0, k: stok + (c0, k)
    @block.vector
    def _(v):
        for tt in range(ntt):
            c0 = cols[tt]
            p.op("v", v, lambda e, tt=tt, c0=c0: e.reduce_sum(
                stat[:, c0 + 0: c0 + 1], x_ap_fn(tt), axis=AX.X),
                reads=[x_toks[tt]], writes=[ST(c0, 0)])
            p.op("v", v, lambda e, c0=c0: e.tensor_scalar_mul(
                stat[:, c0 + 1: c0 + 2], stat[:, c0 + 0: c0 + 1], 1.0 / width),
                reads=[ST(c0, 0)], writes=[ST(c0, 1)])

    @block.scalar
    def _(s):
        for tt in range(ntt):
            c0 = cols[tt]
            p.op("s", s, lambda e, tt=tt, c0=c0: e.activation(
                sq_scr[:, 0:width], x_ap_fn(tt), AF.Square,
                accum_out=stat[:, c0 + 2: c0 + 3]),
                reads=[x_toks[tt]], writes=[sqtok, ST(c0, 2)])

    @block.vector
    def _(v):
        for tt in range(ntt):
            c0 = cols[tt]
            p.op("v", v, lambda e, c0=c0: e.tensor_tensor(
                stat[:, c0 + 3: c0 + 4], stat[:, c0 + 1: c0 + 2],
                stat[:, c0 + 1: c0 + 2], ALU.mult),
                reads=[ST(c0, 1)], writes=[ST(c0, 3)])
            p.op("v", v, lambda e, c0=c0: e.tensor_scalar_mul(
                stat[:, c0 + 4: c0 + 5], stat[:, c0 + 2: c0 + 3], 1.0 / width),
                reads=[ST(c0, 2)], writes=[ST(c0, 4)])
            p.op("v", v, lambda e, c0=c0: e.tensor_tensor(
                stat[:, c0 + 5: c0 + 6], stat[:, c0 + 4: c0 + 5],
                stat[:, c0 + 3: c0 + 4], ALU.subtract),
                reads=[ST(c0, 4), ST(c0, 3)], writes=[ST(c0, 5)])

    @block.scalar
    def _(s):
        for tt in range(ntt):
            c0 = cols[tt]
            p.op("s", s, lambda e, c0=c0: e.activation(
                stat[:, c0 + 6: c0 + 7], stat[:, c0 + 5: c0 + 6], AF.Ln,
                bias=eps_col[:, 0:1]),
                reads=[ST(c0, 5), eps_tok], writes=[ST(c0, 6)])
            p.op("s", s, lambda e, c0=c0: e.activation(
                stat[:, c0 + 7: c0 + 8], stat[:, c0 + 6: c0 + 7], AF.Exp, scale=-0.5),
                reads=[ST(c0, 6)], writes=[ST(c0, 7)])


def _std_wrap(decl_fn, emit_fn, T_, SEQ_, **kw):
    nc = bass.Bass("TRN2", target_bir_lowering=False, debug=False)
    aps = decl_fn(nc, "", T_, SEQ_)
    with ExitStack() as st_:
        p = Prog(nc, st_)
        block = FakeBlock()
        with ExitStack() as sst:
            emit_fn(nc, p, block, sst, "", aps, T_, SEQ_, **kw)
        p.final_wait()
        real_block = st_.enter_context(nc.Block())
        p.flush(real_block)
    return nc


def decl_stage_a(nc, pfx, T_, SEQ_=None, x_in=None, x_out=None):
    d = lambda n, s, dt=F32, kind="ExternalInput": nc.dram_tensor(pfx + n, s, dt, kind=kind).ap()
    return {
        "x_in": x_in if x_in is not None else d("x_in", [T_, D]), "w_q": d("w_q", [D, D]), "w_k": d("w_k", [D, D]),
        "w_v": d("w_v", [D, D]), "w_c": d("w_c", [D, D]),
        "ident": d("ident", [128, 128], BF16),
        "x_out": x_out if x_out is not None else d("x_out", [T_, D], F32, "ExternalOutput"),
    }


def emit_stage_a(nc, p, block, st_, pfx, A, T_=T, SEQ_=SEQ):
    x_in, w_q, w_k, w_v, w_c, ident, x_out = (A["x_in"], A["w_q"], A["w_k"], A["w_v"],
                                              A["w_c"], A["ident"], A["x_out"])
    CH = 256
    NCH = T_ // CH
    CPS = SEQ_ // CH
    NTT = CH // 128

    if True:
        sb = lambda n, s, dt=F32: st_.enter_context(nc.sbuf_tensor(pfx + n, s, dt))
        ps = lambda n, s, dt=F32: st_.enter_context(nc.psum_tensor(pfx + n, s, dt))
        wq = sb("wq", [128, DC, D], BF16)
        wk = sb("wk", [128, DC, D], BF16)
        wv = sb("wv", [128, DC, D], BF16)
        wc = sb("wc", [128, DC, D], BF16)
        idn = sb("idn", [128, 128], BF16)
        x_tm = sb("x_tm", [128, 2, NTT, D])
        out_tm = sb("out_tm", [128, 2, NTT, D])
        xn_tm = sb("xn_tm", [128, 2, NTT, D], BF16)
        xn_fm = sb("xn_fm", [128, 2, DC, CH], BF16)
        v_fm = sb("v_fm", [128, 2, DC, CH])
        k_fm = sb("k_fm", [128, 2, DC, CH])
        kv_fm = sb("kv_fm", [128, DC, CH])
        kc_fm = sb("kc_fm", [128, DC, CH])
        qa = sb("qa", [128, 2, DC, CH])
        qb = sb("qb", [128, DC, CH])
        o_fm = sb("o_fm", [128, 2, DC, CH], BF16)
        sq_scr = sb("sq_scr", [128, 2, D])
        zero_col = sb("zero_col", [128, 1])
        col_eps5 = sb("col_eps5", [128, 1])
        col_eps6 = sb("col_eps6", [128, 1])
        col_one = sb("col_one", [128, 1])
        carry_k = sb("carry_k", [128, DC])
        carry_kv = sb("carry_kv", [128, DC])
        stat = sb("stat", [128, 2, 8 * NTT])
        ysum = sb("ysum", [128, 2, NTT, 2])
        ysq = sb("ysq", [128, 2, NTT, 2])
        ptr = [ps("ptr0", [128, 4, 128], BF16), ps("ptr1", [128, 4, 128], BF16)]
        pa = [ps("pa0", [128, CH]), ps("pa1", [128, CH])]
        py = [ps("py0", [128, 512]), ps("py1", [128, 512])]

        @block.gpsimd
        def _(g):
            for w_hbm, w_sb, tok in ((w_q, wq, "wq"), (w_k, wk, "wk"),
                                     (w_v, wv, "wv"), (w_c, wc, "wc")):
                p.op("g", g, lambda e, a=w_hbm, b=w_sb: e.dma_start(out=b[:], in_=_chunked(a)),
                     writes=[tok], inc=16, dma=tok)

        def load_chunk(sync, ci):
            p.op("q", sync, lambda e, ci=ci: e.dma_start(
                out=x_tm[:, ci % 2, :, :], in_=_tok_tiles(x_in, ci * CH, NTT)),
                writes=[("x", ci % 2)], inc=16, dma="x%d" % (ci % 2))

        @block.sync
        def _(sync):
            p.op("q", sync, lambda e: e.dma_start(out=idn[:], in_=ident[:, :]),
                 writes=["idn"], inc=16, dma="idn")
            load_chunk(sync, 0)
            load_chunk(sync, 1)

        @block.vector
        def _(v):
            p.op("v", v, lambda e: e.memset(zero_col[:], 0.0), writes=["zero"])
            p.op("v", v, lambda e: e.memset(col_eps5[:], 1e-5), writes=["eps5"])
            p.op("v", v, lambda e: e.memset(col_eps6[:], 1e-6), writes=["eps6"])
            p.op("v", v, lambda e: e.memset(col_one[:], 1.0), writes=["one"])

        ev_par = [0]
        pa_i = [0]
        py_i = [0]

        def tr_pack4(srcs, dst_view, dtoks, rtoks):
            par = ev_par[0] % 2
            ev_par[0] += 1
            n = len(srcs)

            @block.tensor
            def _(t, srcs=srcs, par=par):
                def mmg(e):
                    for i2, src in enumerate(srcs):
                        ins = e.transpose(ptr[par][:, i2, :], src, idn[:])
                    return ins
                p.op("t", t, mmg, reads=rtoks + ["idn"], writes=[("ptr", par)])

            @block.scalar
            def _(s, dst_view=dst_view, par=par, n=n):
                p.op("s", s, lambda e: e.copy(dst_view, ptr[par][:, 0:n, :]),
                     reads=[("ptr", par)], writes=dtoks)

        def front(ci):
            cb = ci % 2
            xtok = ("x", cb)

            ln_stats_tm(p, block, lambda tt, cb=cb: x_tm[:, cb, tt, :], [xtok] * NTT,
                        stat[:, cb, :], [0, 8], sq_scr[:, cb, :], NTT, col_eps5, "eps5",
                        stok=("st", cb), sqtok=("sq", cb))

            @block.vector
            def _(v, cb=cb):
                for tt in range(NTT):
                    c0 = tt * 8
                    p.op("v", v, lambda e, tt=tt, c0=c0: e.tensor_scalar(
                        xn_tm[:, cb, tt, :], x_tm[:, cb, tt, :],
                        stat[:, cb, c0 + 1: c0 + 2], stat[:, cb, c0 + 7: c0 + 8],
                        ALU.subtract, ALU.mult),
                        reads=[xtok, ("st", cb, c0, 1), ("st", cb, c0, 7)],
                        writes=[("xn_tm", cb, tt)])

            for tt in range(NTT):
                for cg in range(2):
                    srcs = [xn_tm[:, cb, tt, (cg * 4 + i2) * 128:(cg * 4 + i2 + 1) * 128]
                            for i2 in range(4)]
                    tr_pack4(srcs,
                             xn_fm[:, cb, cg * 4:(cg + 1) * 4, tt * 128:(tt + 1) * 128],
                             [("xn_fm", cb)], [("xn_tm", cb, tt)])

            for pj, wsb, wtok, dst, dtok in (("v", wv, "wv", v_fm, "v_fm"),
                                             ("k", wk, "wk", k_fm, "k_fm"),
                                             ("q", wq, "wq", qa, "qa")):
                for oc in range(DC):
                    par = pa_i[0] % 2
                    pa_i[0] += 1

                    @block.tensor
                    def _(t, cb=cb, wsb=wsb, wtok=wtok, oc=oc, par=par):
                        def mmg(e):
                            for kc in range(DC):
                                ins = e.matmul(pa[par][:, :], wsb[:, kc, oc * 128:(oc + 1) * 128],
                                               xn_fm[:, cb, kc, :], start=(kc == 0), stop=(kc == DC - 1))
                            return ins
                        p.op("t", t, mmg, reads=[("xn_fm", cb), wtok], writes=[("pa", par)])

                    @block.scalar
                    def _(s, cb=cb, pj=pj, dst=dst, dtok=dtok, oc=oc, par=par):
                        if pj == "v":
                            fn = lambda e: e.copy(dst[:, cb, oc, :], pa[par][:, :])
                        elif pj == "k":
                            fn = lambda e: e.activation(dst[:, cb, oc, :], pa[par][:, :], AF.Exp)
                        else:
                            fn = lambda e: e.activation(dst[:, cb, oc, :], pa[par][:, :], AF.Exp, scale=-1.0)
                        p.op("s", s, fn, reads=[("pa", par)], writes=[(dtok, cb, oc)])

        def back(ci):
            cb = ci % 2
            first_in_seq = (ci % CPS == 0)
            xtok = ("x", cb)

            @block.scalar
            def _(s, cb=cb):
                p.op("s", s, lambda e: e.activation(qb[:, :, :], qa[:, cb, :, :], AF.Ln,
                                                    bias=col_one[:, 0:1]),
                     reads=[("qa", cb, oc) for oc in range(DC)] + ["one"], writes=["qb"])
                p.op("s", s, lambda e: e.activation(qa[:, cb, :, :], qb[:, :, :], AF.Exp, scale=-1.0),
                     reads=["qb"], writes=[("sig", cb)])

            @block.vector
            def _(v, cb=cb, first_in_seq=first_in_seq):
                p.op("v", v, lambda e: e.tensor_tensor(kv_fm[:, :, :], k_fm[:, cb, :, :],
                                                       v_fm[:, cb, :, :], ALU.mult),
                     reads=[("k_fm", cb, oc) for oc in range(DC)] +
                           [("v_fm", cb, oc) for oc in range(DC)],
                     writes=["kv_fm"])
                for c in range(DC):
                    if first_in_seq:
                        init_k, init_kv = 0.0, 0.0
                        rk, rkv = [], []
                    else:
                        init_k = carry_k[:, c:c + 1]
                        init_kv = carry_kv[:, c:c + 1]
                        rk, rkv = [("ck", c)], [("ckv", c)]
                    p.op("v", v, lambda e, c=c, init_k=init_k: e.tensor_tensor_scan(
                        kc_fm[:, c, :], k_fm[:, cb, c, :],
                        zero_col[:, 0:1].broadcast_to([128, CH]), init_k, ALU.add, ALU.add),
                        reads=[("k_fm", cb, c), "zero"] + rk, writes=[("kc_fm", c)])
                    p.op("v", v, lambda e, c=c: e.tensor_copy(
                        carry_k[:, c:c + 1], kc_fm[:, c, CH - 1:CH]),
                        reads=[("kc_fm", c)], writes=[("ck", c)])
                    p.op("v", v, lambda e, c=c, init_kv=init_kv: e.tensor_tensor_scan(
                        k_fm[:, cb, c, :], kv_fm[:, c, :],
                        zero_col[:, 0:1].broadcast_to([128, CH]), init_kv, ALU.add, ALU.add),
                        reads=["kv_fm", "zero"] + rkv, writes=[("kvc", c), ("k_fm", cb, c)])
                    p.op("v", v, lambda e, c=c: e.tensor_copy(
                        carry_kv[:, c:c + 1], k_fm[:, cb, c, CH - 1:CH]),
                        reads=[("kvc", c)], writes=[("ckv", c)])

            @block.scalar
            def _(s, cb=cb):
                p.op("s", s, lambda e: e.activation(qb[:, :, :], kc_fm[:, :, :], AF.Ln,
                                                    bias=col_eps6[:, 0:1]),
                     reads=[("kc_fm", c) for c in range(DC)] + ["eps6", ("sig", cb)],
                     writes=["qb"])
                p.op("s", s, lambda e: e.activation(qb[:, :, :], qb[:, :, :], AF.Exp, scale=-1.0),
                     reads=["qb"], writes=["qb"])

            @block.vector
            def _(v, cb=cb):
                p.op("v", v, lambda e: e.tensor_tensor(v_fm[:, cb, :, :], k_fm[:, cb, :, :],
                                                       qb[:, :, :], ALU.mult),
                     reads=[("kvc", c) for c in range(DC)] + ["qb"],
                     writes=[("v_fm", cb, oc) for oc in range(DC)] + [("S", cb)])
                p.op("v", v, lambda e: e.tensor_tensor(o_fm[:, cb, :, :], v_fm[:, cb, :, :],
                                                       qa[:, cb, :, :], ALU.mult),
                     reads=[("S", cb), ("sig", cb)], writes=[("o_fm", cb)])

            for tt in range(NTT):
                for nb in range(2):
                    par = py_i[0] % 2
                    py_i[0] += 1

                    @block.tensor
                    def _(t, cb=cb, tt=tt, nb=nb, par=par):
                        def mmg(e):
                            for c in range(DC):
                                ins = e.matmul(py[par][:, :],
                                               o_fm[:, cb, c, tt * 128:(tt + 1) * 128],
                                               wc[:, c, nb * 512:(nb + 1) * 512],
                                               start=(c == 0), stop=(c == DC - 1))
                            return ins
                        p.op("t", t, mmg, reads=[("o_fm", cb), "wc"], writes=[("py", par)])

                    @block.scalar
                    def _(s, cb=cb, tt=tt, nb=nb, par=par):
                        p.op("s", s, lambda e: e.copy(
                            out_tm[:, cb, tt, nb * 512:(nb + 1) * 512], py[par][:, :]),
                            reads=[("py", par)], writes=[("out", cb, tt, nb)])
                        p.op("s", s, lambda e: e.activation(
                            sq_scr[:, cb, 0:512], py[par][:, :], AF.Square,
                            accum_out=ysq[:, cb, tt, nb:nb + 1]),
                            reads=[("py", par)], writes=[("sq", cb), ("ysq", cb, tt, nb)])

            @block.vector
            def _(v, cb=cb):
                for tt in range(NTT):
                    c0 = tt * 8
                    p.op("v", v, lambda e, tt=tt: e.reduce_sum(
                        ysum[:, cb, tt, 0:1], out_tm[:, cb, tt, 0:512], axis=AX.X),
                        reads=[("out", cb, tt, 0)], writes=[("ysum", cb, tt, 0)])
                    p.op("v", v, lambda e, tt=tt: e.reduce_sum(
                        ysum[:, cb, tt, 1:2], out_tm[:, cb, tt, 512:1024], axis=AX.X),
                        reads=[("out", cb, tt, 1)], writes=[("ysum", cb, tt, 1)])
                    p.op("v", v, lambda e, tt=tt, c0=c0: e.tensor_tensor(
                        stat[:, cb, c0 + 0: c0 + 1], ysum[:, cb, tt, 0:1], ysum[:, cb, tt, 1:2], ALU.add),
                        reads=[("ysum", cb, tt, 0), ("ysum", cb, tt, 1)], writes=[("st", cb, c0, 0)])
                    p.op("v", v, lambda e, tt=tt, c0=c0: e.tensor_tensor(
                        stat[:, cb, c0 + 2: c0 + 3], ysq[:, cb, tt, 0:1], ysq[:, cb, tt, 1:2], ALU.add),
                        reads=[("ysq", cb, tt, 0), ("ysq", cb, tt, 1)], writes=[("st", cb, c0, 2)])
                    p.op("v", v, lambda e, c0=c0: e.tensor_scalar_mul(
                        stat[:, cb, c0 + 1: c0 + 2], stat[:, cb, c0 + 0: c0 + 1], 1.0 / D),
                        reads=[("st", cb, c0, 0)], writes=[("st", cb, c0, 1)])
                    p.op("v", v, lambda e, c0=c0: e.tensor_scalar_mul(
                        stat[:, cb, c0 + 4: c0 + 5], stat[:, cb, c0 + 2: c0 + 3], 1.0 / D),
                        reads=[("st", cb, c0, 2)], writes=[("st", cb, c0, 4)])
                    p.op("v", v, lambda e, c0=c0: e.tensor_tensor(
                        stat[:, cb, c0 + 3: c0 + 4], stat[:, cb, c0 + 1: c0 + 2],
                        stat[:, cb, c0 + 1: c0 + 2], ALU.mult),
                        reads=[("st", cb, c0, 1)], writes=[("st", cb, c0, 3)])
                    p.op("v", v, lambda e, c0=c0: e.tensor_tensor(
                        stat[:, cb, c0 + 5: c0 + 6], stat[:, cb, c0 + 4: c0 + 5],
                        stat[:, cb, c0 + 3: c0 + 4], ALU.subtract),
                        reads=[("st", cb, c0, 4), ("st", cb, c0, 3)], writes=[("st", cb, c0, 5)])

            @block.scalar
            def _(s, cb=cb):
                for tt in range(NTT):
                    c0 = tt * 8
                    p.op("s", s, lambda e, c0=c0: e.activation(
                        stat[:, cb, c0 + 6: c0 + 7], stat[:, cb, c0 + 5: c0 + 6], AF.Ln,
                        bias=col_eps5[:, 0:1]),
                        reads=[("st", cb, c0, 5), "eps5"], writes=[("st", cb, c0, 6)])
                    p.op("s", s, lambda e, c0=c0: e.activation(
                        stat[:, cb, c0 + 7: c0 + 8], stat[:, cb, c0 + 6: c0 + 7], AF.Exp, scale=-0.5),
                        reads=[("st", cb, c0, 6)], writes=[("st", cb, c0, 7)])

            @block.vector
            def _(v, cb=cb):
                for tt in range(NTT):
                    c0 = tt * 8
                    p.op("v", v, lambda e, tt=tt, c0=c0: e.tensor_scalar(
                        out_tm[:, cb, tt, :], out_tm[:, cb, tt, :],
                        stat[:, cb, c0 + 1: c0 + 2], stat[:, cb, c0 + 7: c0 + 8],
                        ALU.subtract, ALU.mult),
                        reads=[("out", cb, tt, 0), ("out", cb, tt, 1),
                               ("st", cb, c0, 1), ("st", cb, c0, 7)],
                        writes=[("out", cb, tt, 0), ("out", cb, tt, 1)])
                    p.op("v", v, lambda e, tt=tt: e.tensor_tensor(
                        out_tm[:, cb, tt, :], out_tm[:, cb, tt, :], x_tm[:, cb, tt, :], ALU.add),
                        reads=[("out", cb, tt, 0), ("out", cb, tt, 1), xtok],
                        writes=[("out", cb, tt, 0), ("out", cb, tt, 1)])

            @block.sync
            def _(sync, cb=cb, ci=ci):
                p.op("q", sync, lambda e, ci=ci: e.dma_start(
                    out=_tok_tiles(x_out, ci * CH, NTT), in_=out_tm[:, cb, :, :]),
                    reads=[("out", cb, tt, nb) for tt in range(NTT) for nb in range(2)],
                    writes=["x_out_hbm"], inc=16, dma="xout%d" % cb)
                if ci + 2 < NCH:
                    load_chunk(sync, ci + 2)

        front(0)
        for ci in range(NCH):
            if ci + 1 < NCH:
                front(ci + 1)
            back(ci)

def build_stage_a(T_=T, SEQ_=SEQ):
    return _std_wrap(decl_stage_a, emit_stage_a, T_, SEQ_)


def decl_stage_c(nc, pfx, T_, SEQ_=None, x_in=None, x_out=None):
    d = lambda n, s, dt=F32, kind="ExternalInput": nc.dram_tensor(pfx + n, s, dt, kind=kind).ap()
    return {
        "x_in": x_in if x_in is not None else d("x_in", [T_, D]), "w_q": d("w_q", [D, D]), "w_k": d("w_k", [D, D]),
        "w_v": d("w_v", [D, D]), "ident": d("ident", [128, 128], BF16),
        "identf": d("identf", [128, 128]), "arange": d("arange", [128, 1]),
        "x_out": x_out if x_out is not None else d("x_out", [T_, D], F32, "ExternalOutput"),
    }


def emit_stage_c(nc, p, block, st_, pfx, A, T_=T, SEQ_=SEQ):
    x_in, w_q, w_k, w_v, ident, identf, arange, x_out = (
        A["x_in"], A["w_q"], A["w_k"], A["w_v"], A["ident"], A["identf"],
        A["arange"], A["x_out"])
    CH = 256
    NCH = T_ // CH
    CPS = SEQ_ // CH
    NTT = CH // 128
    NH = 16

    if True:
        sb = lambda n, s, dt=F32: st_.enter_context(nc.sbuf_tensor(pfx + n, s, dt))
        ps = lambda n, s, dt=F32: st_.enter_context(nc.psum_tensor(pfx + n, s, dt))
        wq = sb("wq", [128, DC, D], BF16)
        wk = sb("wk", [128, DC, D], BF16)
        wv = sb("wv", [128, DC, D], BF16)
        idn = sb("idn", [128, 128], BF16)
        idnf = sb("idnf", [128, 128], F32)
        arangec = sb("arangec", [128, 1])
        x_tm = sb("x_tm", [128, 2, NTT, D])
        out_tm = sb("out_tm", [128, NTT, D])
        xn_tm = sb("xn_tm", [128, 2, NTT, D], BF16)
        xn_fm = sb("xn_fm", [128, 2, DC, CH], BF16)
        v_fm = sb("v_fm", [128, 2, DC, CH])
        vc_fm = sb("vc_fm", [128, DC, CH])
        v_tm = sb("v_tm", [128, NTT, D])
        vc_tm = sb("vc_tm", [128, NTT, D])
        q_tm = sb("q_tm", [128, 2, NTT, D])
        k_tm = sb("k_tm", [128, 2, NTT, D])
        att_tm = sb("att_tm", [128, NTT, D])
        sq_scr = sb("sq_scr", [128, 2, D])
        col_eps5 = sb("col_eps5", [128, 1])
        zero_col = sb("zero_col", [128, 1])
        carry_v = sb("carry_v", [128, DC])
        stat = sb("stat", [128, 2, 8 * NTT])
        sd = sb("sd", [128, NTT, NH])
        sm_ = sb("sm_", [128, NTT, NH])
        mm_ = sb("mm_", [128, NTT, NH])
        e_ = sb("e_", [128, NTT, NH])
        z_ = sb("z_", [128, NTT, NH])
        den_ = sb("den_", [128, NTT, NH])
        rden_ = sb("rden_", [128, NTT, NH])
        icol = sb("icol", [128, 2, NTT])
        ptr = [ps("ptr0", [128, 4, 128], BF16), ps("ptr1", [128, 4, 128], BF16)]
        ptrf = [ps("ptrf0", [128, 128], F32), ps("ptrf1", [128, 128], F32)]
        pa = [ps("pa0", [128, CH]), ps("pa1", [128, CH])]
        py = [ps("py0", [128, 512]), ps("py1", [128, 512])]

        @block.gpsimd
        def _(g):
            for w_hbm, w_sb, tok in ((w_q, wq, "wq"), (w_k, wk, "wk"), (w_v, wv, "wv")):
                p.op("g", g, lambda e, a=w_hbm, b=w_sb: e.dma_start(out=b[:], in_=_chunked(a)),
                     writes=[tok], inc=16, dma=tok)

        def load_chunk(sync, ci):
            p.op("q", sync, lambda e, ci=ci: e.dma_start(
                out=x_tm[:, ci % 2, :, :], in_=_tok_tiles(x_in, ci * CH, NTT)),
                writes=[("x", ci % 2)], inc=16, dma="x%d" % (ci % 2))

        @block.sync
        def _(sync):
            p.op("q", sync, lambda e: e.dma_start(out=idn[:], in_=ident[:, :]),
                 writes=["idn"], inc=16, dma="idn")
            p.op("q", sync, lambda e: e.dma_start(out=idnf[:], in_=identf[:, :]),
                 writes=["idnf"], inc=16, dma="idnf")
            p.op("q", sync, lambda e: e.dma_start(out=arangec[:], in_=arange[:, :]),
                 writes=["arange"], inc=16, dma="arange")
            load_chunk(sync, 0)
            load_chunk(sync, 1)

        @block.vector
        def _(v):
            p.op("v", v, lambda e: e.memset(col_eps5[:], 1e-5), writes=["eps5"])
            p.op("v", v, lambda e: e.memset(zero_col[:], 0.0), writes=["zero"])

        ev_par = [0]
        evf_par = [0]
        pa_i = [0]
        py_i = [0]

        def tr_pack4(srcs, dst_view, dtoks, rtoks):
            par = ev_par[0] % 2
            ev_par[0] += 1
            n = len(srcs)

            @block.tensor
            def _(t, srcs=srcs, par=par):
                def mmg(e):
                    for i2, src in enumerate(srcs):
                        ins = e.transpose(ptr[par][:, i2, :], src, idn[:])
                    return ins
                p.op("t", t, mmg, reads=rtoks + ["idn"], writes=[("ptr", par)])

            @block.scalar
            def _(s, dst_view=dst_view, par=par, n=n):
                p.op("s", s, lambda e: e.copy(dst_view, ptr[par][:, 0:n, :]),
                     reads=[("ptr", par)], writes=dtoks)

        def front(ci):
            cb = ci % 2
            xtok = ("x", cb)
            seq_base = (ci % CPS) * CH

            ln_stats_tm(p, block, lambda tt, cb=cb: x_tm[:, cb, tt, :], [xtok] * NTT,
                        stat[:, cb, :], [0, 8], sq_scr[:, cb, :], NTT, col_eps5, "eps5",
                        stok=("st", cb), sqtok=("sq", cb))

            @block.vector
            def _(v, cb=cb, seq_base=seq_base):
                for tt in range(NTT):
                    c0 = tt * 8
                    p.op("v", v, lambda e, tt=tt, c0=c0: e.tensor_scalar(
                        xn_tm[:, cb, tt, :], x_tm[:, cb, tt, :],
                        stat[:, cb, c0 + 1: c0 + 2], stat[:, cb, c0 + 7: c0 + 8],
                        ALU.subtract, ALU.mult),
                        reads=[xtok, ("st", cb, c0, 1), ("st", cb, c0, 7)],
                        writes=[("xn_tm", cb, tt)])
                    p.op("v", v, lambda e, tt=tt: e.tensor_scalar_add(
                        icol[:, cb, tt:tt + 1], arangec[:, 0:1], float(seq_base + tt * 128)),
                        reads=["arange"], writes=[("icol", cb, tt)])

            for tt in range(NTT):
                for cg in range(2):
                    srcs = [xn_tm[:, cb, tt, (cg * 4 + i2) * 128:(cg * 4 + i2 + 1) * 128]
                            for i2 in range(4)]
                    tr_pack4(srcs,
                             xn_fm[:, cb, cg * 4:(cg + 1) * 4, tt * 128:(tt + 1) * 128],
                             [("xn_fm", cb)], [("xn_tm", cb, tt)])

            for oc in range(DC):
                par = pa_i[0] % 2
                pa_i[0] += 1

                @block.tensor
                def _(t, cb=cb, oc=oc, par=par):
                    def mmg(e):
                        for kc in range(DC):
                            ins = e.matmul(pa[par][:, :], wv[:, kc, oc * 128:(oc + 1) * 128],
                                           xn_fm[:, cb, kc, :], start=(kc == 0), stop=(kc == DC - 1))
                        return ins
                    p.op("t", t, mmg, reads=[("xn_fm", cb), "wv"], writes=[("pa", par)])

                @block.scalar
                def _(s, cb=cb, oc=oc, par=par):
                    p.op("s", s, lambda e: e.copy(v_fm[:, cb, oc, :], pa[par][:, :]),
                         reads=[("pa", par)], writes=[("v_fm", cb, oc)])

            for pj, wsb, wtok, dst, dtok in (("q", wq, "wq", q_tm, "q_tm"),
                                             ("k", wk, "wk", k_tm, "k_tm")):
                for tt in range(NTT):
                    for nb in range(2):
                        par = py_i[0] % 2
                        py_i[0] += 1

                        @block.tensor
                        def _(t, cb=cb, wsb=wsb, wtok=wtok, tt=tt, nb=nb, par=par):
                            def mmg(e):
                                for c in range(DC):
                                    ins = e.matmul(py[par][:, :],
                                                   xn_fm[:, cb, c, tt * 128:(tt + 1) * 128],
                                                   wsb[:, c, nb * 512:(nb + 1) * 512],
                                                   start=(c == 0), stop=(c == DC - 1))
                                return ins
                            p.op("t", t, mmg, reads=[("xn_fm", cb), wtok], writes=[("py", par)])

                        @block.scalar
                        def _(s, cb=cb, dst=dst, dtok=dtok, tt=tt, nb=nb, par=par):
                            p.op("s", s, lambda e: e.copy(
                                dst[:, cb, tt, nb * 512:(nb + 1) * 512], py[par][:, :]),
                                reads=[("py", par)], writes=[(dtok, cb, tt)])

        def back(ci):
            cb = ci % 2
            xtok = ("x", cb)
            first_in_seq = (ci % CPS == 0)

            @block.vector
            def _(v, cb=cb):
                for tt in range(NTT):
                    p.op("v", v, lambda e, tt=tt: e.tensor_tensor(
                        att_tm[:, tt, :], q_tm[:, cb, tt, :], k_tm[:, cb, tt, :], ALU.mult),
                        reads=[("q_tm", cb, tt), ("k_tm", cb, tt)], writes=[("att", tt)])
                    p.op("v", v, lambda e, tt=tt: e.tensor_reduce(
                        sd[:, tt, :], att_tm[:, tt, :].rearrange("p (h d) -> p h d", d=64),
                        AX.X, ALU.add),
                        reads=[("att", tt)], writes=[("sd", tt)])
                    p.op("v", v, lambda e, tt=tt: e.tensor_scalar_mul(
                        sm_[:, tt, :], sd[:, tt, :], 0.125),
                        reads=[("sd", tt)], writes=[("sm", tt)])
                    p.op("v", v, lambda e, tt=tt: e.tensor_scalar_max(
                        mm_[:, tt, :], sm_[:, tt, :], 0.0),
                        reads=[("sm", tt)], writes=[("m", tt)])
                    p.op("v", v, lambda e, tt=tt: e.tensor_tensor(
                        e_[:, tt, :], sm_[:, tt, :], mm_[:, tt, :], ALU.subtract),
                        reads=[("sm", tt), ("m", tt)], writes=[("e", tt)])

            @block.scalar
            def _(s):
                for tt in range(NTT):
                    p.op("s", s, lambda e, tt=tt: e.activation(
                        e_[:, tt, :], e_[:, tt, :], AF.Exp),
                        reads=[("e", tt)], writes=[("e", tt)])
                    p.op("s", s, lambda e, tt=tt: e.activation(
                        z_[:, tt, :], mm_[:, tt, :], AF.Exp, scale=-1.0),
                        reads=[("m", tt)], writes=[("z", tt)])

            @block.vector
            def _(v, cb=cb):
                for tt in range(NTT):
                    p.op("v", v, lambda e, tt=tt, cb=cb: e.scalar_tensor_tensor(
                        den_[:, tt, :], z_[:, tt, :], icol[:, cb, tt:tt + 1], e_[:, tt, :],
                        ALU.mult, ALU.add),
                        reads=[("z", tt), ("icol", cb, tt), ("e", tt)], writes=[("den", tt)])

            @block.scalar
            def _(s):
                for tt in range(NTT):
                    p.op("s", s, lambda e, tt=tt: e.activation(
                        rden_[:, tt, :], den_[:, tt, :], AF.Ln),
                        reads=[("den", tt)], writes=[("rden", tt)])
                    p.op("s", s, lambda e, tt=tt: e.activation(
                        rden_[:, tt, :], rden_[:, tt, :], AF.Exp, scale=-1.0),
                        reads=[("rden", tt)], writes=[("rden", tt)])

            @block.vector
            def _(v, cb=cb, first_in_seq=first_in_seq):
                for c in range(DC):
                    if first_in_seq:
                        init_v = 0.0
                        rv = []
                    else:
                        init_v = carry_v[:, c:c + 1]
                        rv = [("cv", c)]
                    p.op("v", v, lambda e, c=c, init_v=init_v: e.tensor_tensor_scan(
                        vc_fm[:, c, :], v_fm[:, cb, c, :],
                        zero_col[:, 0:1].broadcast_to([128, CH]), init_v, ALU.add, ALU.add),
                        reads=[("v_fm", cb, c), "zero"] + rv, writes=[("vc_fm", c)])
                    p.op("v", v, lambda e, c=c: e.tensor_copy(
                        carry_v[:, c:c + 1], vc_fm[:, c, CH - 1:CH]),
                        reads=[("vc_fm", c)], writes=[("cv", c)])

            for src3, stok3, dst, dtok in ((v_fm, "v_fm", v_tm, "v_tm"),
                                           (vc_fm, "vc_fm", vc_tm, "vc_tm")):
                for tt in range(NTT):
                    for c in range(DC):
                        par = evf_par[0] % 2
                        evf_par[0] += 1

                        @block.tensor
                        def _(t, cb=cb, src3=src3, stok3=stok3, tt=tt, c=c, par=par):
                            src_ap = (src3[:, cb, c, tt * 128:(tt + 1) * 128]
                                      if stok3 == "v_fm" else src3[:, c, tt * 128:(tt + 1) * 128])
                            rd = ("v_fm", cb, c) if stok3 == "v_fm" else ("vc_fm", c)
                            p.op("t", t, lambda e, src_ap=src_ap: e.transpose(
                                ptrf[par][:, :], src_ap, idnf[:]),
                                reads=[rd, "idnf"], writes=[("ptrf", par)])

                        @block.scalar
                        def _(s, dst=dst, dtok=dtok, tt=tt, c=c, par=par):
                            p.op("s", s, lambda e: e.copy(
                                dst[:, tt, c * 128:(c + 1) * 128], ptrf[par][:, :]),
                                reads=[("ptrf", par)], writes=[(dtok, tt)])

            @block.vector
            def _(v):
                for tt in range(NTT):
                    h3 = lambda ap: ap.unsqueeze(2).broadcast_to([128, NH, 64])
                    vv = lambda buf_, tt=tt: buf_[:, tt, :].rearrange("p (h d) -> p h d", d=64)
                    p.op("v", v, lambda e, tt=tt: e.tensor_tensor(
                        vc_tm[:, tt, :], vc_tm[:, tt, :], v_tm[:, tt, :], ALU.subtract),
                        reads=[("vc_tm", tt), ("v_tm", tt)], writes=[("vc_tm", tt)])
                    p.op("v", v, lambda e, tt=tt, h3=h3, vv=vv: e.tensor_tensor(
                        vv(vc_tm), vv(vc_tm), h3(z_[:, tt, :]), ALU.mult),
                        reads=[("vc_tm", tt), ("z", tt)], writes=[("vc_tm", tt)])
                    p.op("v", v, lambda e, tt=tt, h3=h3, vv=vv: e.tensor_tensor(
                        vv(v_tm), vv(v_tm), h3(e_[:, tt, :]), ALU.mult),
                        reads=[("v_tm", tt), ("e", tt)], writes=[("v_tm", tt)])
                    p.op("v", v, lambda e, tt=tt: e.tensor_tensor(
                        att_tm[:, tt, :], vc_tm[:, tt, :], v_tm[:, tt, :], ALU.add),
                        reads=[("vc_tm", tt), ("v_tm", tt)], writes=[("att", tt)])
                    p.op("v", v, lambda e, tt=tt, h3=h3, vv=vv: e.tensor_tensor(
                        vv(att_tm), vv(att_tm), h3(rden_[:, tt, :]), ALU.mult),
                        reads=[("att", tt), ("rden", tt)], writes=[("att", tt)])

            ln_stats_tm(p, block, lambda tt: att_tm[:, tt, :],
                        [("att", tt) for tt in range(NTT)],
                        stat[:, cb, :], [0, 8], sq_scr[:, cb, :], NTT, col_eps5, "eps5",
                        stok=("st", cb), sqtok=("sq", cb))

            @block.vector
            def _(v, cb=cb):
                for tt in range(NTT):
                    c0 = tt * 8
                    p.op("v", v, lambda e, tt=tt, c0=c0: e.tensor_scalar(
                        out_tm[:, tt, :], att_tm[:, tt, :],
                        stat[:, cb, c0 + 1: c0 + 2], stat[:, cb, c0 + 7: c0 + 8],
                        ALU.subtract, ALU.mult),
                        reads=[("att", tt), ("st", cb, c0, 1), ("st", cb, c0, 7)],
                        writes=[("out", tt)])
                    p.op("v", v, lambda e, tt=tt: e.tensor_tensor(
                        out_tm[:, tt, :], out_tm[:, tt, :], x_tm[:, cb, tt, :], ALU.add),
                        reads=[("out", tt), xtok], writes=[("out", tt)])

            @block.sync
            def _(sync, ci=ci):
                p.op("q", sync, lambda e, ci=ci: e.dma_start(
                    out=_tok_tiles(x_out, ci * CH, NTT), in_=out_tm[:, :, :]),
                    reads=[("out", tt) for tt in range(NTT)],
                    writes=["x_out_hbm"], inc=16, dma="xout")
                if ci + 2 < NCH:
                    load_chunk(sync, ci + 2)

        front(0)
        for ci in range(NCH):
            if ci + 1 < NCH:
                front(ci + 1)
            back(ci)


def build_stage_c(T_=T, SEQ_=SEQ):
    return _std_wrap(decl_stage_c, emit_stage_c, T_, SEQ_)


def decl_stage_b(nc, pfx, T_, SEQ_, depth_=6, x_in=None, x_out=None):
    NT = SEQ_ // 128
    NB1 = 22
    d = lambda n, s, dt=F32, kind="ExternalInput": nc.dram_tensor(pfx + n, s, dt, kind=kind).ap()
    return {
        "x_in": x_in if x_in is not None else d("x_in", [T_, D]),
        "wqkv_in": d("wqkv_in", [depth_, D, 3072]),
        "wo_in": d("wo_in", [depth_, D, D]),
        "qs_in": d("qs_in", [depth_, 1, 64]),
        "ks_in": d("ks_in", [depth_, 1, 64]),
        "lng_in": d("lng_in", [depth_, DC, 128]),
        "lnb_in": d("lnb_in", [depth_, DC, 128]),
        "w1_in": d("w1_in", [depth_, D, 2 * NB1 * 128]),
        "b1_in": d("b1_in", [depth_, 2 * NB1, 128]),
        "w2_in": d("w2_in", [depth_, NB1 * 128, D]),
        "b2_in": d("b2_in", [depth_, 1, D]),
        "tab_cs": d("tab_cs", [128, NT, 64], BF16),
        "tab_ss": d("tab_ss", [128, NT, 64], BF16),
        "tab_ci": d("tab_ci", [128, NT, 64], BF16),
        "tab_si": d("tab_si", [128, NT, 64], BF16),
        "maskA": d("maskA", [128, 128], BF16),
        "maskC": d("maskC", [128, 128], BF16),
        "ident": d("ident", [128, 128], BF16),
        "identf": d("identf", [128, 128]),
        "onesf": d("onesf", [1, 128]),
        "onescb": d("onescb", [128, 1], BF16),
        "x_out": x_out if x_out is not None else d("x_out", [T_, D], F32, "ExternalOutput"),
    }


def emit_stage_b(nc, p, block, st_, pfx, A, T_=T, SEQ_=SEQ, depth_=6,
                 half_layers=None, debug_r=False):
    if half_layers is None:
        half_layers = 2 * depth_
    NT = SEQ_ // 128
    NW = SEQ_ // 256
    NSEQ = T_ // SEQ_
    NB1 = 22
    (x_in, wqkv_in, wo_in, qs_in, ks_in, lng_in, lnb_in, w1_in, b1_in, w2_in,
     b2_in, tab_cs_in, tab_ss_in, tab_ci_in, tab_si_in, maskA_in, maskC_in,
     ident, identf, onesf, onescb, x_out) = (
        A["x_in"], A["wqkv_in"], A["wo_in"], A["qs_in"], A["ks_in"], A["lng_in"],
        A["lnb_in"], A["w1_in"], A["b1_in"], A["w2_in"], A["b2_in"], A["tab_cs"],
        A["tab_ss"], A["tab_ci"], A["tab_si"], A["maskA"], A["maskC"], A["ident"],
        A["identf"], A["onesf"], A["onescb"], A["x_out"])

    if True:
        sb = lambda n, s, dt=F32: st_.enter_context(nc.sbuf_tensor(pfx + n, s, dt))
        r_tm = sb("r_tm", [128, NT, D])
        idn = sb("idn", [128, 128], BF16)
        idnf = sb("idnf", [128, 128])
        onesf_sb = sb("onesf_sb", [1, 128])
        onescb_sb = sb("onescb_sb", [128, 1], BF16)
        tabs = {k2: sb("tabsb_" + k2, [128, NT, 64], BF16) for k2 in ("cs", "ss", "ci", "si")}
        maskA = sb("mA", [128, 128], BF16)
        maskC = sb("mC", [128, 128], BF16)
        col_eps5 = sb("col_eps5", [128, 1])

        @block.sync
        def _(sync):
            for a, b2, tok in ((ident, idn, "idn"), (identf, idnf, "idnf"),
                               (onesf, onesf_sb, "onesf"),
                               (onescb, onescb_sb, "onescb"),
                               (maskA_in, maskA, "mA"), (maskC_in, maskC, "mC")):
                p.op("q", sync, lambda e, a=a, b2=b2: e.dma_start(out=b2[:], in_=a[:, :]),
                     writes=[tok], inc=16, dma=tok)
            for k2 in ("cs", "ss", "ci", "si"):
                p.op("q", sync, lambda e, k2=k2: e.dma_start(
                    out=tabs[k2][:], in_=[tab_cs_in, tab_ss_in, tab_ci_in, tab_si_in][
                        ["cs", "ss", "ci", "si"].index(k2)][:, :, :]),
                    writes=["tab" + k2], inc=16, dma="tab" + k2)

        @block.vector
        def _(v):
            p.op("v", v, lambda e: e.memset(col_eps5[:], 1e-5), writes=["eps5"])

        for sq in range(NSEQ):
            pro_stack = ExitStack()
            x_io = pro_stack.enter_context(nc.sbuf_tensor(pfx + "x_io_p%d" % sq, [128, 2, 2, D], F32))
            sq_scr = pro_stack.enter_context(nc.sbuf_tensor(pfx + "sqs_p%d" % sq, [128, D], F32))
            stat = pro_stack.enter_context(nc.sbuf_tensor(pfx + "stat_p%d" % sq, [128, 16], F32))

            def load_x(sync, sq, ci, key, x_io=x_io):
                p.op("q", sync, lambda e: e.dma_start(
                    out=x_io[:, ci % 2, :, :], in_=_tok_tiles(x_in, sq * SEQ_ + ci * 256, 2)),
                    writes=[(key, ci % 2)], inc=16, dma="%s%d" % (key, ci % 2))

            @block.sync
            def _(sync, sq=sq):
                load_x(sync, sq, 0, "xp")
                load_x(sync, sq, 1, "xp")

            for ci in range(NT // 2):
                buf = ci % 2

                ln_stats_tm(p, block, lambda tt, ci=ci, buf=buf, x_io=x_io: x_io[:, buf, tt, :],
                            [("xp", buf)] * 2, stat, [0, 8], sq_scr, 2, col_eps5, "eps5")

                @block.vector
                def _(v, ci=ci, buf=buf, x_io=x_io, stat=stat):
                    for tt in range(2):
                        c0 = tt * 8
                        tile = ci * 2 + tt
                        p.op("v", v, lambda e, tt=tt, c0=c0, tile=tile: e.tensor_scalar(
                            r_tm[:, tile, :], x_io[:, buf, tt, :],
                            stat[:, c0 + 1: c0 + 2], stat[:, c0 + 7: c0 + 8],
                            ALU.subtract, ALU.mult),
                            reads=[("xp", buf), ("st", c0, 1), ("st", c0, 7)],
                            writes=[("r", tile)])

                @block.sync
                def _(sync, sq=sq, ci=ci):
                    if ci + 2 < NT // 2:
                        load_x(sync, sq, ci + 2, "xp")

            p.barrier(block)
            pro_stack.close()

            for li in range(depth_):
                if 2 * li >= half_layers:
                    break
                _emit_attn_layer(nc, p, block, st_, li, sq, r_tm, idn, onesf_sb,
                                 tabs, maskA, maskC, wqkv_in, wo_in, qs_in, ks_in,
                                 NT, NW, pfx)
                if 2 * li + 1 >= half_layers:
                    break
                _emit_ffn_layer(nc, p, block, st_, li, sq, r_tm, idn, idnf, onesf_sb,
                                onescb_sb, col_eps5, lng_in, lnb_in, w1_in, b1_in,
                                w2_in, b2_in, NT, NB1, pfx)

            epi_stack = ExitStack()
            x_io = epi_stack.enter_context(nc.sbuf_tensor(pfx + "x_io_e%d" % sq, [128, 2, 2, D], F32))
            sq_scr = epi_stack.enter_context(nc.sbuf_tensor(pfx + "sqs_e%d" % sq, [128, D], F32))
            stat = epi_stack.enter_context(nc.sbuf_tensor(pfx + "stat_e%d" % sq, [128, 16], F32))

            def load_x(sync, sq, ci, key, x_io=x_io):
                p.op("q", sync, lambda e: e.dma_start(
                    out=x_io[:, ci % 2, :, :], in_=_tok_tiles(x_in, sq * SEQ_ + ci * 256, 2)),
                    writes=[(key, ci % 2)], inc=16, dma="%s%d" % (key, ci % 2))

            @block.sync
            def _(sync, sq=sq):
                load_x(sync, sq, 0, "xe")
                load_x(sync, sq, 1, "xe")

            for ci in range(NT // 2):
                buf = ci % 2

                if not debug_r:
                    ln_stats_tm(p, block, lambda tt, ci=ci: r_tm[:, ci * 2 + tt, :],
                                [("r", ci * 2 + tt) for tt in range(2)],
                                stat, [0, 8], sq_scr, 2, col_eps5, "eps5")

                @block.vector
                def _(v, sq=sq, ci=ci, buf=buf, x_io=x_io, stat=stat, sq_scr=sq_scr):
                    for tt in range(2):
                        c0 = tt * 8
                        tile = ci * 2 + tt
                        if debug_r:
                            p.op("v", v, lambda e, tt=tt, tile=tile: e.tensor_copy(
                                x_io[:, buf, tt, :], r_tm[:, tile, :]),
                                reads=[("r", tile), ("xe", buf)], writes=[("xe", buf)])
                        else:
                            p.op("v", v, lambda e, tt=tt, c0=c0, tile=tile: e.tensor_scalar(
                                sq_scr[:, :], r_tm[:, tile, :],
                                stat[:, c0 + 1: c0 + 2], stat[:, c0 + 7: c0 + 8],
                                ALU.subtract, ALU.mult),
                                reads=[("r", tile), ("st", c0, 1), ("st", c0, 7)],
                                writes=["sq_scr"])
                            p.op("v", v, lambda e, tt=tt: e.tensor_tensor(
                                x_io[:, buf, tt, :], x_io[:, buf, tt, :], sq_scr[:, :],
                                ALU.add),
                                reads=["sq_scr", ("xe", buf)], writes=[("xe", buf)])

                @block.sync
                def _(sync, sq=sq, ci=ci, buf=buf, x_io=x_io):
                    p.op("q", sync, lambda e: e.dma_start(
                        out=_tok_tiles(x_out, sq * SEQ_ + ci * 256, 2),
                        in_=x_io[:, buf, :, :]),
                        reads=[("xe", buf)], writes=["xoutb"], inc=16, dma="xoutb")
                    if ci + 2 < NT // 2:
                        load_x(sync, sq, ci + 2, "xe")

            p.barrier(block)
            epi_stack.close()


def build_stage_b(T_=T, SEQ_=SEQ, depth_=6, half_layers=None, debug_r=False):
    return _std_wrap(lambda nc, pfx, T2, S2: decl_stage_b(nc, pfx, T2, S2, depth_),
                     emit_stage_b, T_, SEQ_, depth_=depth_,
                     half_layers=half_layers, debug_r=debug_r)


def _emit_attn_layer(nc, p, block, st_, li, sq, r_tm, idn, onesf_sb, tabs,
                     maskA, maskC, wqkv_in, wo_in, qs_in, ks_in, NT, NW, pfx=""):
    tag = pfx + "_a%d_%d" % (li, sq)
    with ExitStack() as ph:
        sb = lambda n, s, dt=F32: ph.enter_context(nc.sbuf_tensor(n + tag, s, dt))
        ps = lambda n, s, dt=F32: ph.enter_context(nc.psum_tensor(n + tag, s, dt))
        wqkv_sb = sb("wqkv_sb", [128, DC, 3072], BF16)
        wo_sb = sb("wo_sb", [128, DC, D], BF16)
        qs_bc = sb("qs_bc", [128, 64])
        ks_bc = sb("ks_bc", [128, 64])
        qs_sg = sb("qs_sg", [128, 64])
        ks_sg = sb("ks_sg", [128, 64])
        qs_row = sb("qs_row", [1, 64])
        ks_row = sb("ks_row", [1, 64])
        xs_fm = sb("xs_fm", [128, 2, DC, 257], BF16)
        rcast = sb("rcast", [128, D], BF16)
        q_tm = sb("q_tm", [128, 2, D], BF16)
        k_tm = sb("k_tm", [128, 2, D], BF16)
        v_tm = sb("v_tm", [128, 3, 2, D], BF16)
        q_fm = sb("q_fm", [128, 2, DC, 256], BF16)
        k_fm = sb("k_fm", [128, 3, DC, 256], BF16)
        ao_fm = sb("ao_fm", [128, DC, 256], BF16)
        scr2 = sb("scr2", [128, D], BF16)
        scr3 = sb("scr3", [128, D], BF16)
        ss_t = sb("ss_t", [128, 16])
        rn_t = sb("rn_t", [128, 16])
        a1_t = sb("a1_t", [128, 64])
        a2_t = sb("a2_t", [128, 64])
        p_sb = sb("p_sb", [128, 6, 384], BF16)
        pT_sb = sb("pT_sb", [128, 6, 3, 128], BF16)
        sm1 = sb("sm1", [128, 6, 4])
        ptr = [ps("bptr0", [128, 4, 128], BF16), ps("bptr1", [128, 4, 128], BF16)]
        pq = [ps("bpq0", [128, 512])]
        psim = [ps("bpsim%d" % i2, [128, 384]) for i2 in range(4)]
        ppv = [ps("bppv0", [128, 2, 128])]

        @block.gpsimd
        def _(g):
            p.op("g", g, lambda e: e.dma_start(
                out=wqkv_sb[:], in_=_chunked(wqkv_in[li])), writes=["wqkv"], inc=16, dma="wqkv")
            p.op("g", g, lambda e: e.dma_start(
                out=wo_sb[:], in_=_chunked(wo_in[li])), writes=["wo"], inc=16, dma="wo")

        @block.sync
        def _(sync):
            p.op("q", sync, lambda e: e.dma_start(out=qs_row[:], in_=qs_in[li, :, :]),
                 writes=["qs_row"], inc=16, dma="qs_row")
            p.op("q", sync, lambda e: e.dma_start(out=ks_row[:], in_=ks_in[li, :, :]),
                 writes=["ks_row"], inc=16, dma="ks_row")

        col_ln8 = sb("col_ln8", [128, 1])

        @block.vector
        def _(v):
            p.op("v", v, lambda e: e.memset(col_ln8[:], 2.0794415416798357),
                 writes=["ln8"])

        for row, rtok, bc, btok, sg, sgtok in ((qs_row, "qs_row", qs_bc, "qs_bc", qs_sg, "qs_sg"),
                                               (ks_row, "ks_row", ks_bc, "ks_bc", ks_sg, "ks_sg")):
            @block.tensor
            def _(t, row=row, rtok=rtok):
                p.op("t", t, lambda e: e.matmul(pq[0][:, 0:64], onesf_sb[0:1, :], row[0:1, :],
                                                start=True, stop=True),
                     reads=["onesf", rtok], writes=[("bpq", 0)])

            @block.scalar
            def _(s, bc=bc, btok=btok):
                p.op("s", s, lambda e: e.copy(bc[:, :], pq[0][:, 0:64]),
                     reads=[("bpq", 0)], writes=[btok])

            @block.vector
            def _(v, bc=bc, btok=btok, sg=sg, sgtok=sgtok):
                p.op("v", v, lambda e: e.tensor_scalar_mul(sg[:, 0:32], bc[:, 32:64], -1.0),
                     reads=[btok], writes=[sgtok])
                p.op("v", v, lambda e: e.tensor_copy(sg[:, 32:64], bc[:, 0:32]),
                     reads=[btok], writes=[sgtok])

        tr_par = [0]
        pq_par = [0]
        pv_par = [0]
        unit_k = [0]

        def tr_pack4(srcs, dst_view, dtoks, rtoks):
            par = tr_par[0] % 2
            tr_par[0] += 1
            n = len(srcs)

            @block.tensor
            def _(t, srcs=srcs, par=par):
                def mmg(e):
                    for i2, src in enumerate(srcs):
                        ins = e.transpose(ptr[par][:, i2, :], src, idn[:])
                    return ins
                p.op("t", t, mmg, reads=rtoks + ["idn"], writes=[("bptr", par)])

            @block.scalar
            def _(s, dst_view=dst_view, par=par, n=n):
                p.op("s", s, lambda e: e.copy(dst_view, ptr[par][:, 0:n, :]),
                     reads=[("bptr", par)], writes=dtoks)

        def build_xs(w):
            wb = w % 2

            @block.vector
            def _(v, w=w, wb=wb):
                if w == 0:
                    p.op("v", v, lambda e: e.memset(xs_fm[:, wb, :, 0:1], 0.0),
                         reads=[("xs", w - 2)] if w >= 2 else [], writes=[("xscol", w)])
                else:
                    p.op("v", v, lambda e: e.tensor_copy(
                        xs_fm[:, wb, :, 0:1], xs_fm[:, 1 - wb, :, 256:257]),
                        reads=[("xs", w - 1)] + ([("xs", w - 2)] if w >= 2 else []),
                        writes=[("xscol", w)])

            for tt in range(2):
                tile = 2 * w + tt

                @block.vector
                def _(v, tile=tile):
                    p.op("v", v, lambda e: e.tensor_copy(rcast[:, :], r_tm[:, tile, :]),
                         reads=[("r", tile)], writes=["rcast"])

                for cg in range(2):
                    srcs = [rcast[:, (cg * 4 + i2) * 128:(cg * 4 + i2 + 1) * 128]
                            for i2 in range(4)]
                    tr_pack4(srcs,
                             xs_fm[:, wb, cg * 4:(cg + 1) * 4, 1 + tt * 128:1 + (tt + 1) * 128],
                             [("xs", w)], ["rcast"])

        def prep(w):
            wb = w % 2
            for o in range(6):
                for tt in range(2):
                    par = 0

                    @block.tensor
                    def _(t, w=w, wb=wb, o=o, tt=tt, par=par):
                        def mmg(e):
                            for c in range(DC):
                                off = 1 if c < 4 else 0
                                ins = e.matmul(pq[par][:, :],
                                               xs_fm[:, wb, c, off + tt * 128: off + (tt + 1) * 128],
                                               wqkv_sb[:, c, o * 512:(o + 1) * 512],
                                               start=(c == 0), stop=(c == DC - 1))
                            return ins
                        p.op("t", t, mmg, reads=[("xs", w), ("xscol", w), "wqkv"],
                             writes=[("bpq", par)])

                    @block.scalar
                    def _(s, w=w, wb=wb, o=o, tt=tt, par=par):
                        if o < 2:
                            p.op("s", s, lambda e: e.copy(
                                q_tm[:, tt, (o % 2) * 512:(o % 2 + 1) * 512], pq[par][:, :]),
                                reads=[("bpq", par)], writes=[("q_tm", tt)])
                        elif o < 4:
                            p.op("s", s, lambda e: e.copy(
                                k_tm[:, tt, (o % 2) * 512:(o % 2 + 1) * 512], pq[par][:, :]),
                                reads=[("bpq", par)], writes=[("k_tm", tt)])
                        else:
                            p.op("s", s, lambda e: e.copy(
                                v_tm[:, w % 3, tt, (o % 2) * 512:(o % 2 + 1) * 512], pq[par][:, :]),
                                reads=[("bpq", par)], writes=[("v", w % 3, tt)])

            for src, stok, bc, btok, sg, sgtok, tcs, tss in (
                    (q_tm, "q_tm", qs_bc, "qs_bc", qs_sg, "qs_sg", "cs", "ss"),
                    (k_tm, "k_tm", ks_bc, "ks_bc", ks_sg, "ks_sg", "ci", "si")):
                for tt in range(2):
                    tg = 2 * w + tt

                    @block.scalar
                    def _(s, src=src, stok=stok, tt=tt):
                        p.op("s", s, lambda e: e.activation(
                            scr2[:, :], src[:, tt, :], AF.Square),
                            reads=[(stok, tt)], writes=["scr2"])

                    @block.vector
                    def _(v, tt=tt):
                        p.op("v", v, lambda e: e.tensor_reduce(
                            ss_t[:, :], scr2[:, :].rearrange("p (h d) -> p h d", d=64),
                            AX.X, ALU.add),
                            reads=["scr2"], writes=["ss_t"])
                        p.op("v", v, lambda e: e.tensor_scalar_max(
                            ss_t[:, :], ss_t[:, :], 1e-24),
                            reads=["ss_t"], writes=["ss_t"])

                    @block.scalar
                    def _(s, stok=stok):
                        p.op("s", s, lambda e: e.activation(rn_t[:, :], ss_t[:, :], AF.Ln),
                             reads=["ss_t"], writes=["rn_t"])
                        if stok == "q_tm":
                            p.op("s", s, lambda e: e.activation(
                                rn_t[:, :], rn_t[:, :], AF.Exp, scale=-0.5,
                                bias=col_ln8[:, 0:1]),
                                reads=["rn_t", "ln8"], writes=["rn_t"])
                        else:
                            p.op("s", s, lambda e: e.activation(
                                rn_t[:, :], rn_t[:, :], AF.Exp, scale=-0.5),
                                reads=["rn_t"], writes=["rn_t"])

                    @block.vector
                    def _(v, tg=tg, bc=bc, btok=btok, sg=sg, sgtok=sgtok, tcs=tcs, tss=tss):
                        p.op("v", v, lambda e: e.tensor_tensor(
                            a1_t[:, :], tabs[tcs][:, tg, :], bc[:, :], ALU.mult),
                            reads=["tab" + tcs, btok], writes=["a1_t"])
                        p.op("v", v, lambda e: e.tensor_tensor(
                            a2_t[:, :], tabs[tss][:, tg, :], sg[:, :], ALU.mult),
                            reads=["tab" + tss, sgtok], writes=["a2_t"])

                    h64 = lambda buf_, ttt: buf_[:, ttt, :].rearrange("p (h d) -> p h d", d=64)

                    @block.gpsimd
                    def _(g, src=src, stok=stok, tt=tt, h64=h64):
                        p.op("g", g, lambda e: e.tensor_tensor(
                            scr2[:, :].rearrange("p (h d) -> p h d", d=64), h64(src, tt),
                            a1_t[:, :].unsqueeze(1).broadcast_to([128, 16, 64]), ALU.mult),
                            reads=[(stok, tt), "a1_t"], writes=["scr2"])
                        p.op("g", g, lambda e: e.tensor_tensor(
                            scr3[:, :].rearrange("p (h d) -> p h d", d=64)[:, :, 0:32],
                            h64(src, tt)[:, :, 32:64],
                            a2_t[:, 0:32].unsqueeze(1).broadcast_to([128, 16, 32]), ALU.mult),
                            reads=[(stok, tt), "a2_t"], writes=["scr3"])
                        p.op("g", g, lambda e: e.tensor_tensor(
                            scr3[:, :].rearrange("p (h d) -> p h d", d=64)[:, :, 32:64],
                            h64(src, tt)[:, :, 0:32],
                            a2_t[:, 32:64].unsqueeze(1).broadcast_to([128, 16, 32]), ALU.mult),
                            reads=[(stok, tt), "a2_t"], writes=["scr3"])
                        p.op("g", g, lambda e: e.tensor_tensor(
                            scr2[:, :], scr2[:, :], scr3[:, :], ALU.add),
                            reads=["scr2", "scr3"], writes=["scr2"])

                    @block.vector
                    def _(v, src=src, stok=stok, tt=tt):
                        p.op("v", v, lambda e: e.tensor_tensor(
                            src[:, tt, :].rearrange("p (h d) -> p h d", d=64),
                            scr2[:, :].rearrange("p (h d) -> p h d", d=64),
                            rn_t[:, :].unsqueeze(2).broadcast_to([128, 16, 64]), ALU.mult),
                            reads=["scr2", "rn_t", (stok, tt)],
                            writes=[(stok + "r", tt)])

        def prep_tr(w):
            wb = w % 2
            for src, stok, dstf, dtokf in ((q_tm, "q_tm", q_fm, "q_fm"),
                                           (k_tm, "k_tm", k_fm, "k_fm")):
                wi = wb if dtokf == "q_fm" else w % 3
                for tt in range(2):
                    for cg in range(2):
                        srcs = [src[:, tt, (cg * 4 + i2) * 128:(cg * 4 + i2 + 1) * 128]
                                for i2 in range(4)]
                        tr_pack4(srcs,
                                 dstf[:, wi, cg * 4:(cg + 1) * 4, tt * 128:(tt + 1) * 128],
                                 [(dtokf, wi)], [(stok + "r", tt)])

        def head_sims(w, u):
            wb = w % 2
            tt, h, subs, valid = u
            hp, hc = h % 2, h // 2
            k = unit_k[0]
            spar = k % 4

            @block.tensor
            def _(t, wb=wb, tt=tt, hp=hp, hc=hc, spar=spar, subs=subs):
                def mmg(e):
                    ins = None
                    for ks, s_ in enumerate(subs):
                        if s_ is None:
                            continue
                        kb2, kt = s_
                        m = [maskA, None, maskC][ks]
                        ins = e.matmul(
                            psim[spar][:, ks * 128:(ks + 1) * 128],
                            q_fm[hp * 64:(hp + 1) * 64, wb, hc, tt * 128:(tt + 1) * 128],
                            k_fm[hp * 64:(hp + 1) * 64, kb2, hc, kt * 128:(kt + 1) * 128],
                            start=True, stop=(m is None),
                            tile_position=(hp * 64, 0))
                        if m is not None:
                            ins = e.matmul(psim[spar][:, ks * 128:(ks + 1) * 128],
                                           idn[:, :], m[:, :], start=False, stop=True)
                    return ins
                p.op("t", t, mmg, reads=[("q_fm", wb), ("k_fm", 0), ("k_fm", 1),
                                         ("k_fm", 2), "mA", "mC", "idn"],
                     writes=[("psim", spar)])

        def head_stats(k, u):
            tt, h, subs, valid = u
            spar = k % 4
            pi = k % 6

            @block.vector
            def _(v, spar=spar, valid=valid, pi=pi):
                vs = slice(valid[0] * 128, 384)
                p.op("v", v, lambda e, vs=vs: e.reduce_max(
                    sm1[:, pi, 1:2], psim[spar][:, vs], axis=AX.X, negate=True),
                    reads=[("psim", spar)], writes=[("negm", pi)])

            @block.scalar
            def _(s, valid=valid, spar=spar, pi=pi):
                vs = slice(valid[0] * 128, 384)
                p.op("s", s, lambda e, vs=vs: e.activation(
                    p_sb[:, pi, vs], psim[spar][:, vs], AF.Exp,
                    bias=sm1[:, pi, 1:2], accum_out=sm1[:, pi, 2:3]),
                    reads=[("psim", spar), ("negm", pi)],
                    writes=[("p", pi), ("sumex", pi)])

        def head_out(w, k, u, gpos, gfirst, glast, hc0):
            tt, h, subs, valid = u
            hp, hc = h % 2, h // 2
            pi = k % 6

            @block.vector
            def _(v, valid=valid, pi=pi):
                vs = slice(valid[0] * 128, 384)
                p.op("v", v, lambda e: e.reciprocal(sm1[:, pi, 3:4], sm1[:, pi, 2:3]),
                     reads=[("sumex", pi)], writes=[("rsum", pi)])
                p.op("v", v, lambda e, vs=vs: e.tensor_scalar_mul(
                    p_sb[:, pi, vs], p_sb[:, pi, vs], sm1[:, pi, 3:4]),
                    reads=[("p", pi), ("rsum", pi)], writes=[("p", pi)])

            srcs = [p_sb[:, pi, ks * 128:(ks + 1) * 128] for ks in valid]
            par = tr_par[0] % 2
            tr_par[0] += 1

            @block.tensor
            def _(t, srcs=srcs, par=par):
                def mmg(e):
                    for i2, src in enumerate(srcs):
                        ins = e.transpose(ptr[par][:, i2, :], src, idn[:])
                    return ins
                p.op("t", t, mmg, reads=[("p", pi), "idn"], writes=[("bptr", par)])

            @block.scalar
            def _(s, par=par, pi=pi, valid=valid):
                p.op("s", s, lambda e: e.copy(
                    pT_sb[:, pi, 0:len(valid), :], ptr[par][:, 0:len(valid), :]),
                    reads=[("bptr", par)], writes=[("pT", pi)])

            if gfirst:
                pv_par[0] += 1
            pvp = 0
            slot = hc - hc0

            @block.tensor
            def _(t, h=h, hp=hp, slot=slot, subs=subs, valid=valid, pvp=pvp, pi=pi):
                def mmg(e):
                    first = True
                    for i2, ks in enumerate(valid):
                        kb2, kt = subs[ks]
                        ins = e.matmul(ppv[pvp][hp * 64:(hp + 1) * 64, slot, :],
                                       v_tm[:, kb2, kt, h * 64:(h + 1) * 64],
                                       pT_sb[:, pi, i2, :],
                                       start=first, stop=(i2 == len(valid) - 1),
                                       tile_position=(0, hp * 64))
                        first = False
                    return ins
                p.op("t", t, mmg,
                     reads=[("pT", pi)] +
                           [("v", i, j) for i in range(3) for j in range(2)],
                     writes=[("ppv", pvp)])

            if glast:
                @block.scalar
                def _(s, tt=tt, hc0=hc0, pvp=pvp):
                    p.op("s", s, lambda e: e.copy(
                        ao_fm[:, hc0:hc0 + 2, tt * 128:(tt + 1) * 128],
                        ppv[pvp][:, :, :]),
                        reads=[("ppv", pvp)], writes=["ao_fm"])

        def attn_units(w):
            p0 = w % 3
            pm1 = (w - 1) % 3
            units = []
            for tt in range(2):
                if tt == 0:
                    subs = [(pm1, 0), (pm1, 1), (p0, 0)]
                else:
                    subs = [(pm1, 1), (p0, 0), (p0, 1)]
                if w == 0:
                    subs = [None if s_[0] == pm1 else s_ for s_ in subs]
                valid = [i for i, s_ in enumerate(subs) if s_ is not None]
                for h in range(16):
                    units.append((tt, h, subs, valid))
            return units

        def wo_update(w):
            for tt in range(2):
                tile = 2 * w + tt
                for nb in range(2):
                    par = 0

                    @block.tensor
                    def _(t, tt=tt, nb=nb, par=par):
                        def mmg(e):
                            for c in range(DC):
                                ins = e.matmul(pq[par][:, :],
                                               ao_fm[:, c, tt * 128:(tt + 1) * 128],
                                               wo_sb[:, c, nb * 512:(nb + 1) * 512],
                                               start=(c == 0), stop=(c == DC - 1))
                            return ins
                        p.op("t", t, mmg, reads=["ao_fm", "wo"], writes=[("bpq", par)])

                    @block.vector
                    def _(v, tile=tile, nb=nb, par=par):
                        p.op("v", v, lambda e: e.tensor_tensor(
                            r_tm[:, tile, nb * 512:(nb + 1) * 512],
                            r_tm[:, tile, nb * 512:(nb + 1) * 512],
                            pq[par][:, :], ALU.add),
                            reads=[("bpq", par), ("r", tile)], writes=[("r", tile)])

        build_xs(0)
        prep(0)
        prep_tr(0)
        for w in range(NW):
            units = attn_units(w)
            n_u = len(units)
            k0 = unit_k[0]
            for j in range(n_u + 5):
                if j == 2 and w + 1 < NW:
                    build_xs(w + 1)
                    prep(w + 1)
                if j == n_u // 2 and w + 1 < NW:
                    prep_tr(w + 1)
                if j < n_u:
                    head_sims(w, units[j])
                    unit_k[0] += 1
                if 0 <= j - 5 < n_u:
                    jj = j - 5
                    gpos = jj % 4
                    hc0 = units[jj - gpos][1] // 2
                    head_out(w, k0 + jj, units[jj], gpos, gpos == 0, gpos == 3, hc0)
                if 0 <= j - 1 < n_u:
                    head_stats(k0 + j - 1, units[j - 1])
            wo_update(w)

        p.barrier(block)

def _emit_ffn_layer(nc, p, block, st_, li, sq, r_tm, idn, idnf, onesf_sb, onescb_sb,
                    col_eps5, lng_in, lnb_in, w1_in, b1_in, w2_in, b2_in, NT, NB1, pfx=""):
    NCF = NT // 4
    tag = pfx + "_f%d_%d" % (li, sq)
    with ExitStack() as ph:
        sb = lambda n, s, dt=F32: ph.enter_context(nc.sbuf_tensor(n + tag, s, dt))
        ps = lambda n, s, dt=F32: ph.enter_context(nc.psum_tensor(n + tag, s, dt))
        w2_sb = sb("w2_sb", [128, NB1, D], BF16)
        w1blk = sb("w1blk", [128, 2, DC, 256], BF16)
        act_fm = sb("act_fm", [128, NB1, 512], BF16)
        xs2_fm = sb("xs2_fm", [128, DC, 513], BF16)
        sqf = sb("sqf", [128, DC, 513], BF16)
        xn_fm = sb("xn_fm", [128, DC, 512], BF16)
        rcast = sb("rcast2", [128, D], BF16)
        gb_s = sb("gb_s", [128, 2, 512], BF16)
        m_bc = sb("m_bc", [128, 512])
        rs_bc = sb("rs_bc", [128, 512])
        b2_bc = sb("b2_bc", [128, D])
        scr_w = sb("scr_w", [128, 512])
        g_cols = sb("g_cols", [128, DC])
        b_cols = sb("b_cols", [128, DC])
        b1_cols = sb("b1_cols", [128, 2 * NB1])
        gtmp = sb("gtmp", [DC, 128])
        btmp = sb("btmp", [DC, 128])
        b1tmp = sb("b1tmp", [2 * NB1, 128])
        b2row = sb("b2row", [1, D])
        srow = sb("srow", [1, 2 * 512])
        rrow = sb("rrow", [1, 512])
        ptr2 = [ps("fptr0", [128, 4, 128], BF16)]
        pab = [ps("fpa0", [128, 512]), ps("fpb0", [128, 512]),
               ps("fpa1", [128, 512]), ps("fpb1", [128, 512])]
        pbc = ps("fpbc", [128, 512])
        pst = pbc[0:1, :]
        pyw = [ps("fpy0", [128, 512]), ps("fpy1", [128, 512])]

        @block.gpsimd
        def _(g):
            p.op("g", g, lambda e: e.dma_start(
                out=w2_sb[:], in_=w2_in[li].rearrange("(kc kp) n -> kp kc n", kp=128)),
                writes=["w2sb"], inc=16, dma="w2sb")

        @block.sync
        def _(sync):
            p.op("q", sync, lambda e: e.dma_start(out=gtmp[:], in_=lng_in[li, :, :]),
                 writes=["gtmp"], inc=16, dma="gtmp")
            p.op("q", sync, lambda e: e.dma_start(out=btmp[:], in_=lnb_in[li, :, :]),
                 writes=["btmp"], inc=16, dma="btmp")
            p.op("q", sync, lambda e: e.dma_start(out=b1tmp[:], in_=b1_in[li, :, :]),
                 writes=["b1tmp"], inc=16, dma="b1tmp")
            p.op("q", sync, lambda e: e.dma_start(out=b2row[:], in_=b2_in[li, :, :]),
                 writes=["b2row"], inc=16, dma="b2row")

        for src, stok, dst, dtok, n in ((gtmp, "gtmp", g_cols, "g_cols", DC),
                                        (btmp, "btmp", b_cols, "b_cols", DC),
                                        (b1tmp, "b1tmp", b1_cols, "b1_cols", 2 * NB1)):
            @block.tensor
            def _(t, src=src, stok=stok, n=n):
                p.op("t", t, lambda e: e.transpose(
                    pbc[:, 0:n], src[:, :], idnf[0:n, 0:n]),
                    reads=[stok, "idnf"], writes=["fpbc"])

            @block.scalar
            def _(s, dst=dst, dtok=dtok, n=n):
                p.op("s", s, lambda e: e.copy(dst[:, 0:n], pbc[:, 0:n]),
                     reads=["fpbc"], writes=[dtok])

        for nb in range(2):
            @block.tensor
            def _(t, nb=nb):
                p.op("t", t, lambda e: e.matmul(
                    pbc[:, :], onesf_sb[0:1, :], b2row[0:1, nb * 512:(nb + 1) * 512],
                    start=True, stop=True),
                    reads=["onesf", "b2row"], writes=["fpbc"])

            @block.scalar
            def _(s, nb=nb):
                p.op("s", s, lambda e: e.copy(b2_bc[:, nb * 512:(nb + 1) * 512], pbc[:, :]),
                     reads=["fpbc"], writes=["b2_bc"])

        tr_par = [0]
        ab_seq = [0]
        py_par = [0]

        def build_xs2(cf):
            @block.vector
            def _(v, cf=cf):
                if cf == 0:
                    p.op("v", v, lambda e: e.memset(xs2_fm[:, :, 0:1], 0.0),
                         reads=["xs2"], writes=["xs2"])
                else:
                    p.op("v", v, lambda e: e.tensor_copy(
                        xs2_fm[:, :, 0:1], xs2_fm[:, :, 512:513]),
                        reads=["xs2"], writes=["xs2"])

            for tt in range(4):
                tile = 4 * cf + tt

                @block.vector
                def _(v, tile=tile):
                    p.op("v", v, lambda e: e.tensor_copy(rcast[:, :], r_tm[:, tile, :]),
                         reads=[("r", tile)], writes=["rcast2"])

                for cg in range(2):
                    @block.tensor
                    def _(t, cg=cg):
                        def mmg(e):
                            for i2 in range(4):
                                c = cg * 4 + i2
                                ins = e.transpose(ptr2[0][:, i2, :],
                                                  rcast[:, c * 128:(c + 1) * 128], idn[:])
                            return ins
                        p.op("t", t, mmg, reads=["rcast2", "idn"], writes=["fptr0"])

                    @block.scalar
                    def _(s, tt=tt, cg=cg):
                        p.op("s", s, lambda e: e.copy(
                            xs2_fm[:, cg * 4:(cg + 1) * 4, 1 + tt * 128:1 + (tt + 1) * 128],
                            ptr2[0][:, :, :]),
                            reads=["fptr0"], writes=["xs2"])

        build_xs2(0)

        for cf in range(NCF):
            def shslice(c, n=512):
                off = 1 if c < 4 else 0
                return slice(off, off + n)

            @block.scalar
            def _(s, cf=cf):
                p.op("s", s, lambda e: e.activation(
                    sqf[:, :, :], xs2_fm[:, :, :], AF.Square),
                    reads=["xs2"], writes=["sqf"])

            for which, srcbuf, rowi in (("sum", None, 0), ("sq", None, 1)):
                @block.tensor
                def _(t, which=which, cf=cf, rowi=rowi):
                    def mmg(e):
                        for c in range(DC):
                            src = (xs2_fm[:, c, shslice(c)] if which == "sum"
                                   else sqf[:, c, shslice(c)])
                            ins = e.matmul(pst[0:1, :], onescb_sb[:, 0:1],
                                           src, start=(c == 0), stop=(c == DC - 1))
                        return ins
                    p.op("t", t, mmg,
                         reads=["xs2", "sqf", "onescb"],
                         writes=["fpbc"])

                @block.vector
                def _(v, rowi=rowi):
                    p.op("v", v, lambda e: e.tensor_copy(
                        srow[0:1, rowi * 512:(rowi + 1) * 512], pst[0:1, :]),
                        reads=["fpbc"], writes=[("srow", rowi)])

            @block.vector
            def _(v):
                p.op("v", v, lambda e: e.tensor_scalar_mul(
                    srow[0:1, 0:512], srow[0:1, 0:512], 1.0 / D),
                    reads=[("srow", 0)], writes=[("srow", 0)])
                p.op("v", v, lambda e: e.tensor_tensor(
                    rrow[0:1, :], srow[0:1, 0:512], srow[0:1, 0:512], ALU.mult),
                    reads=[("srow", 0)], writes=["rrow"])
                p.op("v", v, lambda e: e.scalar_tensor_tensor(
                    srow[0:1, 512:1024], srow[0:1, 512:1024], 1.0 / D, rrow[0:1, :],
                    ALU.mult, ALU.subtract),
                    reads=[("srow", 1), "rrow"], writes=[("srow", 1)])

            @block.scalar
            def _(s):
                p.op("s", s, lambda e: e.activation(
                    rrow[0:1, :], srow[0:1, 512:1024], AF.Ln, bias=col_eps5[0:1, 0:1]),
                    reads=[("srow", 1), "eps5"], writes=["rrow"])
                p.op("s", s, lambda e: e.activation(
                    rrow[0:1, :], rrow[0:1, :], AF.Exp, scale=-0.5),
                    reads=["rrow"], writes=["rrow"])

            for src_row, rtok, dst, dtok in ((srow[0:1, 0:512], ("srow", 0), m_bc, "m_bc"),
                                             (rrow[0:1, :], "rrow", rs_bc, "rs_bc")):
                @block.tensor
                def _(t, src_row=src_row, rtok=rtok):
                    p.op("t", t, lambda e: e.matmul(pbc[:, :], onesf_sb[0:1, :], src_row,
                                                    start=True, stop=True),
                         reads=["onesf", rtok], writes=["fpbc"])

                @block.scalar
                def _(s, dst=dst, dtok=dtok):
                    p.op("s", s, lambda e: e.copy(dst[:, :], pbc[:, :]),
                         reads=["fpbc"], writes=[dtok])

            for c in range(DC):
                @block.gpsimd
                def _(g, cf=cf, c=c):
                    p.op("g", g, lambda e: e.tensor_tensor(
                        scr_w[:, :], xs2_fm[:, c, shslice(c)], m_bc[:, :], ALU.subtract),
                        reads=["xs2", "m_bc"], writes=["scr_w"])
                    p.op("g", g, lambda e: e.tensor_tensor(
                        scr_w[:, :], scr_w[:, :], rs_bc[:, :], ALU.mult),
                        reads=["scr_w", "rs_bc"], writes=["scr_w"])

                @block.vector
                def _(v, c=c):
                    p.op("v", v, lambda e: e.tensor_scalar(
                        xn_fm[:, c, :], scr_w[:, :], g_cols[:, c:c + 1], b_cols[:, c:c + 1],
                        ALU.mult, ALU.add),
                        reads=["scr_w", "g_cols", "b_cols"], writes=["xn_fm"])

            for i in range(NB1):
                wpar = i % 2

                @block.gpsimd
                def _(g, li=li, i=i, wpar=wpar):
                    p.op("g", g, lambda e: e.dma_start(
                        out=w1blk[:, wpar, :, :],
                        in_=w1_in[li, :, i * 256:(i + 1) * 256].rearrange(
                            "(kc kp) n -> kp kc n", kp=128)),
                        writes=[("w1b", wpar)], inc=16, dma="w1b%d" % wpar)

                @block.tensor
                def _(t, i=i, wpar=wpar):
                    def mma(e):
                        for c in range(DC):
                            ins = e.matmul(pab[2 * wpar][:, :], w1blk[:, wpar, c, 0:128],
                                           xn_fm[:, c, :], start=(c == 0), stop=(c == DC - 1))
                        return ins
                    p.op("t", t, mma, reads=[("w1b", wpar), "xn_fm"], writes=[("fpa", wpar)])

                    def mmb(e):
                        for c in range(DC):
                            ins = e.matmul(pab[2 * wpar + 1][:, :], w1blk[:, wpar, c, 128:256],
                                           xn_fm[:, c, :], start=(c == 0), stop=(c == DC - 1))
                        return ins
                    p.op("t", t, mmb, reads=[("w1b", wpar), "xn_fm"], writes=[("fpb", wpar)])

                @block.scalar
                def _(s, i=i, wpar=wpar):
                    p.op("s", s, lambda e: e.activation(
                        gb_s[:, wpar, :], pab[2 * wpar + 1][:, :], AF.Gelu,
                        bias=b1_cols[:, 2 * i + 1:2 * i + 2]),
                        reads=[("fpb", wpar), "b1_cols"], writes=[("gb_s", wpar)])

                @block.vector
                def _(v, i=i, wpar=wpar):
                    p.op("v", v, lambda e: e.scalar_tensor_tensor(
                        act_fm[:, i, :], pab[2 * wpar][:, :], b1_cols[:, 2 * i:2 * i + 1],
                        gb_s[:, wpar, :], ALU.add, ALU.mult),
                        reads=[("fpa", wpar), ("gb_s", wpar), "b1_cols"], writes=[("act", i)])

            for tt in range(4):
                tile = 4 * cf + tt
                for nb in range(2):
                    par = py_par[0] % 2
                    py_par[0] += 1

                    @block.tensor
                    def _(t, tt=tt, nb=nb, par=par):
                        def mmg(e):
                            for kc in range(NB1):
                                ins = e.matmul(pyw[par][:, :], act_fm[:, kc, tt * 128:(tt + 1) * 128],
                                               w2_sb[:, kc, nb * 512:(nb + 1) * 512],
                                               start=(kc == 0), stop=(kc == NB1 - 1))
                            return ins
                        p.op("t", t, mmg, reads=[("act", kc2) for kc2 in range(NB1)] + ["w2sb"],
                             writes=[("fpy", par)])

                    @block.vector
                    def _(v, tile=tile, nb=nb, par=par):
                        p.op("v", v, lambda e: e.tensor_tensor(
                            scr_w[:, :], pyw[par][:, :], b2_bc[:, nb * 512:(nb + 1) * 512],
                            ALU.add),
                            reads=[("fpy", par), "b2_bc"], writes=["scr_w"])
                        p.op("v", v, lambda e: e.tensor_tensor(
                            r_tm[:, tile, nb * 512:(nb + 1) * 512],
                            r_tm[:, tile, nb * 512:(nb + 1) * 512],
                            scr_w[:, :], ALU.add),
                            reads=["scr_w", ("r", tile)], writes=[("r", tile)])

            if cf + 1 < NCF:
                build_xs2(cf + 1)

        p.barrier(block)


def _stage_b_consts(SEQ_):
    dim_head, window = 64, 256
    half = np.arange(0, dim_head, 2, dtype=np.float32)
    inv_freq = (1.0 / 10000.0 ** (half / dim_head)).astype(np.float32)
    t_ = np.arange(SEQ_, dtype=np.float32)
    freqs = t_[:, None] * inv_freq[None, :]
    freqs = np.concatenate([freqs, freqs], -1)
    base_scale = (half + 0.4 * dim_head) / (1.4 * dim_head)
    power = (t_ - SEQ_ // 2) / (window // 2)
    scale = (base_scale[None, :] ** power[:, None]).astype(np.float32)
    scale = np.concatenate([scale, scale], -1)
    cos, sin = np.cos(freqs), np.sin(freqs)
    NT = SEQ_ // 128
    shp = lambda a: np.ascontiguousarray(
        a.reshape(NT, 128, 64).transpose(1, 0, 2).astype(ml_dtypes.bfloat16))
    u = np.arange(128)[:, None]
    v = np.arange(128)[None, :]
    maskA = np.where(u <= v, 0.0, -3.0e38).astype(ml_dtypes.bfloat16)
    maskC = np.where(v <= u, 0.0, -3.0e38).astype(ml_dtypes.bfloat16)
    return {
        "tab_cs": shp(cos * scale), "tab_ss": shp(sin * scale),
        "tab_ci": shp(cos / scale), "tab_si": shp(sin / scale),
        "maskA": maskA, "maskC": maskC,
        "onescb": np.ones((128, 1), dtype=ml_dtypes.bfloat16),
        "onesf": np.ones((1, 128), np.float32),
    }


def _stage_b_weights(Wqkv, q_scale, k_scale, Wo, ln_g, ln_b, W1, b1, W2, b2):
    dep = Wqkv.shape[0]
    NB1 = 22
    a = W1[:, :, :2730]
    g = W1[:, :, 2730:]
    ba = b1[:, :2730]
    bg = b1[:, 2730:]
    W1P = np.zeros((dep, D, 2 * NB1 * 128), dtype=np.float32)
    b1P = np.zeros((dep, 2 * NB1, 128), dtype=np.float32)
    for i in range(NB1):
        lo, hi = i * 128, min((i + 1) * 128, 2730)
        n = hi - lo
        W1P[:, :, i * 256: i * 256 + n] = a[:, :, lo:hi]
        W1P[:, :, i * 256 + 128: i * 256 + 128 + n] = g[:, :, lo:hi]
        b1P[:, 2 * i, :n] = ba[:, lo:hi]
        b1P[:, 2 * i + 1, :n] = bg[:, lo:hi]
    W2P = np.zeros((dep, NB1 * 128, D), dtype=np.float32)
    W2P[:, :2730, :] = W2
    return {
        "wqkv_in": np.ascontiguousarray(Wqkv), "wo_in": np.ascontiguousarray(Wo),
        "qs_in": np.ascontiguousarray(q_scale.reshape(dep, 1, 64)),
        "ks_in": np.ascontiguousarray(k_scale.reshape(dep, 1, 64)),
        "lng_in": np.ascontiguousarray(ln_g.reshape(dep, DC, 128)),
        "lnb_in": np.ascontiguousarray(ln_b.reshape(dep, DC, 128)),
        "w1_in": W1P, "b1_in": b1P, "w2_in": W2P,
        "b2_in": np.ascontiguousarray(b2.reshape(dep, 1, D)),
    }


def run_stage_b(x_full, Wqkv, q_scale, k_scale, Wo, ln_g, ln_b, W1, b1, W2, b2,
                half_layers=None, debug_r=False):
    nc = build_stage_b(half_layers=half_layers, debug_r=debug_r)
    shards = _shard_x(x_full)
    wts = _stage_b_weights(Wqkv, q_scale, k_scale, Wo, ln_g, ln_b, W1, b1, W2, b2)
    consts = {**_stage_b_consts(SEQ), **_consts()}
    del consts["arange"]
    in_maps = [{"x_in": shards[c], **wts, **consts} for c in range(NCORES)]
    res = run_bass_kernel_spmd(nc, in_maps, core_ids=list(range(NCORES)))
    return np.stack([res.results[c]["x_out"].reshape(SPC, SEQ, D) for c in range(NCORES)]
                    ).reshape(NCORES * SPC, SEQ, D)


def run_stage_c(x_full, wqkv):
    nc = build_stage_c()
    shards = _shard_x(x_full)
    wr = np.ascontiguousarray(wqkv.reshape(D, 16, 192))
    w1q = np.ascontiguousarray(wr[:, :, 0:64].reshape(D, D))
    w1k = np.ascontiguousarray(wr[:, :, 64:128].reshape(D, D))
    w1v = np.ascontiguousarray(wr[:, :, 128:192].reshape(D, D))
    consts = _consts()
    in_maps = [{"x_in": shards[c], "w_q": w1q, "w_k": w1k, "w_v": w1v, **consts}
               for c in range(NCORES)]
    res = run_bass_kernel_spmd(nc, in_maps, core_ids=list(range(NCORES)))
    return np.stack([res.results[c]["x_out"].reshape(SPC, SEQ, D) for c in range(NCORES)]
                    ).reshape(NCORES * SPC, SEQ, D)


def decl_stage_d(nc, pfx, T_, SEQ_=None, x_in=None, x_out=None):
    NE = 16
    d = lambda n, s, dt=F32, kind="ExternalInput": nc.dram_tensor(pfx + n, s, dt, kind=kind).ap()
    return {
        "x_in": x_in if x_in is not None else d("x_in", [T_, D]),
        "ns_in": d("ns_in", [1, D]),
        "wg_in": d("wg_in", [D, NE]), "bg_in": d("bg_in", [1, NE]),
        "w1_in": d("w1_in", [NE, D, 2048]), "b1_in": d("b1_in", [NE, 16, 128]),
        "w2_in": d("w2_in", [NE, D, D]), "b2_in": d("b2_in", [NE, D]),
        "ident": d("ident", [128, 128], BF16), "identf": d("identf", [128, 128]),
        "onesf": d("onesf", [1, 128]), "esel": d("esel", [NE, NE * 128]),
        "x_out": x_out if x_out is not None else d("x_out", [T_, D], F32, "ExternalOutput"),
        "z_hbm": d("z_hbm", [T_, D], F32, "Internal"),
        "xn2_hbm": d("xn2_hbm", [DC, 128, T_], BF16, "Internal"),
    }


def emit_stage_d(nc, p, block, st_, pfx, A, T_=T, SEQ_=SEQ):
    NE = 16
    (x_in, ns_in, wg_in, bg_in, w1_in, b1_in, w2_in, b2_in, ident, identf,
     onesf, esel, x_out, z_hbm, xn2_hbm) = (
        A["x_in"], A["ns_in"], A["wg_in"], A["bg_in"], A["w1_in"], A["b1_in"],
        A["w2_in"], A["b2_in"], A["ident"], A["identf"], A["onesf"], A["esel"],
        A["x_out"], A["z_hbm"], A["xn2_hbm"])

    CH = 256
    NCH = T_ // CH
    NTT = CH // 128
    TOK = 512
    NSLAB = T_ // TOK

    if True:
        sb = lambda n, s, dt=F32: st_.enter_context(nc.sbuf_tensor(pfx + n, s, dt))
        ps = lambda n, s, dt=F32: st_.enter_context(nc.psum_tensor(pfx + n, s, dt))
        idn = sb("idn", [128, 128], BF16)
        idnf = sb("idnf", [128, 128], F32)
        onesf_sb = sb("onesf_sb", [1, 128], F32)
        esel_sb = sb("esel_sb", [NE, NE * 128], F32)
        ns_row = sb("ns_row", [1, D], F32)
        ns_bc = sb("ns_bc", [128, D], F32)
        wgsb = sb("wgsb", [128, DC, NE], BF16)
        bg_row = sb("bg_row", [1, NE], F32)
        bg_bc = sb("bg_bc", [128, NE], F32)
        gates_sb = sb("gates_sb", [128, T_ // 128, NE])
        gatesT = sb("gatesT", [NE, T_], F32)
        t_acc = sb("t_acc", [128, DC, T_], BF16)
        b2sb = sb("b2sb", [NE, D], F32)
        col_eps5 = sb("col_eps5", [128, 1])
        sq_scr = sb("sq_scr", [128, D])
        stat = sb("stat", [128, 8 * NTT])

        @block.sync
        def _(sync):
            for ap_hbm, ap_sb, tok in ((ident, idn, "idn"), (identf, idnf, "idnf"),
                                       (onesf, onesf_sb, "onesf"), (esel, esel_sb, "esel"),
                                       (ns_in, ns_row, "ns_row"), (bg_in, bg_row, "bg_row"),
                                       (b2_in, b2sb, "b2sb")):
                p.op("q", sync, lambda e, a=ap_hbm, b=ap_sb: e.dma_start(out=b[:], in_=a[:, :] if a.ndim == 2 else a[:, :, :]),
                     writes=[tok], inc=16, dma=tok)

        @block.gpsimd
        def _(g):
            p.op("g", g, lambda e: e.dma_start(out=wgsb[:], in_=_chunked(wg_in)),
                 writes=["wg"], inc=16, dma="wg")

        @block.vector
        def _(v):
            p.op("v", v, lambda e: e.memset(col_eps5[:], 1e-5), writes=["eps5"])

        with ExitStack() as ph1:
            ps1 = lambda n, s, dt=F32: ph1.enter_context(nc.psum_tensor(pfx + n, s, dt))
            sb1 = lambda n, s, dt=F32: ph1.enter_context(nc.sbuf_tensor(pfx + n, s, dt))
            ptr0 = ps1("ptr0", [128, 128], BF16)
            pgt = ps1("pgt", [16, 128], F32)
            pa0 = ps1("pa0", [128, 512], F32)
            pa1 = ps1("pa1", [128, 512], F32)
            x_tm = sb1("x_tm", [128, 2, NTT, D])
            z_tm = sb1("z_tm", [128, NTT, D])
            xn2_tm = sb1("xn2_tm", [128, NTT, D], BF16)
            xn2_fm = sb1("xn2_fm", [128, DC, CH], BF16)
            lg = sb1("lg", [128, NTT, NE])
            cmp_scr = sb1("cmp_scr", [128, NE * NE])
            cnt = sb1("cnt", [128, NE])
            sel = sb1("sel", [128, NE])
            gsml = sb1("gsml", [128, 6])

            for nb in range(2):
                @block.tensor
                def _(t, nb=nb):
                    p.op("t", t, lambda e: e.matmul(
                        pa0[:, 0:512], onesf_sb[0:1, :], ns_row[0:1, nb * 512:(nb + 1) * 512],
                        start=True, stop=True),
                        reads=["onesf", "ns_row"], writes=[("pa0",)])

                @block.scalar
                def _(s, nb=nb):
                    p.op("s", s, lambda e: e.copy(ns_bc[:, nb * 512:(nb + 1) * 512], pa0[:, 0:512]),
                         reads=[("pa0",)], writes=["ns_bc"])

            @block.tensor
            def _(t):
                p.op("t", t, lambda e: e.matmul(pa1[:, 0:NE], onesf_sb[0:1, :], bg_row[0:1, :],
                                                start=True, stop=True),
                     reads=["onesf", "bg_row"], writes=[("pa1",)])

            @block.scalar
            def _(s):
                p.op("s", s, lambda e: e.copy(bg_bc[:, :], pa1[:, 0:NE]),
                     reads=[("pa1",)], writes=["bg_bc"])

            def load_chunk(sync, ci):
                p.op("q", sync, lambda e, ci=ci: e.dma_start(
                    out=x_tm[:, ci % 2, :, :], in_=_tok_tiles(x_in, ci * CH, NTT)),
                    writes=[("x", ci % 2)], inc=16, dma="x%d" % (ci % 2))

            @block.sync
            def _(sync):
                load_chunk(sync, 0)
                load_chunk(sync, 1)

            ev_par = [0]
            pa_par = [0]
            for ci in range(NCH):
                buf = ci % 2
                xtok = ("x", buf)

                ln_stats_tm(p, block, lambda tt, buf=buf, x_tm=x_tm: x_tm[:, buf, tt, :], [xtok] * NTT,
                            stat, [0, 8], sq_scr, NTT, col_eps5, "eps5")

                @block.vector
                def _(v):
                    for tt in range(NTT):
                        c0 = tt * 8
                        p.op("v", v, lambda e, tt=tt, c0=c0: e.tensor_scalar(
                            z_tm[:, tt, :], x_tm[:, buf, tt, :],
                            stat[:, c0 + 1: c0 + 2], stat[:, c0 + 7: c0 + 8],
                            ALU.subtract, ALU.mult),
                            reads=[xtok, ("st", c0, 1), ("st", c0, 7)],
                            writes=[("z", tt)])

                @block.sync
                def _(sync):
                    p.op("q", sync, lambda e, ci=ci: e.dma_start(
                        out=_tok_tiles(z_hbm, ci * CH, NTT), in_=z_tm[:, :, :]),
                        reads=[("z", tt) for tt in range(NTT)], writes=["z_hbm"],
                        inc=16, dma="zout")

                @block.scalar
                def _(s):
                    for tt in range(NTT):
                        c0 = tt * 8
                        p.op("s", s, lambda e, tt=tt, c0=c0: e.activation(
                            sq_scr[:, :], z_tm[:, tt, :], AF.Square,
                            accum_out=stat[:, c0 + 2: c0 + 3]),
                            reads=[("z", tt)], writes=["sq_scr", ("st", c0, 2)])
                        p.op("s", s, lambda e, c0=c0: e.activation(
                            stat[:, c0 + 6: c0 + 7], stat[:, c0 + 2: c0 + 3], AF.Ln,
                            bias=col_eps5[:, 0:1], scale=1.0 / D),
                            reads=[("st", c0, 2), "eps5"], writes=[("st", c0, 6)])
                        p.op("s", s, lambda e, c0=c0: e.activation(
                            stat[:, c0 + 7: c0 + 8], stat[:, c0 + 6: c0 + 7], AF.Exp, scale=-0.5),
                            reads=[("st", c0, 6)], writes=[("st", c0, 7)])

                @block.vector
                def _(v):
                    for tt in range(NTT):
                        c0 = tt * 8
                        p.op("v", v, lambda e, tt=tt: e.tensor_tensor(
                            z_tm[:, tt, :], z_tm[:, tt, :], ns_bc[:, :], ALU.mult),
                            reads=[("z", tt), "ns_bc"], writes=[("z", tt)])
                        p.op("v", v, lambda e, tt=tt, c0=c0: e.tensor_scalar_mul(
                            xn2_tm[:, tt, :], z_tm[:, tt, :], stat[:, c0 + 7: c0 + 8]),
                            reads=[("z", tt), ("st", c0, 7)], writes=[("xn2", tt)])


                for tt in range(NTT):
                    for c in range(DC):
                        @block.tensor
                        def _(t, tt=tt, c=c):
                            p.op("t", t, lambda e: e.transpose(
                                ptr0[:, :], xn2_tm[:, tt, c * 128:(c + 1) * 128], idn[:]),
                                reads=[("xn2", tt), "idn"], writes=["ptr0"])

                        @block.scalar
                        def _(s, tt=tt, c=c):
                            p.op("s", s, lambda e: e.copy(
                                xn2_fm[:, c, tt * 128:(tt + 1) * 128], ptr0[:, :]),
                                reads=["ptr0"], writes=["xn2_fm"])

                @block.sync
                def _(sync):
                    p.op("q", sync, lambda e, ci=ci: e.dma_start(
                        out=xn2_hbm[:, :, ci * CH:(ci + 1) * CH].rearrange("kc p t -> p kc t"),
                        in_=xn2_fm[:, :, :]),
                        reads=["xn2_fm"], writes=["xn2_hbm"], inc=16, dma="xn2out")

                for tt in range(NTT):
                    tg = ci * NTT + tt
                    par = pa_par[0] % 2
                    pa_par[0] += 1
                    pa_ = [pa0, pa1][par]
                    ptok = ("pa0",) if par == 0 else ("pa1",)

                    @block.tensor
                    def _(t, tt=tt, pa_=pa_, ptok=ptok):
                        def mmg(e):
                            for c in range(DC):
                                ins = e.matmul(pa_[:, 0:NE], xn2_fm[:, c, tt * 128:(tt + 1) * 128],
                                               wgsb[:, c, :], start=(c == 0), stop=(c == DC - 1))
                            return ins
                        p.op("t", t, mmg, reads=["xn2_fm", "wg"], writes=[ptok])

                    @block.vector
                    def _(v, tt=tt, tg=tg, pa_=pa_, ptok=ptok):
                        p.op("v", v, lambda e: e.tensor_tensor(
                            lg[:, tt, :], pa_[:, 0:NE], bg_bc[:, :], ALU.add),
                            reads=[ptok, "bg_bc"], writes=[("lg", tt)])
                        p.op("v", v, lambda e: e.tensor_tensor(
                            cmp_scr[:, :].rearrange("p (a b) -> p a b", b=NE),
                            lg[:, tt, :].unsqueeze(1).broadcast_to([128, NE, NE]),
                            lg[:, tt, :].unsqueeze(2).broadcast_to([128, NE, NE]),
                            ALU.is_gt),
                            reads=[("lg", tt)], writes=["cmp"])
                        p.op("v", v, lambda e: e.tensor_reduce(
                            cnt[:, :], cmp_scr[:, :].rearrange("p (a b) -> p a b", a=NE),
                            AX.X, ALU.add),
                            reads=["cmp"], writes=["cnt"])
                        p.op("v", v, lambda e: e.tensor_single_scalar(
                            sel[:, :], cnt[:, :], 4.0, ALU.is_lt),
                            reads=["cnt"], writes=["sel"])
                        p.op("v", v, lambda e: e.reduce_max(
                            gsml[:, 0:1], lg[:, tt, :], axis=AX.X),
                            reads=[("lg", tt)], writes=["mx"])
                        p.op("v", v, lambda e: e.tensor_scalar(
                            gates_sb[:, tg, :], lg[:, tt, :], gsml[:, 0:1], None,
                            ALU.subtract),
                            reads=[("lg", tt), "mx"], writes=[("gates", tg)])

                    @block.scalar
                    def _(s, tg=tg):
                        p.op("s", s, lambda e: e.activation(
                            gates_sb[:, tg, :], gates_sb[:, tg, :], AF.Exp),
                            reads=[("gates", tg)], writes=[("gates", tg)])

                    @block.vector
                    def _(v, tg=tg):
                        p.op("v", v, lambda e: e.tensor_tensor(
                            gates_sb[:, tg, :], gates_sb[:, tg, :], sel[:, :], ALU.mult),
                            reads=[("gates", tg), "sel"], writes=[("gates", tg)])
                        p.op("v", v, lambda e: e.reduce_sum(
                            gsml[:, 1:2], gates_sb[:, tg, :], axis=AX.X),
                            reads=[("gates", tg)], writes=["dn"])
                        p.op("v", v, lambda e: e.reciprocal(
                            gsml[:, 2:3], gsml[:, 1:2]),
                            reads=["dn"], writes=["rdn"])
                        p.op("v", v, lambda e: e.tensor_scalar_mul(
                            gates_sb[:, tg, :], gates_sb[:, tg, :], gsml[:, 2:3]),
                            reads=[("gates", tg), "rdn"], writes=[("gates", tg)])

                    @block.tensor
                    def _(t, tg=tg):
                        p.op("t", t, lambda e: e.transpose(
                            pgt[:, :], gates_sb[:, tg, :], idnf[:]),
                            reads=[("gates", tg), "idnf"], writes=["pgt"])

                    @block.scalar
                    def _(s, tg=tg):
                        p.op("s", s, lambda e: e.copy(
                            gatesT[:, tg * 128:(tg + 1) * 128], pgt[:, :]),
                            reads=["pgt"], writes=["gatesT"])

                @block.sync
                def _(sync):
                    if ci + 2 < NCH:
                        load_chunk(sync, ci + 2)

            p.barrier(block)

        with ExitStack() as ph2:
            ps2 = lambda n, s, dt=F32: ph2.enter_context(nc.psum_tensor(pfx + n, s, dt))
            sb2 = lambda n, s, dt=F32: ph2.enter_context(nc.sbuf_tensor(pfx + n, s, dt))
            pa0 = ps2("p2a0", [128, TOK], F32)
            pa1 = ps2("p2a1", [128, TOK], F32)
            py0 = ps2("p2y0", [128, TOK], F32)
            py1 = ps2("p2y1", [128, TOK], F32)
            pgb = ps2("pgb", [128, 512], F32)
            w1sb = sb2("w1sb", [128, DC, 2048], BF16)
            w2sb = sb2("w2sb", [128, DC, D], BF16)
            b1r = sb2("b1r", [16, 128], F32)
            b1col = sb2("b1col", [128, 16], F32)
            gate_bc = sb2("gate_bc", [128, T_], BF16)
            xslab = sb2("xslab", [128, 2, DC, TOK], BF16)
            a_s = sb2("a_s", [128, 2, TOK], F32)
            b_s = sb2("b_s", [128, 2, TOK], F32)
            g_s = sb2("g_s", [128, 2, TOK], F32)
            v1_s = sb2("v1_s", [128, 2, TOK], F32)
            ss_s = sb2("ss_s", [128, DC, TOK], BF16)

            def load_slab(sync, e, sl):
                p.op("q", sync, lambda ee, e=e, sl=sl: ee.dma_start(
                    out=xslab[:, sl % 2, :, :],
                    in_=xn2_hbm[:, :, sl * TOK:(sl + 1) * TOK].rearrange("kc p t -> p kc t")),
                    reads=["xn2_hbm"], writes=[("xs", sl % 2)], inc=16, dma="xs%d" % (sl % 2))

            for e_i in range(NE):
                @block.gpsimd
                def _(g, e_i=e_i):
                    p.op("g", g, lambda e: e.dma_start(
                        out=w1sb[:], in_=w1_in[e_i].rearrange("(kc kp) n -> kp kc n", kp=128)),
                        writes=["w1"], inc=16, dma="w1")
                    p.op("g", g, lambda e: e.dma_start(
                        out=w2sb[:], in_=w2_in[e_i].rearrange("(kc kp) n -> kp kc n", kp=128)),
                        writes=["w2"], inc=16, dma="w2")

                @block.sync
                def _(sync, e_i=e_i):
                    p.op("q", sync, lambda e: e.dma_start(out=b1r[:], in_=b1_in[e_i, :, :]),
                         writes=["b1r"], inc=16, dma="b1r")
                    if e_i == 0:
                        load_slab(sync, 0, 0)
                        if NSLAB > 1:
                            load_slab(sync, 0, 1)
                        elif NE > 1:
                            load_slab(sync, 1, 0)

                @block.tensor
                def _(t):
                    p.op("t", t, lambda e: e.transpose(pa0[:, 0:16], b1r[:, :], idnf[0:16, 0:16]),
                         reads=["b1r", "idnf"], writes=[("p2a0",)])

                @block.scalar
                def _(s):
                    p.op("s", s, lambda e: e.copy(b1col[:, :], pa0[:, 0:16]),
                         reads=[("p2a0",)], writes=["b1col"])

                for gslab in range(T_ // 512):
                    @block.tensor
                    def _(t, e_i=e_i, gslab=gslab):
                        p.op("t", t, lambda e: e.matmul(
                            pgb[:, :], esel_sb[:, e_i * 128:(e_i + 1) * 128],
                            gatesT[:, gslab * 512:(gslab + 1) * 512], start=True, stop=True),
                            reads=["esel", "gatesT"], writes=["pgb"])

                    @block.scalar
                    def _(s, gslab=gslab):
                        p.op("s", s, lambda e: e.copy(
                            gate_bc[:, gslab * 512:(gslab + 1) * 512], pgb[:, :]),
                            reads=["pgb"], writes=["gate_bc"])

                for sl in range(NSLAB):
                    sbuf_ = sl % 2
                    xs_tok = ("xs", sbuf_)

                    for j in range(DC):
                        @block.tensor
                        def _(t, j=j, sbuf_=sbuf_):
                            def mma(e):
                                for kc in range(DC):
                                    ins = e.matmul(pa0[:, :], w1sb[:, kc, j * 128:(j + 1) * 128],
                                                   xslab[:, sbuf_, kc, :],
                                                   start=(kc == 0), stop=(kc == DC - 1))
                                return ins
                            p.op("t", t, mma, reads=["w1", xs_tok], writes=[("p2a0",)])

                            def mmb(e):
                                for kc in range(DC):
                                    ins = e.matmul(pa1[:, :], w1sb[:, kc, 1024 + j * 128:1024 + (j + 1) * 128],
                                                   xslab[:, sbuf_, kc, :],
                                                   start=(kc == 0), stop=(kc == DC - 1))
                                return ins
                            p.op("t", t, mmb, reads=["w1", xs_tok], writes=[("p2a1",)])

                        jp = j % 2

                        @block.vector
                        def _(v, j=j, jp=jp):
                            p.op("v", v, lambda e: e.tensor_scalar(
                                a_s[:, jp, :], pa0[:, :], b1col[:, j:j + 1], 7.0, ALU.add, ALU.min),
                                reads=[("p2a0",), "b1col"], writes=[("a_s", jp)])
                            p.op("v", v, lambda e: e.tensor_scalar(
                                b_s[:, jp, :], pa1[:, :], b1col[:, 8 + j:9 + j], -7.0, ALU.add, ALU.max),
                                reads=[("p2a1",), "b1col"], writes=[("b_s", jp)])
                            p.op("v", v, lambda e: e.tensor_scalar_min(
                                b_s[:, jp, :], b_s[:, jp, :], 7.0),
                                reads=[("b_s", jp)], writes=[("b_s", jp)])

                        @block.scalar
                        def _(s, j=j, jp=jp):
                            p.op("s", s, lambda e: e.activation(
                                g_s[:, jp, :], a_s[:, jp, :], AF.Sigmoid, scale=1.702),
                                reads=[("a_s", jp)], writes=[("g_s", jp)])

                        @block.vector
                        def _(v, j=j, jp=jp):
                            p.op("v", v, lambda e: e.scalar_tensor_tensor(
                                v1_s[:, jp, :], b_s[:, jp, :], 1.0, g_s[:, jp, :], ALU.add, ALU.mult),
                                reads=[("b_s", jp), ("g_s", jp)], writes=[("v1_s", jp)])

                        @block.gpsimd
                        def _(g, j=j, sl=sl, jp=jp):
                            p.op("g", g, lambda e: e.tensor_tensor(
                                v1_s[:, jp, :], v1_s[:, jp, :], a_s[:, jp, :], ALU.mult),
                                reads=[("v1_s", jp), ("a_s", jp)], writes=[("v1_s", jp)])
                            p.op("g", g, lambda e, sl=sl, j=j: e.tensor_tensor(
                                ss_s[:, j, :], v1_s[:, jp, :], gate_bc[:, sl * TOK:(sl + 1) * TOK],
                                ALU.mult),
                                reads=[("v1_s", jp), "gate_bc"], writes=[("ss", j)])

                    for oc2 in range(DC):
                        ypar = oc2 % 2
                        py_ = [py0, py1][ypar]
                        ytok = ("p2y", ypar)

                        @block.tensor
                        def _(t, oc2=oc2, py_=py_, ytok=ytok):
                            def mmy(e):
                                for j in range(DC):
                                    ins = e.matmul(py_[:, :], w2sb[:, j, oc2 * 128:(oc2 + 1) * 128],
                                                   ss_s[:, j, :], start=(j == 0), stop=(j == DC - 1))
                                return ins
                            p.op("t", t, mmy,
                                 reads=["w2"] + [("ss", j) for j in range(DC)],
                                 writes=[ytok])

                        @block.vector
                        def _(v, e_i=e_i, sl=sl, oc2=oc2, py_=py_, ytok=ytok):
                            if e_i == 0:
                                p.op("v", v, lambda e: e.tensor_copy(
                                    t_acc[:, oc2, sl * TOK:(sl + 1) * TOK], py_[:, :]),
                                    reads=[ytok], writes=[("tacc", sl, oc2)])
                            else:
                                p.op("v", v, lambda e: e.tensor_tensor(
                                    t_acc[:, oc2, sl * TOK:(sl + 1) * TOK],
                                    t_acc[:, oc2, sl * TOK:(sl + 1) * TOK],
                                    py_[:, :], ALU.add),
                                    reads=[ytok, ("tacc", sl, oc2)],
                                    writes=[("tacc", sl, oc2)])

                    @block.sync
                    def _(sync, e_i=e_i, sl=sl):
                        gi = e_i * NSLAB + sl + 2
                        if gi < NE * NSLAB:
                            load_slab(sync, gi // NSLAB, gi % NSLAB)

            p.barrier(block)

        with ExitStack() as ph3:
            ps3 = lambda n, s, dt=F32: ph3.enter_context(nc.psum_tensor(pfx + n, s, dt))
            sb3 = lambda n, s, dt=F32: ph3.enter_context(nc.sbuf_tensor(pfx + n, s, dt))
            ptr0 = ps3("p3tr", [128, 128], BF16)
            pb2 = [ps3("pb20", [128, 512], F32), ps3("pb21", [128, 512], F32)]
            x_tm = sb3("x3_tm", [128, 2, NTT, D])
            z3_tm = sb3("z3_tm", [128, 2, NTT, D])
            u_tm = sb3("u_tm", [128, NTT, D])
            out_tm = sb3("out3_tm", [128, NTT, D])

            def load_x3(sync, ci):
                p.op("q", sync, lambda e, ci=ci: e.dma_start(
                    out=x_tm[:, ci % 2, :, :], in_=_tok_tiles(x_in, ci * CH, NTT)),
                    writes=[("x3", ci % 2)], inc=16, dma="x3%d" % (ci % 2))
                p.op("q", sync, lambda e, ci=ci: e.dma_start(
                    out=z3_tm[:, ci % 2, :, :], in_=_tok_tiles(z_hbm, ci * CH, NTT)),
                    reads=["z_hbm"], writes=[("z3", ci % 2)], inc=16, dma="z3%d" % (ci % 2))

            @block.sync
            def _(sync):
                load_x3(sync, 0)
                load_x3(sync, 1)

            for ci in range(NCH):
                buf = ci % 2
                for tt in range(NTT):
                    tg = ci * NTT + tt

                    @block.tensor
                    def _(t, tg=tg):
                        p.op("t", t, lambda e: e.matmul(
                            pb2[0][:, 0:512], gatesT[:, tg * 128:(tg + 1) * 128],
                            b2sb[:, 0:512], start=True, stop=True),
                            reads=["gatesT", "b2sb"], writes=[("pb2", 0)])
                        p.op("t", t, lambda e: e.matmul(
                            pb2[1][:, 0:512], gatesT[:, tg * 128:(tg + 1) * 128],
                            b2sb[:, 512:1024], start=True, stop=True),
                            reads=["gatesT", "b2sb"], writes=[("pb2", 1)])

                    for c in range(DC):
                        @block.tensor
                        def _(t, tg=tg, c=c):
                            p.op("t", t, lambda e: e.transpose(
                                ptr0[:, :], t_acc[:, c, tg * 128:(tg + 1) * 128], idn[:]),
                                reads=[("tacc", tg // (TOK // 128), c), "idn"],
                                writes=["p3tr"])

                        @block.scalar
                        def _(s, tt=tt, c=c):
                            p.op("s", s, lambda e: e.copy(
                                u_tm[:, tt, c * 128:(c + 1) * 128], ptr0[:, :]),
                                reads=["p3tr"], writes=[("u", tt)])

                    @block.vector
                    def _(v, tt=tt, buf=buf):
                        p.op("v", v, lambda e: e.tensor_tensor(
                            u_tm[:, tt, :], u_tm[:, tt, :], z3_tm[:, buf, tt, :], ALU.add),
                            reads=[("u", tt), ("z3", buf)], writes=[("u", tt)])
                        p.op("v", v, lambda e: e.tensor_tensor(
                            u_tm[:, tt, 0:512], u_tm[:, tt, 0:512], pb2[0][:, 0:512], ALU.add),
                            reads=[("u", tt), ("pb2", 0)], writes=[("u", tt)])
                        p.op("v", v, lambda e: e.tensor_tensor(
                            u_tm[:, tt, 512:1024], u_tm[:, tt, 512:1024], pb2[1][:, 0:512], ALU.add),
                            reads=[("u", tt), ("pb2", 1)], writes=[("u", tt)])

                ln_stats_tm(p, block, lambda tt: u_tm[:, tt, :],
                            [("u", tt) for tt in range(NTT)],
                            stat, [0, 8], sq_scr, NTT, col_eps5, "eps5")

                @block.vector
                def _(v, ci=ci, buf=buf):
                    for tt in range(NTT):
                        c0 = tt * 8
                        p.op("v", v, lambda e, tt=tt, c0=c0: e.tensor_scalar(
                            out_tm[:, tt, :], u_tm[:, tt, :],
                            stat[:, c0 + 1: c0 + 2], stat[:, c0 + 7: c0 + 8],
                            ALU.subtract, ALU.mult),
                            reads=[("u", tt), ("st", c0, 1), ("st", c0, 7)],
                            writes=[("o3", tt)])
                        p.op("v", v, lambda e, tt=tt: e.tensor_tensor(
                            out_tm[:, tt, :], out_tm[:, tt, :], x_tm[:, buf, tt, :], ALU.add),
                            reads=[("o3", tt), ("x3", buf)], writes=[("o3", tt)])

                @block.sync
                def _(sync, ci=ci):
                    p.op("q", sync, lambda e, ci=ci: e.dma_start(
                        out=_tok_tiles(x_out, ci * CH, NTT), in_=out_tm[:, :, :]),
                        reads=[("o3", tt) for tt in range(NTT)],
                        writes=["x_out_hbm"], inc=16, dma="xo3")
                    if ci + 2 < NCH:
                        load_x3(sync, ci + 2)


def build_stage_d(T_=T, SEQ_=SEQ):
    return _std_wrap(decl_stage_d, emit_stage_d, T_, SEQ_)


def _moe_prep(w1, b1, w2, b2):
    w1p = np.concatenate([w1[:, :, 0::2], w1[:, :, 1::2]], axis=2)
    b1p = np.concatenate([b1[:, 0::2], b1[:, 1::2]], axis=1)
    b1t = np.ascontiguousarray(b1p.reshape(16, 16, 128))
    return (np.ascontiguousarray(w1p), b1t,
            np.ascontiguousarray(w2), np.ascontiguousarray(b2))


def run_stage_d(x_full, norm_scale, wg, bg, w1, b1, w2, b2):
    nc = build_stage_d()
    shards = _shard_x(x_full)
    w1p, b1t, w2c, b2c = _moe_prep(w1, b1, w2, b2)
    esel = np.zeros((16, 16 * 128), dtype=np.float32)
    for e in range(16):
        esel[e, e * 128:(e + 1) * 128] = 1.0
    consts = _consts()
    in_maps = [{"x_in": shards[c], "ns_in": norm_scale.reshape(1, D), "wg_in": wg,
                "bg_in": bg.reshape(1, 16), "w1_in": w1p, "b1_in": b1t,
                "w2_in": w2c, "b2_in": b2c, "onesf": np.ones((1, 128), np.float32),
                "esel": esel, **consts}
               for c in range(NCORES)]
    res = run_bass_kernel_spmd(nc, in_maps, core_ids=list(range(NCORES)))
    return np.stack([res.results[c]["x_out"].reshape(SPC, SEQ, D) for c in range(NCORES)]
                    ).reshape(NCORES * SPC, SEQ, D)


def _consts():
    return {
        "ident": np.eye(128, dtype=ml_dtypes.bfloat16),
        "identf": np.eye(128, dtype=np.float32),
        "arange": np.arange(128, dtype=np.float32).reshape(128, 1),
    }


def _shard_x(x):
    x = np.ascontiguousarray(x, dtype=np.float32)
    return [x[c * SPC:(c + 1) * SPC].reshape(T, D) for c in range(NCORES)]


def run_stage_a(x_full, Wq, Wk, Wv, Wc):
    nc = build_stage_a()
    shards = _shard_x(x_full)
    consts = _consts()
    in_maps = [{"x_in": shards[c], "w_q": Wq, "w_k": Wk, "w_v": Wv, "w_c": Wc, **consts}
               for c in range(NCORES)]
    res = run_bass_kernel_spmd(nc, in_maps, core_ids=list(range(NCORES)))
    return np.stack([res.results[c]["x_out"].reshape(SPC, SEQ, D) for c in range(NCORES)]
                    ).reshape(NCORES * SPC, SEQ, D)


def build_fused(T_=T, SEQ_=SEQ):
    nc = bass.Bass("TRN2", target_bir_lowering=False, debug=False)
    x1 = nc.dram_tensor("x1_hbm", [T_, D], F32, kind="Internal").ap()
    x2 = nc.dram_tensor("x2_hbm", [T_, D], F32, kind="Internal").ap()
    x3 = nc.dram_tensor("x3_hbm", [T_, D], F32, kind="Internal").ap()
    A = decl_stage_a(nc, "a_", T_, x_out=x1)
    B = decl_stage_b(nc, "b_", T_, SEQ_, x_in=x1, x_out=x2)
    C = decl_stage_c(nc, "c_", T_, x_in=x2, x_out=x3)
    Dd = decl_stage_d(nc, "d_", T_, x_in=x3)

    with ExitStack() as st_:
        p = Prog(nc, st_)
        block = FakeBlock()
        with ExitStack() as sA:
            emit_stage_a(nc, p, block, sA, "a_", A, T_, SEQ_)
            p.barrier(block)
        with ExitStack() as sB:
            emit_stage_b(nc, p, block, sB, "b_", B, T_, SEQ_)
            p.barrier(block)
        with ExitStack() as sC:
            emit_stage_c(nc, p, block, sC, "c_", C, T_, SEQ_)
            p.barrier(block)
        with ExitStack() as sD:
            emit_stage_d(nc, p, block, sD, "d_", Dd, T_, SEQ_)
        p.final_wait()
        real_block = st_.enter_context(nc.Block())
        p.flush(real_block)
    return nc


def _fused_in_maps(inputs):
    shards = _shard_x(inputs["L0_attn_X"])
    cst = _consts()
    wr = np.ascontiguousarray(inputs["L1_attn_wqkv"].reshape(D, 16, 192))
    bwts = _stage_b_weights(inputs["L0_ffn_Wqkv"], inputs["L0_ffn_q_scale"],
                            inputs["L0_ffn_k_scale"], inputs["L0_ffn_Wo"],
                            inputs["L0_ffn_ln_g"], inputs["L0_ffn_ln_b"],
                            inputs["L0_ffn_W1"], inputs["L0_ffn_b1"],
                            inputs["L0_ffn_W2"], inputs["L0_ffn_b2"])
    bcst = _stage_b_consts(SEQ)
    w1p, b1t, w2c, b2c = _moe_prep(inputs["L1_ffn_w1"], inputs["L1_ffn_b1"],
                                   inputs["L1_ffn_w2"], inputs["L1_ffn_b2"])
    esel = np.zeros((16, 16 * 128), dtype=np.float32)
    for e in range(16):
        esel[e, e * 128:(e + 1) * 128] = 1.0
    shared = {}
    shared.update({"a_w_q": inputs["L0_attn_Wq"], "a_w_k": inputs["L0_attn_Wk"],
                   "a_w_v": inputs["L0_attn_Wv"], "a_w_c": inputs["L0_attn_Wc"],
                   "a_ident": cst["ident"]})
    shared.update({"b_" + k: v for k, v in bwts.items()})
    shared.update({"b_tab_cs": bcst["tab_cs"], "b_tab_ss": bcst["tab_ss"],
                   "b_tab_ci": bcst["tab_ci"], "b_tab_si": bcst["tab_si"],
                   "b_maskA": bcst["maskA"], "b_maskC": bcst["maskC"],
                   "b_ident": cst["ident"], "b_identf": cst["identf"],
                   "b_onesf": bcst["onesf"], "b_onescb": bcst["onescb"]})
    shared.update({"c_w_q": np.ascontiguousarray(wr[:, :, 0:64].reshape(D, D)),
                   "c_w_k": np.ascontiguousarray(wr[:, :, 64:128].reshape(D, D)),
                   "c_w_v": np.ascontiguousarray(wr[:, :, 128:192].reshape(D, D)),
                   "c_ident": cst["ident"], "c_identf": cst["identf"],
                   "c_arange": cst["arange"]})
    shared.update({"d_ns_in": inputs["L1_ffn_norm_scale"].reshape(1, D),
                   "d_wg_in": inputs["L1_ffn_wg"],
                   "d_bg_in": inputs["L1_ffn_bg"].reshape(1, 16),
                   "d_w1_in": w1p, "d_b1_in": b1t, "d_w2_in": w2c, "d_b2_in": b2c,
                   "d_ident": cst["ident"], "d_identf": cst["identf"],
                   "d_onesf": bcst["onesf"], "d_esel": esel})
    return [{"a_x_in": shards[c], **shared} for c in range(NCORES)]


def kernel(**inputs):
    nc = build_fused()
    in_maps = _fused_in_maps(inputs)
    res = run_bass_kernel_spmd(nc, in_maps, core_ids=list(range(NCORES)))
    return np.stack([res.results[c]["d_x_out"].reshape(SPC, SEQ, D) for c in range(NCORES)]
                    ).reshape(NCORES * SPC, SEQ, D)


def kernel_multilaunch(**inputs):
    x = run_stage_a(inputs["L0_attn_X"], inputs["L0_attn_Wq"], inputs["L0_attn_Wk"],
                    inputs["L0_attn_Wv"], inputs["L0_attn_Wc"])
    x = run_stage_b(x, inputs["L0_ffn_Wqkv"], inputs["L0_ffn_q_scale"],
                    inputs["L0_ffn_k_scale"], inputs["L0_ffn_Wo"], inputs["L0_ffn_ln_g"],
                    inputs["L0_ffn_ln_b"], inputs["L0_ffn_W1"], inputs["L0_ffn_b1"],
                    inputs["L0_ffn_W2"], inputs["L0_ffn_b2"])
    x = run_stage_c(x, inputs["L1_attn_wqkv"])
    x = run_stage_d(x, inputs["L1_ffn_norm_scale"], inputs["L1_ffn_wg"], inputs["L1_ffn_bg"],
                    inputs["L1_ffn_w1"], inputs["L1_ffn_b1"], inputs["L1_ffn_w2"],
                    inputs["L1_ffn_b2"])
    return x
